# version 13
# baseline (speedup 1.0000x reference)
from contextlib import ExitStack

import numpy as np
import ml_dtypes

import concourse.bass as bass
import concourse.mybir as mybir
from concourse.bass_utils import run_bass_kernel_spmd

F32 = mybir.dt.float32
BF16 = mybir.dt.bfloat16
ALU = mybir.AluOpType
ACT = mybir.ActivationFunctionType
AX = mybir.AxisListType

D = 2048
S = 4096
BATCH = 4
NH = 16
NB = 32
NSLOT = 16
NTOK = NSLOT * 128
NC = 16
E, R, ER = 8, 64, 512
NEGINF = -1.0e30
EPS = 1e-5
ISQ = float(1.0 / np.sqrt(128.0))
MSHIFT = 60.0
FFN_SCALE = 4.0

BF = ml_dtypes.bfloat16
ENGS = ("sync", "scalar", "vector", "tensor", "gpsimd")


class Sem:
    def __init__(self, handle):
        self.h = handle
        self.n = 0

    def inc(self, k=1):
        self.n += k
        return self.n


class EngW:

    def __init__(self, e):
        object.__setattr__(self, "_e", e)
        object.__setattr__(self, "last", None)

    def __getattr__(self, n):
        a = getattr(object.__getattribute__(self, "_e"), n)
        if callable(a):
            def f(*args, **kw):
                r = a(*args, **kw)
                if hasattr(r, "then_inc"):
                    object.__setattr__(self, "last", r)
                return r
            return f
        return a


class Ph:

    def __init__(self, blk):
        self.blk = blk
        self.q = {k: [] for k in ENGS}

    def on(self, eng, fn):
        self.q[eng].append(fn)

    def flush(self):
        for eng in ENGS:
            fns = self.q[eng]
            self.q[eng] = []
            if fns:
                getattr(self.blk, eng)(
                    lambda e, fns=tuple(fns): [f(EngW(e)) for f in fns][-1])


def _w(e, t):
    if t is not None:
        e.wait_ge(t[0], t[1])


def _bank_chunks(lo, hi, bank=512):
    out = []
    while lo < hi:
        nxt = min(hi, (lo // bank + 1) * bank)
        out.append((lo, nxt))
        lo = nxt
    return out


def build_program(debug=False):
    nc = bass.Bass("TRN2", target_bir_lowering=False, detect_race_conditions=False)

    xp = nc.dram_tensor("xp", [S, D], F32, kind="ExternalInput").ap()
    wqp = nc.dram_tensor("wqp", [16, D, 128], F32, kind="ExternalInput").ap()
    wkp = nc.dram_tensor("wkp", [16, D, 128], F32, kind="ExternalInput").ap()
    wvt = nc.dram_tensor("wvt", [D, D], F32, kind="ExternalInput").ap()
    wot = nc.dram_tensor("wot", [D, D], F32, kind="ExternalInput").ap()
    att = nc.dram_tensor("att", [D, ER], F32, kind="ExternalInput").ap()
    bwt = nc.dram_tensor("bwt", [ER, D], F32, kind="ExternalInput").ap()
    wgt = nc.dram_tensor("wgt", [D, E], F32, kind="ExternalInput").ap()
    mdiag = nc.dram_tensor("mdiag", [128, 128], F32, kind="ExternalInput").ap()
    mpad = nc.dram_tensor("mpad", [128, 128], F32, kind="ExternalInput").ap()
    ident_d = nc.dram_tensor("ident", [128, 128], BF16, kind="ExternalInput").ap()
    out = nc.dram_tensor("out", [NTOK, D], F32, kind="ExternalOutput").ap()

    scr_kind = "ExternalOutput" if debug else "Internal"
    k_scr = nc.dram_tensor("k_scr", [D, S], BF16, kind=scr_kind).ap()
    v_scr = nc.dram_tensor("v_scr", [S, D], BF16, kind=scr_kind).ap()
    q_scr = nc.dram_tensor("q_scr", [D, NTOK], BF16, kind=scr_kind).ap()
    if debug:
        xn_dbg = nc.dram_tensor("xn_dbg", [128, NC * S], BF16, kind="ExternalOutput").ap()
        attn_dbg = nc.dram_tensor("attn_dbg", [128, NH * NTOK], BF16, kind="ExternalOutput").ap()
        rs_dbg = nc.dram_tensor("rs_dbg", [1, NH * NTOK], F32, kind="ExternalOutput").ap()
        negm_dbg = nc.dram_tensor("negm_dbg", [NH, D], BF16, kind="ExternalOutput").ap()

    with ExitStack() as _ctx:
        def EC(cm):
            return _ctx.enter_context(cm)

        _d_cst = EC(nc.semaphore("d_cst")); _d_x1 = EC(nc.semaphore("d_x1"))
        _v_ph1 = EC(nc.semaphore("v_ph1")); _s_sq1 = EC(nc.semaphore("s_sq1"))
        _t_tr1 = EC(nc.semaphore("t_tr1")); _s_ev1 = EC(nc.semaphore("s_ev1"))
        _d_w2 = EC(nc.semaphore("d_w2")); _t_mm2 = EC(nc.semaphore("t_mm2"))
        _s_ev2 = EC(nc.semaphore("s_ev2")); _t_nm2 = EC(nc.semaphore("t_nm2"))
        _v_nm2 = EC(nc.semaphore("v_nm2")); _d_sp2 = EC(nc.semaphore("d_sp2"))
        _s_bnd = EC(nc.semaphore("s_bnd")); _t_bnd = EC(nc.semaphore("t_bnd"))
        _v_bnd = EC(nc.semaphore("v_bnd")); _d_kvq = EC(nc.semaphore("d_kvq"))
        _t_sc = EC(nc.semaphore("t_sc")); _v_mk = EC(nc.semaphore("v_mk"))
        _s_xp = EC(nc.semaphore("s_xp")); _t_pv = EC(nc.semaphore("t_pv"))
        _v_ev3 = EC(nc.semaphore("v_ev3")); _s_ev3 = EC(nc.semaphore("s_ev3"))
        _t_bc3 = EC(nc.semaphore("t_bc3")); _d_w4 = EC(nc.semaphore("d_w4"))
        _d_x4 = EC(nc.semaphore("d_x4")); _t_mm4 = EC(nc.semaphore("t_mm4"))
        _v_ph4 = EC(nc.semaphore("v_ph4")); _s_ph4 = EC(nc.semaphore("s_ph4"))
        _d_out = EC(nc.semaphore("d_out"))
        _d_x1s = EC(nc.semaphore("d_x1s")); _d_x1sb = EC(nc.semaphore("d_x1sb"))
        _d_x1b = EC(nc.semaphore("d_x1b")); _d_w2b = EC(nc.semaphore("d_w2b"))
        _d_sp2b = EC(nc.semaphore("d_sp2b")); _d_kvqb = EC(nc.semaphore("d_kvqb"))
        _d_w4b = EC(nc.semaphore("d_w4b")); _d_outb = EC(nc.semaphore("d_outb"))
        _d_dbgx = EC(nc.semaphore("d_dbgx")); _d_dbg = EC(nc.semaphore("d_dbg"))
        _t_pa = EC(nc.semaphore("t_pa")); _v_pa = EC(nc.semaphore("v_pa"))
        c_mdiag = EC(nc.sbuf_tensor("c_mdiag", [128, 128], F32))
        c_mpad = EC(nc.sbuf_tensor("c_mpad", [128, 128], F32))
        ones128 = EC(nc.sbuf_tensor("ones128", [128, 128], BF16))
        c_ident = EC(nc.sbuf_tensor("c_ident", [128, 128], BF16))
        ones_col = EC(nc.sbuf_tensor("ones_col", [128, 1], BF16))
        ones_row = EC(nc.sbuf_tensor("ones_row", [1, 128], BF16))
        eps_col = EC(nc.sbuf_tensor("eps_col", [128, 1], F32))
        blk = EC(nc.Block())
        d_cst = Sem(_d_cst); d_x1 = Sem(_d_x1); v_ph1 = Sem(_v_ph1)
        s_sq1 = Sem(_s_sq1); t_tr1 = Sem(_t_tr1); s_ev1 = Sem(_s_ev1)
        d_w2 = Sem(_d_w2); t_mm2 = Sem(_t_mm2)
        s_ev2 = Sem(_s_ev2); t_nm2 = Sem(_t_nm2); v_nm2 = Sem(_v_nm2)
        d_sp2 = Sem(_d_sp2); s_bnd = Sem(_s_bnd); t_bnd = Sem(_t_bnd)
        v_bnd = Sem(_v_bnd); d_kvq = Sem(_d_kvq); t_sc = Sem(_t_sc)
        v_mk = Sem(_v_mk); s_xp = Sem(_s_xp); t_pv = Sem(_t_pv)
        v_ev3 = Sem(_v_ev3); s_ev3 = Sem(_s_ev3); t_bc3 = Sem(_t_bc3)
        d_w4 = Sem(_d_w4); d_x4 = Sem(_d_x4)
        t_mm4 = Sem(_t_mm4); v_ph4 = Sem(_v_ph4); s_ph4 = Sem(_s_ph4)
        d_out = Sem(_d_out)
        d_x1p = [Sem(_d_x1), Sem(_d_x1b)]
        d_x1sp = [Sem(_d_x1s), Sem(_d_x1sb)]
        d_w2p = [Sem(_d_w2), Sem(_d_w2b)]
        d_sp2p = [Sem(_d_sp2), Sem(_d_sp2b)]
        d_kvqp = [Sem(_d_kvq), Sem(_d_kvqb)]
        d_w4b = Sem(_d_w4b)
        t_pa = Sem(_t_pa); v_pa = Sem(_v_pa)
        d_outp = [Sem(_d_out), Sem(_d_outb)]
        d_dbgx = Sem(_d_dbgx); d_dbg = Sem(_d_dbg)

        ph = Ph(blk)

        def p0_dma(e):
            e.dma_start(out=c_mdiag[:, :], in_=mdiag[:, :]).then_inc(_d_cst, 16)
            e.dma_start(out=c_mpad[:, :], in_=mpad[:, :]).then_inc(_d_cst, 16)
            e.dma_start(out=c_ident[:, :], in_=ident_d[:, :]).then_inc(_d_cst, 16)
        ph.on("sync", p0_dma)
        cst_ld = d_cst.inc(48)

        def p0_v(e):
            e.memset(ones_col[:, :], 1.0)
            e.memset(ones128[:, :], 1.0)
            e.memset(eps_col[:, :], EPS)
            e.memset(ones_row[:, :], 1.0).then_inc(_v_ph1, 1)
        ph.on("vector", p0_v)
        ones_ready = (_v_ph1, v_ph1.inc(1))
        ident_ready = (_d_cst, cst_ld)
        ph.flush()

        with nc.sbuf_tensor("xn", [128, NC * S], BF16) as xn:
            xn3 = xn[:, :].rearrange("p (c t) -> p c t", t=S)

            with ExitStack() as _c1:
                xt0 = _c1.enter_context(nc.sbuf_tensor("xt0", [128, D], F32))
                xt1 = _c1.enter_context(nc.sbuf_tensor("xt1", [128, D], F32))
                xn_tm0 = _c1.enter_context(nc.sbuf_tensor("xn_tm0", [128, D], BF16))
                xn_tm1 = _c1.enter_context(nc.sbuf_tensor("xn_tm1", [128, D], BF16))
                sq0 = _c1.enter_context(nc.sbuf_tensor("sq0", [128, D], F32))
                sq1 = _c1.enter_context(nc.sbuf_tensor("sq1", [128, D], F32))
                sm1 = _c1.enter_context(nc.sbuf_tensor("sm1", [128, 1], F32))
                ssq1 = _c1.enter_context(nc.sbuf_tensor("ssq1", [128, 1], F32))
                mv = _c1.enter_context(nc.sbuf_tensor("mv", [128, 2], F32))
                sd1 = _c1.enter_context(nc.sbuf_tensor("sd1", [128, 1], F32))
                rstd1 = _c1.enter_context(nc.sbuf_tensor("rstd1", [128, 1], F32))
                tr_ps0 = _c1.enter_context(nc.psum_tensor("tr_ps0", [128, 512], BF16))
                tr_ps1 = _c1.enter_context(nc.psum_tensor("tr_ps1", [128, 512], BF16))
                xts = [xt0, xt1]
                xn_tms = [xn_tm0, xn_tm1]
                sqs = [sq0, sq1]
                tr_pss = [tr_ps0, tr_ps1]

                v_apply = [0] * NB
                v_aggr = [0] * NB
                s_sq = [0] * NB
                s_sqr = [0] * NB
                t_trg = []
                s_evg = []

                for i in range(NB):
                    pp = i % 2

                    def p1_dma(e, i=i, pp=pp, w=(None if i < 2 else (_v_ph1, v_apply[i - 2]))):
                        _w(e, w)
                        e.dma_start(out=xts[pp][:, 0:1024],
                                    in_=xp[128 * i:128 * (i + 1), 0:1024]).then_inc(d_x1p[pp].h, 16)
                    ph.on("sync", p1_dma)
                    x_ld = (d_x1p[pp].h, d_x1p[pp].inc(16))

                    def p1_dma2(e, i=i, pp=pp, w=(None if i < 2 else (_v_ph1, v_apply[i - 2]))):
                        _w(e, w)
                        e.dma_start(out=xts[pp][:, 1024:2048],
                                    in_=xp[128 * i:128 * (i + 1), 1024:2048]).then_inc(d_x1sp[pp].h, 16)
                    ph.on("scalar", p1_dma2)
                    x_ld2 = (d_x1sp[pp].h, d_x1sp[pp].inc(16))

                    def p1_sqr(e, i=i, pp=pp, x_ld=x_ld, x_ld2=x_ld2,
                               w=(None if i < 2 else (_v_ph1, v_aggr[i - 2]))):
                        _w(e, x_ld)
                        _w(e, x_ld2)
                        _w(e, w)
                        e.activation(sqs[pp][:, :], xts[pp][:, :], ACT.Square).then_inc(_s_sq1, 1)
                    ph.on("scalar", p1_sqr)
                    s_sqr[i] = s_sq1.inc(1)

                    va = v_ph1.inc(1)
                    vb = v_ph1.inc(1)

                    def p1_stats(e, i=i, pp=pp, x_ld=x_ld, x_ld2=x_ld2,
                                 wsq=s_sqr[i], va=va, vb=vb,
                                 w=(None if i < 1 else (_s_sq1, s_sq[i - 1]))):
                        _w(e, x_ld)
                        _w(e, x_ld2)
                        _w(e, w)
                        e.reduce_sum(sm1[:, :], xts[pp][:, :], axis=AX.X)
                        e.wait_ge(_s_sq1, wsq)
                        e.reduce_sum(ssq1[:, :], sqs[pp][:, :], axis=AX.X).then_inc(_v_ph1, 1)
                        e.wait_ge(_v_ph1, va)
                        e.tensor_scalar(mv[:, 0:1], sm1[:, :], 1.0 / D, None, ALU.mult)
                        e.tensor_scalar(mv[:, 1:2], ssq1[:, :], 1.0 / D, None, ALU.mult)
                        e.tensor_scalar(sm1[:, :], sm1[:, :], -1.0 / D, None,
                                        ALU.mult).then_inc(_v_ph1, 1)
                        e.wait_ge(_v_ph1, vb)
                        e.scalar_tensor_tensor(mv[:, 1:2], mv[:, 0:1], sm1[:, :], mv[:, 1:2],
                                               ALU.mult, ALU.add).then_inc(_v_ph1, 1)
                    ph.on("vector", p1_stats)
                    v_aggr[i] = v_ph1.inc(1)
                    aggr = v_aggr[i]

                    def p1_sqrt(e, aggr=aggr, w1=(ones_ready if i == 0 else None)):
                        _w(e, w1)
                        e.wait_ge(_v_ph1, aggr)
                        e.activation(sd1[:, :], mv[:, 1:2], ACT.Sqrt, bias=eps_col[:, :]).then_inc(_s_sq1, 1)
                    ph.on("scalar", p1_sqrt)
                    s_sq[i] = s_sq1.inc(1)

                    vr = v_ph1.inc(1)

                    def p1_apply(e, i=i, pp=pp, sq=s_sq[i], vr=vr,
                                 w=(None if i < 2 else (_t_tr1, None))):
                        e.wait_ge(_s_sq1, sq)
                        e.reciprocal(rstd1[:, :], sd1[:, :]).then_inc(_v_ph1, 1)
                        if i >= 2:
                            e.wait_ge(_t_tr1, t_trg[4 * (i - 2) + 3])
                        e.wait_ge(_v_ph1, vr)
                        e.tensor_scalar(xn_tms[pp][:, :], xts[pp][:, :], mv[:, 0:1], rstd1[:, :],
                                        ALU.subtract, ALU.mult).then_inc(_v_ph1, 1)
                    ph.on("vector", p1_apply)
                    v_apply[i] = v_ph1.inc(1)

                    for g in range(4):
                        gi = 4 * i + g
                        tpp = gi % 2

                        def p1_tr(e, i=i, pp=pp, g=g, gi=gi, tpp=tpp,
                                  wa=v_apply[i],
                                  we=(None if gi < 2 else (_s_ev1, None)),
                                  wid=(ident_ready if gi == 0 else None)):
                            _w(e, wid)
                            e.wait_ge(_v_ph1, wa)
                            if gi >= 2:
                                e.wait_ge(_s_ev1, s_evg[gi - 2])
                            for u in range(4):
                                c = 4 * g + u
                                e.transpose(tr_pss[tpp][:, 128 * u:128 * (u + 1)],
                                            xn_tms[pp][:, 128 * c:128 * (c + 1)],
                                            c_ident[:, :])
                            e.last.then_inc(_t_tr1, 1)
                        ph.on("tensor", p1_tr)
                        t_trg.append(t_tr1.inc(1))

                        def p1_ev(e, i=i, g=g, gi=gi, tpp=tpp, wt=t_trg[gi]):
                            e.wait_ge(_t_tr1, wt)
                            dst = xn3[:, 4 * g:4 * (g + 1), 128 * i:128 * (i + 1)]
                            src = tr_pss[tpp][:, :].rearrange("p (u t) -> p u t", t=128)
                            e.copy(out=dst, in_=src).then_inc(_s_ev1, 1)
                        ph.on("scalar", p1_ev)
                        s_evg.append(s_ev1.inc(1))
                    if i % 4 == 3:
                        ph.flush()
                ph.flush()
                xn_ready = (_s_ev1, s_ev1.n)

                def xn_gate(sg):
                    last_tile = 16 * sg + 15
                    return (_s_ev1, s_evg[4 * last_tile + 3])

            wvb0_cm = nc.sbuf_tensor("wvb0", [128, NC * 1024], BF16)
            wvb0 = wvb0_cm.__enter__()
            wv3 = wvt[:, :].rearrange("(c p) v -> p c v", p=128)

            def wvb0_dma(e, w=xn_ready):
                _w(e, w)
                e.dma_start(out=wvb0[:, :].rearrange("p (c u) -> p c u", u=1024),
                            in_=wv3[:, :, 0:1024]).then_inc(d_w2p[0].h, 16)
            ph.on("gpsimd", wvb0_dma)
            wvb0_ld = (d_w2p[0].h, d_w2p[0].inc(16))

            with ExitStack() as _c2:
                wb0 = _c2.enter_context(nc.sbuf_tensor("wb0", [128, D], BF16))
                wb1 = _c2.enter_context(nc.sbuf_tensor("wb1", [128, D], BF16))
                kst0 = _c2.enter_context(nc.sbuf_tensor("kst0", [128, 2048], BF16))
                kst1 = _c2.enter_context(nc.sbuf_tensor("kst1", [128, 2048], BF16))
                kps0 = _c2.enter_context(nc.psum_tensor("kps0", [128, 2048], F32))
                kps1 = _c2.enter_context(nc.psum_tensor("kps1", [128, 2048], F32))
                wbs = [wb0, wb1]
                ksts = [kst0, kst1]
                kpss = [kps0, kps1]

                mm_f = []
                ev_f = []
                sp_f = []
                ld_f = []
                lastmm_of_o = []

                def emit_proj(wpack, scr, ntok, rhs_for, scale, gate):
                    nsg = ntok // 2048
                    base = len(ld_f)
                    for o in range(16):
                        pp = o % 2
                        oo = base + o

                        def w_dma(e, o=o, pp=pp, oo=oo,
                                  wxn=(xn_ready if oo < 2 else None)):
                            _w(e, wxn)
                            if oo >= 2:
                                e.wait_ge(_t_mm2, lastmm_of_o[oo - 2])
                            e.dma_start(
                                out=wbs[pp][:, :].rearrange("p (c u) -> p c u", u=128),
                                in_=wpack[o].rearrange("(c p) u -> p c u", p=128),
                            ).then_inc(d_w2p[pp].h, 16)
                        ph.on("gpsimd", w_dma)
                        ld_f.append((d_w2p[pp].h, d_w2p[pp].inc(16)))

                        for sg in range(nsg):
                            st = len(mm_f)
                            spp = st % 2

                            def p2_mm(e, o=o, pp=pp, sg=sg, st=st, spp=spp, oo=oo,
                                      rhs_for=rhs_for,
                                      wxn=(gate(sg) if oo in (0, 1) else None)):
                                _w(e, wxn)
                                _w(e, ld_f[oo])
                                if st >= 2:
                                    e.wait_ge(_s_ev2, ev_f[st - 2])
                                for c in range(16):
                                    for s4 in range(4):
                                        e.matmul(kpss[spp][:, 512 * s4:512 * (s4 + 1)],
                                                 wbs[pp][:, 128 * c:128 * (c + 1)],
                                                 rhs_for(c, sg * 2048 + 512 * s4, 512),
                                                 start=(c == 0), stop=(c == 15))
                                e.last.then_inc(_t_mm2, 1)
                            ph.on("tensor", p2_mm)
                            mm_f.append(t_mm2.inc(1))

                            def p2_ev(e, st=st, spp=spp, scale=scale):
                                e.wait_ge(_t_mm2, mm_f[st])
                                if st >= 2:
                                    _w(e, sp_f[st - 2])
                                e.activation(ksts[spp][:, :], kpss[spp][:, :], ACT.Copy,
                                             scale=scale).then_inc(_s_ev2, 1)
                            ph.on("scalar", p2_ev)
                            ev_f.append(s_ev2.inc(1))

                            def p2_sp(e, o=o, sg=sg, st=st, spp=spp, scr=scr):
                                e.wait_ge(_s_ev2, ev_f[st])
                                e.dma_start(out=scr[128 * o:128 * (o + 1), 2048 * sg:2048 * (sg + 1)],
                                            in_=ksts[spp][:, :]).then_inc(d_sp2p[spp].h, 16)
                            ph.on("sync", p2_sp)
                            sp_f.append((d_sp2p[spp].h, d_sp2p[spp].inc(16)))

                        lastmm_of_o.append(mm_f[-1])
                        if o % 4 == 3:
                            ph.flush()
                    ph.flush()

                def k_rhs(c, lo, width):
                    return xn[:, c * S + lo: c * S + lo + width]

                xq = xn3.rearrange("p c (j u) -> p c j u", u=128)

                def q_rhs(c, lo, width):
                    j0 = lo // 128
                    nj = width // 128
                    return xq[:, c, 1 + 2 * j0: 2 * (j0 + nj):2, :]

                emit_proj(wkp, k_scr, S, k_rhs, 1.0, xn_gate)
                emit_proj(wqp, q_scr, NTOK, q_rhs, ISQ, lambda sg: xn_ready)
                kq_spills = [(d_sp2p[0].h, d_sp2p[0].n), (d_sp2p[1].h, d_sp2p[1].n)]
                kq_mm_last = (_t_mm2, t_mm2.n)
                kq_ev_last = (_s_ev2, s_ev2.n)


            with ExitStack() as _c2v:
                wvb1 = _c2v.enter_context(nc.sbuf_tensor("wvb1", [128, NC * 1024], BF16))
                vst0 = _c2v.enter_context(nc.sbuf_tensor("vst0", [128, 1024], BF16))
                vst1 = _c2v.enter_context(nc.sbuf_tensor("vst1", [128, 1024], BF16))
                vps0 = _c2v.enter_context(nc.psum_tensor("vps0", [128, 1024], F32))
                vps1 = _c2v.enter_context(nc.psum_tensor("vps1", [128, 1024], F32))
                wvbs = [wvb0, wvb1]
                vsts = [vst0, vst1]
                vpss = [vps0, vps1]

                vld = [0] * 2
                vld[0] = wvb0_ld
                vmm_f = []
                vev_f = []
                vsp_f = []
                def v1_dma(e, wsp=tuple(kq_spills), wnm=kq_mm_last):
                    for t in wsp:
                        _w(e, t)
                    _w(e, wnm)
                    e.dma_start(out=wvb1[:, :].rearrange("p (c u) -> p c u", u=1024),
                                in_=wv3[:, :, 1024:2048]).then_inc(d_w2p[1].h, 16)
                ph.on("gpsimd", v1_dma)
                vld[1] = (d_w2p[1].h, d_w2p[1].inc(16))

                for h2 in range(2):
                    hpp = h2 % 2

                    for i in range(NB):
                        st = len(vmm_f)
                        pp = st % 2

                        def v_mm(e, h2=h2, hpp=hpp, i=i, st=st, pp=pp,
                                 wb=(kq_ev_last if st == 0 else None)):
                            _w(e, wb)
                            _w(e, vld[h2])
                            if st >= 2:
                                e.wait_ge(_s_ev2, vev_f[st - 2])
                            for c in range(16):
                                for s2 in range(2):
                                    e.matmul(vpss[pp][:, 512 * s2:512 * (s2 + 1)],
                                             xn[:, c * S + 128 * i: c * S + 128 * (i + 1)],
                                             wvbs[hpp][:, 1024 * c + 512 * s2:1024 * c + 512 * (s2 + 1)],
                                             start=(c == 0), stop=(c == 15))
                            e.last.then_inc(_t_mm2, 1)
                        ph.on("tensor", v_mm)
                        vmm_f.append(t_mm2.inc(1))

                        def v_ev(e, st=st, pp=pp):
                            e.wait_ge(_t_mm2, vmm_f[st])
                            if st >= 2:
                                _w(e, vsp_f[st - 2])
                            e.activation(vsts[pp][:, :], vpss[pp][:, :], ACT.Copy).then_inc(_s_ev2, 1)
                        ph.on("scalar", v_ev)
                        vev_f.append(s_ev2.inc(1))

                        def v_sp(e, h2=h2, i=i, st=st, pp=pp):
                            e.wait_ge(_s_ev2, vev_f[st])
                            e.dma_start(out=v_scr[128 * i:128 * (i + 1), 1024 * h2:1024 * (h2 + 1)],
                                        in_=vsts[pp][:, :]).then_inc(d_sp2p[pp].h, 16)
                        ph.on("sync", v_sp)
                        vsp_f.append((d_sp2p[pp].h, d_sp2p[pp].inc(16)))
                        if i % 8 == 7:
                            ph.flush()
                    ph.flush()
                all_spills = [(d_sp2p[0].h, d_sp2p[0].n), (d_sp2p[1].h, d_sp2p[1].n)]

            dbg_xn_done = None
            if debug:
                def dbg_xn(e, w=tuple(all_spills)):
                    for t in w:
                        _w(e, t)
                    e.dma_start(out=xn_dbg[:, :], in_=xn[:, :]).then_inc(_d_dbgx, 16)
                ph.on("gpsimd", dbg_xn)
                dbg_xn_done = (_d_dbgx, d_dbgx.inc(16))
                ph.flush()

            wvb0_cm.__exit__(None, None, None)

        with ExitStack() as _c3o:
            attn_sb = _c3o.enter_context(nc.sbuf_tensor("attn_sb", [128, NH * NTOK], BF16))
            wo_bf = _c3o.enter_context(nc.sbuf_tensor("wo_bf", [128, NC * D], BF16))

            with ExitStack() as _c3:
                kb0 = _c3.enter_context(nc.sbuf_tensor("kb0", [128, S], BF16))
                kb1 = _c3.enter_context(nc.sbuf_tensor("kb1", [128, S], BF16))
                vb0 = _c3.enter_context(nc.sbuf_tensor("vb0", [128, S], BF16))
                vb1 = _c3.enter_context(nc.sbuf_tensor("vb1", [128, S], BF16))
                qb0 = _c3.enter_context(nc.sbuf_tensor("qb0", [128, NTOK], BF16))
                qb1 = _c3.enter_context(nc.sbuf_tensor("qb1", [128, NTOK], BF16))
                pT0 = _c3.enter_context(nc.sbuf_tensor("pT0", [128, 1024], BF16))
                pT1 = _c3.enter_context(nc.sbuf_tensor("pT1", [128, 1024], BF16))
                pT2 = _c3.enter_context(nc.sbuf_tensor("pT2", [128, 1024], BF16))
                bc_sb = _c3.enter_context(nc.sbuf_tensor("bc_sb", [128, 1024], F32))
                negm0 = _c3.enter_context(nc.sbuf_tensor("negm0", [128, D], BF16))
                negm1 = _c3.enter_context(nc.sbuf_tensor("negm1", [128, D], BF16))
                skq0 = _c3.enter_context(nc.sbuf_tensor("skq0", [128, 1024], BF16))
                skq1 = _c3.enter_context(nc.sbuf_tensor("skq1", [128, 1024], BF16))
                kslot = _c3.enter_context(nc.sbuf_tensor("kslot", [128, NSLOT], F32))
                ksc1 = _c3.enter_context(nc.sbuf_tensor("ksc1", [128, NSLOT], F32))
                ksc2 = _c3.enter_context(nc.sbuf_tensor("ksc2", [128, NSLOT], F32))
                nrm = _c3.enter_context(nc.sbuf_tensor("nrm", [128, 1024], F32))
                sqt = _c3.enter_context(nc.sbuf_tensor("sqt", [128, 1024], F32))
                attn_ps = _c3.enter_context(nc.psum_tensor("attn_ps", [128, 1024], F32))
                sT0 = _c3.enter_context(nc.psum_tensor("sT0", [128, 1024], F32))
                sT1 = _c3.enter_context(nc.psum_tensor("sT1", [128, 1024], F32))
                rs_ps = _c3.enter_context(nc.psum_tensor("rs_ps", [128, 1024], F32))
                kbs = [kb0, kb1]; vbs = [vb0, vb1]; qbs = [qb0, qb1]
                pTs = [pT0, pT1, pT2]; sTs = [sT0, sT1]; negms = [negm0, negm1]
                skq = [skq0, skq1]

                head_ld = [0] * NH
                head_t_done = [0] * NH

                def emit_head_load(h, extra=()):
                    hp2 = h % 2

                    def ld(e, h=h, hp2=hp2, extra=tuple(extra)):
                        for t in extra:
                            _w(e, t)
                        if h >= 2:
                            e.wait_ge(_t_pv, head_t_done[h - 2])
                        e.dma_start(out=kbs[hp2][:, :],
                                    in_=k_scr[128 * h:128 * (h + 1), :]).then_inc(d_kvqp[hp2].h, 16)
                        e.dma_start(out=vbs[hp2][:, :].rearrange("p (t u) -> p t u", u=128),
                                    in_=v_scr[:, 128 * h:128 * (h + 1)].rearrange("(t p) u -> p t u", p=128),
                                    ).then_inc(d_kvqp[hp2].h, 16)
                        e.dma_start(out=qbs[hp2][:, :],
                                    in_=q_scr[128 * h:128 * (h + 1), :]).then_inc(d_kvqp[hp2].h, 16)
                    ph.on("sync", ld)
                    head_ld[h] = (d_kvqp[hp2].h, d_kvqp[hp2].inc(48))

                emit_head_load(0, extra=all_spills)

                h0 = head_ld[0]

                def w4_dma(e, w=h0, wd=dbg_xn_done):
                    _w(e, w)
                    _w(e, wd)
                    for c in range(16):
                        e.dma_start(out=wo_bf[:, D * c:D * (c + 1)],
                                    in_=wot[128 * c:128 * (c + 1), :]).then_inc(_d_w4, 16)
                ph.on("gpsimd", w4_dma)
                w4_ready = (_d_w4, d_w4.inc(16 * 16))
                ph.flush()

                last_sT_exp = [None, None]
                last_sT_red = [None, None]
                last_pT_rd = [None, None, None]
                last_attn_rd = None
                last_rs_rd = None
                last_rs_dbg = None
                negm_last_rd = [None, None]
                negm_dbg_rd = [None, None]
                negm_rdy = [None] * NH
                sT_step = [0]
                skq_rd = [None, None]
                nrm_rd = [None]

                def makeA(h):
                    hp = h % 2
                    thunks = []
                    state = {"first": True}

                    def kn2_thunk(c):
                        def emit():
                            pc = c % 2
                            pp = sT_step[0] % 2
                            sT_step[0] += 1
                            wld = head_ld[h] if state["first"] else None
                            state["first"] = False
                            wsq_rd = skq_rd[pc]
                            w1 = last_sT_exp[pp]
                            w2 = last_sT_red[pp]

                            def pa_sq(e, c=c, pc=pc, hp=hp, wld=wld, wsq_rd=wsq_rd):
                                _w(e, wld)
                                _w(e, wsq_rd)
                                e.activation(skq[pc][:, :],
                                             kbs[hp][:, 1024 * c:1024 * (c + 1)],
                                             ACT.Square).then_inc(_s_xp, 1)
                            ph.on("scalar", pa_sq)
                            sq_v = s_xp.inc(1)

                            def pa_mm(e, pc=pc, pp=pp, sq_v=sq_v, w1=w1, w2=w2,
                                      won=(ones_ready if (h == 0 and c == 0) else None)):
                                _w(e, won)
                                e.wait_ge(_s_xp, sq_v)
                                _w(e, w1)
                                _w(e, w2)
                                ins = None
                                for lo, hi in _bank_chunks(0, 1024):
                                    ins = e.matmul(sTs[pp][:, lo:hi], ones128[:, :],
                                                   skq[pc][:, lo:hi],
                                                   start=True, stop=True)
                                ins.then_inc(_t_pa, 1)
                            ph.on("tensor", pa_mm)
                            mm_v = t_pa.inc(1)
                            skq_rd[pc] = (_t_pa, mm_v)

                            def pa_red(e, c=c, pp=pp, mm_v=mm_v):
                                e.wait_ge(_t_pa, mm_v)
                                e.tensor_reduce(
                                    kslot[:, 4 * c:4 * (c + 1)],
                                    sTs[pp][:, 0:1024].rearrange("p (j u) -> p j u", u=256),
                                    axis=AX.X, op=ALU.max).then_inc(_v_pa, 1)
                            ph.on("vector", pa_red)
                            last_sT_red[pp] = (_v_pa, v_pa.inc(1))
                        return emit

                    def scan_thunk():
                        def emit():
                            vred = v_pa.n

                            def pa_scan(e, vred=vred):
                                e.wait_ge(_v_pa, vred)
                                e.tensor_tensor(ksc1[:, 1:16], kslot[:, 1:16],
                                                kslot[:, 0:15], ALU.max).then_inc(_v_pa, 1)
                                e.tensor_copy(out=ksc1[:, 0:1],
                                              in_=kslot[:, 0:1]).then_inc(_v_pa, 1)
                                e.wait_ge(_v_pa, vred + 2)
                                e.tensor_tensor(ksc2[:, 2:16], ksc1[:, 2:16],
                                                ksc1[:, 0:14], ALU.max).then_inc(_v_pa, 1)
                                e.tensor_copy(out=ksc2[:, 0:2],
                                              in_=ksc1[:, 0:2]).then_inc(_v_pa, 1)
                                e.wait_ge(_v_pa, vred + 4)
                                e.tensor_tensor(ksc1[:, 4:16], ksc2[:, 4:16],
                                                ksc2[:, 0:12], ALU.max).then_inc(_v_pa, 1)
                                e.tensor_copy(out=ksc1[:, 0:4],
                                              in_=ksc2[:, 0:4]).then_inc(_v_pa, 1)
                                e.wait_ge(_v_pa, vred + 6)
                                e.tensor_tensor(ksc2[:, 8:16], ksc1[:, 8:16],
                                                ksc1[:, 0:8], ALU.max).then_inc(_v_pa, 1)
                                e.tensor_copy(out=ksc2[:, 0:8],
                                              in_=ksc1[:, 0:8]).then_inc(_v_pa, 1)
                            ph.on("vector", pa_scan)
                            state["scan_v"] = v_pa.inc(8)
                        return emit

                    def qn2_thunk(c):
                        def emit():
                            pc = c % 2
                            pp = sT_step[0] % 2
                            sT_step[0] += 1
                            wsq_rd = skq_rd[pc]
                            w1 = last_sT_exp[pp]
                            w2 = last_sT_red[pp]

                            def pa_sqq(e, c=c, pc=pc, hp=hp, wsq_rd=wsq_rd):
                                _w(e, wsq_rd)
                                e.activation(skq[pc][:, :],
                                             qbs[hp][:, 1024 * c:1024 * (c + 1)],
                                             ACT.Square).then_inc(_s_xp, 1)
                            ph.on("scalar", pa_sqq)
                            sq_v = s_xp.inc(1)

                            def pa_mmq(e, pc=pc, pp=pp, sq_v=sq_v, w1=w1, w2=w2):
                                e.wait_ge(_s_xp, sq_v)
                                _w(e, w1)
                                _w(e, w2)
                                ins = None
                                for lo, hi in _bank_chunks(0, 1024):
                                    ins = e.matmul(sTs[pp][:, lo:hi], ones128[:, :],
                                                   skq[pc][:, lo:hi],
                                                   start=True, stop=True)
                                ins.then_inc(_t_pa, 1)
                            ph.on("tensor", pa_mmq)
                            mm_v = t_pa.inc(1)
                            skq_rd[pc] = (_t_pa, mm_v)

                            def pa_mul(e, c=c, pp=pp, mm_v=mm_v,
                                       scv=state["scan_v"], wnr=nrm_rd[0]):
                                e.wait_ge(_t_pa, mm_v)
                                e.wait_ge(_v_pa, scv)
                                _w(e, wnr)
                                e.tensor_tensor(
                                    nrm[:, :].rearrange("p (j u) -> p j u", u=128),
                                    sTs[pp][:, 0:1024].rearrange("p (j u) -> p j u", u=128),
                                    ksc2[:, 8 * c:8 * (c + 1)].unsqueeze(2).to_broadcast([128, 8, 128]),
                                    ALU.mult).then_inc(_v_pa, 1)
                            ph.on("vector", pa_mul)
                            mul_v = v_pa.inc(1)
                            last_sT_red[pp] = (_v_pa, mul_v)

                            sq2_v = s_xp.inc(1)
                            af_v = s_xp.inc(1)

                            def pa_sqrt(e, c=c, hp=hp, mul_v=mul_v, sq2_v=sq2_v,
                                        wn=(negm_last_rd[hp] if c == 0 else None),
                                        wnd=(negm_dbg_rd[hp] if c == 0 else None)):
                                e.wait_ge(_v_pa, mul_v)
                                e.activation(sqt[:, :], nrm[:, :],
                                             ACT.Sqrt).then_inc(_s_xp, 1)
                                _w(e, wn)
                                _w(e, wnd)
                                e.wait_ge(_s_xp, sq2_v)
                                e.activation(negms[hp][:, 1024 * c:1024 * (c + 1)],
                                             sqt[:, :], ACT.Copy, bias=MSHIFT,
                                             scale=-1.0).then_inc(_s_xp, 1)
                            ph.on("scalar", pa_sqrt)
                            nrm_rd[0] = (_s_xp, sq2_v)
                            if c == 1:
                                negm_rdy[h] = (_s_xp, af_v)
                        return emit

                    for c in range(4):
                        thunks.append(kn2_thunk(c))
                    thunks.append(scan_thunk())
                    thunks.append(qn2_thunk(0))
                    thunks.append(qn2_thunk(1))

                    if debug:
                        def dbg_thunk():
                            def pa_dbg(e, h=h, hp=hp, w=negm_rdy[h]):
                                _w(e, w)
                                e.dma_start(out=negm_dbg[h:h + 1, :],
                                            in_=negms[hp][0:1, :]).then_inc(_d_dbg, 16)
                            ph.on("gpsimd", pa_dbg)
                            negm_dbg_rd[hp] = (_d_dbg, d_dbg.inc(16))
                        thunks.append(dbg_thunk)
                    return thunks

                for t in makeA(0):
                    t()
                ph.flush()

                for h in range(NH):
                    hp = h % 2
                    if h + 1 < NH:
                        emit_head_load(h + 1)
                        thunksA = makeA(h + 1)
                    else:
                        thunksA = []
                    ai = 0

                    for g in range(2):
                        kmax = 16 * g + 16
                        sc_d = [0] * kmax
                        mk_d = [0] * kmax
                        xp_d = [0] * kmax
                        pv_d = [0] * kmax
                        kap_pp = [0] * kmax
                        kap_qp = [0] * kmax

                        def geom(kap, g=g):
                            jmin = max(8 * g, kap // 2)
                            astart = (jmin - 8 * g) * 128
                            return astart, 1024 - astart

                        def emit_pv(kap, g=g, h=h, hp=hp, kmax=kmax, pv_d=pv_d,
                                    xp_d=xp_d, kap_pp=kap_pp, kap_qp=kap_qp):
                            astart, width = geom(kap)
                            pp = kap_qp[kap]
                            wattn = last_attn_rd if kap == 0 else None
                            wrs = last_rs_rd if kap == 0 else None

                            def p3_pv(e, kap=kap, pp=pp, hp=hp, astart=astart,
                                      g=g, kmax=kmax, wattn=wattn, wrs=wrs,
                                      xpv=xp_d[kap]):
                                e.wait_ge(_s_xp, xpv)
                                _w(e, wattn)
                                _w(e, wrs)
                                for lo, hi in _bank_chunks(astart, 1024):
                                    stp = (kap == 16 * g + 8 * (lo // 512) + 7)
                                    e.matmul(rs_ps[:, lo:hi],
                                             ones128[:, :],
                                             pTs[pp][:, lo:hi],
                                             start=(kap == 0), stop=stp)
                                ins = None
                                for lo, hi in _bank_chunks(astart, 1024):
                                    stp = (kap == 16 * g + 8 * (lo // 512) + 7)
                                    ins = e.matmul(attn_ps[:, lo:hi],
                                                   vbs[hp][:, 128 * kap:128 * (kap + 1)],
                                                   pTs[pp][:, lo:hi],
                                                   start=(kap == 0), stop=stp)
                                ins.then_inc(_t_pv, 1)
                            ph.on("tensor", p3_pv)
                            pv_d[kap] = t_pv.inc(1)
                            last_pT_rd[pp] = (_t_pv, pv_d[kap])

                        for kap in range(kmax):
                            astart, width = geom(kap)
                            pp = sT_step[0] % 2
                            sT_step[0] += 1
                            kap_pp[kap] = pp
                            qp = kap % 3
                            kap_qp[kap] = qp
                            has_mask = (kap // 2 >= 8 * g)

                            w1 = last_sT_exp[pp]
                            w2 = last_sT_red[pp]
                            wnm = (negm_rdy[h] if (g == 0 and kap == 0) else None)

                            def p3_sc(e, kap=kap, pp=pp, hp=hp, g=g, astart=astart,
                                      w1=w1, w2=w2):
                                _w(e, w1)
                                _w(e, w2)
                                ins = None
                                for lo, hi in _bank_chunks(astart, 1024):
                                    ins = e.matmul(sTs[pp][:, lo:hi],
                                                   kbs[hp][:, 128 * kap:128 * (kap + 1)],
                                                   qbs[hp][:, 1024 * g + lo:1024 * g + hi],
                                                   start=True, stop=True)
                                ins.then_inc(_t_sc, 1)
                            ph.on("tensor", p3_sc)
                            sc_d[kap] = t_sc.inc(1)

                            mk_base = v_mk.n

                            def p3_mk(e, kap=kap, pp=pp, hp=hp, g=g, astart=astart,
                                      sc=sc_d[kap], has_mask=has_mask, wnm=wnm,
                                      mk_base=mk_base):
                                e.wait_ge(_t_sc, sc)
                                _w(e, wnm)
                                e.tensor_tensor(sTs[pp][:, astart:1024],
                                                sTs[pp][:, astart:1024],
                                                negms[hp][:, 1024 * g + astart:1024 * (g + 1)],
                                                ALU.add).then_inc(_v_mk, 1)
                                if has_mask:
                                    e.wait_ge(_v_mk, mk_base + 1)
                                    msk = c_mdiag if (kap % 2 == 1) else c_mpad
                                    e.tensor_tensor(sTs[pp][:, astart:astart + 128],
                                                    sTs[pp][:, astart:astart + 128],
                                                    msk[:, :], ALU.add).then_inc(_v_mk, 1)
                            ph.on("vector", p3_mk)
                            mk_d[kap] = v_mk.inc(2 if has_mask else 1)

                            wpT = last_pT_rd[qp]

                            def p3_xp(e, kap=kap, pp=pp, qp=qp, astart=astart,
                                      mk=mk_d[kap], wpT=wpT):
                                e.wait_ge(_v_mk, mk)
                                _w(e, wpT)
                                e.activation(pTs[qp][:, astart:1024],
                                             sTs[pp][:, astart:1024],
                                             ACT.Exp).then_inc(_s_xp, 1)
                            ph.on("scalar", p3_xp)
                            xp_d[kap] = s_xp.inc(1)
                            last_sT_exp[pp] = (_s_xp, xp_d[kap])

                            if kap >= 2:
                                emit_pv(kap - 2)

                            if g == 1 and kap % 2 == 0 and kap < 16 and ai < len(thunksA):
                                thunksA[ai]()
                                ai += 1

                        emit_pv(kmax - 2)
                        emit_pv(kmax - 1)

                        ra = v_ev3.inc(1)

                        def p3_rs(e, pv=pv_d[kmax - 1], wd=last_rs_dbg, ra=ra, h=h, g=g,
                                  wxd=(dbg_xn_done if (h == 0 and g == 0) else None)):
                            e.wait_ge(_t_pv, pv)
                            _w(e, wd)
                            _w(e, wxd)
                            e.reciprocal(bc_sb[:, :], rs_ps[:, 0:1024]).then_inc(_v_ev3, 1)
                            e.wait_ge(_v_ev3, ra)
                            e.tensor_tensor(attn_sb[:, h * NTOK + 1024 * g: h * NTOK + 1024 * (g + 1)],
                                            attn_ps[:, :], bc_sb[:, :], ALU.mult).then_inc(_v_ev3, 1)
                        ph.on("vector", p3_rs)
                        last_rs_rd = (_v_ev3, ra)
                        last_attn_rd = (_v_ev3, v_ev3.inc(1))

                        if debug:
                            def p3_dbg(e, h=h, g=g, w=last_attn_rd):
                                _w(e, w)
                                e.dma_start(out=rs_dbg[0:1, h * NTOK + 1024 * g: h * NTOK + 1024 * (g + 1)],
                                            in_=bc_sb[0:1, :]).then_inc(_d_dbg, 16)
                            ph.on("gpsimd", p3_dbg)
                            last_rs_dbg = (_d_dbg, d_dbg.inc(16))
                        ph.flush()

                    while ai < len(thunksA):
                        thunksA[ai]()
                        ai += 1
                    ph.flush()
                    negm_last_rd[hp] = (_v_mk, v_mk.n)
                    head_t_done[h] = t_pv.n

                attn_all = last_attn_rd
                ph.flush()

            if debug:
                def dbg_attn(e, w=attn_all):
                    _w(e, w)
                    e.dma_start(out=attn_dbg[:, :], in_=attn_sb[:, :]).then_inc(_d_dbg, 16)
                ph.on("gpsimd", dbg_attn)
                d_dbg.inc(16)
                ph.flush()

            with ExitStack() as _c4:
                at_bf = _c4.enter_context(nc.sbuf_tensor("at_bf", [128, NC * ER], BF16))
                bw_bf = _c4.enter_context(nc.sbuf_tensor("bw_bf", [128, 4 * D], BF16))
                wg_bf = _c4.enter_context(nc.sbuf_tensor("wg_bf", [128, NC * E], BF16))
                x4t = _c4.enter_context(nc.sbuf_tensor("x4t", [128, D], F32))
                x1t = _c4.enter_context(nc.sbuf_tensor("x1t", [128, D], F32))
                xn2tm = _c4.enter_context(nc.sbuf_tensor("xn2tm", [128, D], BF16))
                xn2fm = _c4.enter_context(nc.sbuf_tensor("xn2fm", [128, D], BF16))
                sq4 = _c4.enter_context(nc.sbuf_tensor("sq4", [128, D], F32))
                sm4 = _c4.enter_context(nc.sbuf_tensor("sm4", [128, 1], F32))
                ssq4 = _c4.enter_context(nc.sbuf_tensor("ssq4", [128, 1], F32))
                mv4 = _c4.enter_context(nc.sbuf_tensor("mv4", [128, 2], F32))
                sd4 = _c4.enter_context(nc.sbuf_tensor("sd4", [128, 1], F32))
                rstd4 = _c4.enter_context(nc.sbuf_tensor("rstd4", [128, 1], F32))
                lg = _c4.enter_context(nc.sbuf_tensor("lg", [128, E], F32))
                nmx8 = _c4.enter_context(nc.sbuf_tensor("nmx8", [128, 1], F32))
                mask8 = _c4.enter_context(nc.sbuf_tensor("mask8", [128, E], BF16))
                dlg = _c4.enter_context(nc.sbuf_tensor("dlg", [128, E], F32))
                p8 = _c4.enter_context(nc.sbuf_tensor("p8", [128, E], F32))
                se8 = _c4.enter_context(nc.sbuf_tensor("se8", [128, 1], F32))
                fscale = _c4.enter_context(nc.sbuf_tensor("fscale", [128, 1], F32))
                hbf = _c4.enter_context(nc.sbuf_tensor("hbf", [128, ER], BF16))
                hT = _c4.enter_context(nc.sbuf_tensor("hT", [128, ER], BF16))
                x2t0 = _c4.enter_context(nc.sbuf_tensor("x2t0", [128, D], F32))
                big = _c4.enter_context(nc.psum_tensor("big", [128, 2048], F32))
                yps = _c4.enter_context(nc.psum_tensor("yps", [128, 1024], F32))
                aux = _c4.enter_context(nc.psum_tensor("aux", [128, 512], F32))
                trx = _c4.enter_context(nc.psum_tensor("trx", [128, 512], BF16))
                x2ts = [x2t0, x2t0]

                def ffn_w_dma(e, w=attn_all):
                    _w(e, w)
                    for c in range(16):
                        e.dma_start(out=at_bf[:, ER * c:ER * (c + 1)],
                                    in_=att[128 * c:128 * (c + 1), :]).then_inc(_d_w4b, 16)
                    for c in range(4):
                        e.dma_start(out=bw_bf[:, D * c:D * (c + 1)],
                                    in_=bwt[128 * c:128 * (c + 1), :]).then_inc(_d_w4b, 16)
                    e.dma_start(out=wg_bf[:, :].rearrange("p (c u) -> p c u", u=E),
                                in_=wgt[:, :].rearrange("(c p) u -> p c u", p=128)).then_inc(_d_w4b, 16)
                ph.on("gpsimd", ffn_w_dma)
                ffn_w_ready = (_d_w4b, d_w4b.inc(16 * 21))
                ph.flush()

                x_ld = [0] * NSLOT
                t_oproj = [0] * NSLOT
                t_ophs = [[0, 0] for _ in range(NSLOT)]
                v_x1 = [0] * NSLOT
                s_ln = [0] * NSLOT
                v_xn2 = [0] * NSLOT
                trp_t = [[0] * 4 for _ in range(NSLOT)]
                trp_s = [[0] * 4 for _ in range(NSLOT)]
                t_lg = [0] * NSLOT
                v_lg = [0] * NSLOT
                s_g8 = [0] * NSLOT
                v_g8 = [0] * NSLOT
                t_h = [0] * NSLOT
                v_h = [0] * NSLOT
                t_hT = [0] * NSLOT
                s_hT = [0] * NSLOT
                t_y = [[0, 0] for _ in range(NSLOT)]
                v_x2 = [[0, 0] for _ in range(NSLOT)]
                d_st = [0] * NSLOT

                def emit_front(j):
                    def x4_dma(e, j=j, wa=(attn_all if j == 0 else None)):
                        _w(e, wa)
                        if j >= 1:
                            e.wait_ge(_v_ph4, v_x1[j - 1])
                        e.dma_start(out=x4t[:, :],
                                    in_=xp[128 * (2 * j + 1):128 * (2 * j + 2), :]).then_inc(_d_x4, 16)
                    ph.on("scalar", x4_dma)
                    x_ld[j] = (_d_x4, d_x4.inc(16))

                    def oproj(e, j=j,
                              ww=(w4_ready if j == 0 else None),
                              wa=(attn_all if j == 0 else None),
                              wb=(None if j == 0 else (_v_ph4, v_x1[j - 1]))):
                        _w(e, ww)
                        _w(e, wa)
                        _w(e, wb)
                        for hh in range(16):
                            for s4 in range(4):
                                e.matmul(big[:, 512 * s4:512 * (s4 + 1)],
                                         attn_sb[:, hh * NTOK + 128 * j:hh * NTOK + 128 * (j + 1)],
                                         wo_bf[:, D * hh + 512 * s4: D * hh + 512 * (s4 + 1)],
                                         start=(hh == 0), stop=(hh == 15))
                        e.last.then_inc(_t_mm4, 1)
                    ph.on("tensor", oproj)
                    t_ophs[j][0] = t_mm4.n
                    t_ophs[j][1] = t_mm4.n + 1
                    t_oproj[j] = t_mm4.inc(1)
                    t_ophs[j][0] = t_oproj[j]
                    t_ophs[j][1] = t_oproj[j]

                def emit_vecA(j):
                    xa = v_ph4.inc(1)
                    xb = v_ph4.inc(1)

                    def x1ev(e, j=j, wtA=t_ophs[j][0], wtB=t_ophs[j][1], wx=x_ld[j], xa=xa, xb=xb):
                        e.wait_ge(_t_mm4, wtA)
                        _w(e, wx)
                        if j >= 1:
                            e.wait_ge(_s_ph4, s_ln[j - 1])
                            e.wait_ge(_v_ph4, v_x2[j - 1][1])
                        e.tensor_tensor(x1t[:, 0:1024], big[:, 0:1024],
                                        x4t[:, 0:1024], ALU.add).then_inc(_v_ph4, 1)
                        e.wait_ge(_t_mm4, wtB)
                        e.tensor_tensor(x1t[:, 1024:2048], big[:, 1024:2048],
                                        x4t[:, 1024:2048], ALU.add).then_inc(_v_ph4, 1)
                        e.wait_ge(_v_ph4, xb)
                        e.reduce_sum(sm4[:, :], x1t[:, :], axis=AX.X).then_inc(_v_ph4, 1)
                    ph.on("vector", x1ev)
                    v_x1[j] = v_ph4.inc(1)

                    def x1sqr(e, j=j, w=v_x1[j]):
                        e.wait_ge(_v_ph4, w)
                        e.activation(sq4[:, :], x1t[:, :], ACT.Square).then_inc(_s_ph4, 1)
                    ph.on("scalar", x1sqr)
                    s_x1sq = s_ph4.inc(1)

                    xva = v_ph4.inc(1)
                    xvb = v_ph4.inc(1)

                    def x1var(e, j=j, w=s_x1sq, xva=xva, xvb=xvb):
                        e.wait_ge(_s_ph4, w)
                        e.reduce_sum(ssq4[:, :], sq4[:, :], axis=AX.X).then_inc(_v_ph4, 1)
                        e.wait_ge(_v_ph4, xva)
                        e.tensor_scalar(mv4[:, 0:1], sm4[:, :], 1.0 / D, None, ALU.mult)
                        e.tensor_scalar(mv4[:, 1:2], ssq4[:, :], 1.0 / D, None, ALU.mult)
                        e.tensor_scalar(sm4[:, :], sm4[:, :], -1.0 / D, None,
                                        ALU.mult).then_inc(_v_ph4, 1)
                        e.wait_ge(_v_ph4, xvb)
                        e.scalar_tensor_tensor(mv4[:, 1:2], mv4[:, 0:1], sm4[:, :], mv4[:, 1:2],
                                               ALU.mult, ALU.add).then_inc(_v_ph4, 1)
                    ph.on("vector", x1var)
                    v_x1v = v_ph4.inc(1)

                    def ln2s(e, j=j, w=v_x1v):
                        e.wait_ge(_v_ph4, w)
                        e.activation(sd4[:, :], mv4[:, 1:2], ACT.Sqrt, bias=eps_col[:, :]).then_inc(_s_ph4, 1)
                    ph.on("scalar", ln2s)
                    s_ln[j] = s_ph4.inc(1)

                    lra = v_ph4.inc(1)

                    def ln2v(e, j=j, w=s_ln[j], lra=lra):
                        e.wait_ge(_s_ph4, w)
                        e.reciprocal(rstd4[:, :], sd4[:, :]).then_inc(_v_ph4, 1)
                        if j >= 1:
                            e.wait_ge(_t_mm4, trp_t[j - 1][3])
                        e.wait_ge(_v_ph4, lra)
                        e.tensor_scalar(xn2tm[:, :], x1t[:, :], mv4[:, 0:1], rstd4[:, :],
                                        ALU.subtract, ALU.mult).then_inc(_v_ph4, 1)
                    ph.on("vector", ln2v)
                    v_xn2[j] = v_ph4.inc(1)

                def emit_mid(j):
                    for gq in range(4):
                        def trq(e, j=j, gq=gq, wv=v_xn2[j]):
                            e.wait_ge(_v_ph4, wv)
                            if gq >= 1:
                                e.wait_ge(_s_ph4, trp_s[j][gq - 1])
                            elif j >= 1:
                                e.wait_ge(_s_ph4, s_hT[j - 1])
                            if gq == 0 and j >= 1:
                                e.wait_ge(_t_mm4, t_h[j - 1])
                            ins = None
                            for u in range(4):
                                c = 4 * gq + u
                                ins = e.transpose(trx[:, 128 * u:128 * (u + 1)],
                                                  xn2tm[:, 128 * c:128 * (c + 1)], c_ident[:, :])
                            ins.then_inc(_t_mm4, 1)
                        ph.on("tensor", trq)
                        trp_t[j][gq] = t_mm4.inc(1)

                        def trqe(e, j=j, gq=gq, wt=trp_t[j][gq]):
                            e.wait_ge(_t_mm4, wt)
                            e.copy(out=xn2fm[:, 512 * gq:512 * (gq + 1)], in_=trx[:, :]).then_inc(_s_ph4, 1)
                        ph.on("scalar", trqe)
                        trp_s[j][gq] = s_ph4.inc(1)

                    def lgmm(e, j=j, w=trp_s[j][3],
                             wf=(ffn_w_ready if j == 0 else None)):
                        e.wait_ge(_s_ph4, w)
                        _w(e, wf)
                        if j >= 1:
                            e.wait_ge(_v_ph4, v_h[j - 1])
                        ins = None
                        for c in range(16):
                            ins = e.matmul(aux[:, 0:E], xn2fm[:, 128 * c:128 * (c + 1)],
                                           wg_bf[:, E * c:E * (c + 1)],
                                           start=(c == 0), stop=(c == 15))
                        ins.then_inc(_t_mm4, 1)
                    ph.on("tensor", lgmm)
                    t_lg[j] = t_mm4.inc(1)

                    la = v_ph4.inc(1)
                    lb = v_ph4.inc(1)

                    def lgev(e, j=j, w=t_lg[j], la=la, lb=lb):
                        e.wait_ge(_t_mm4, w)
                        if j >= 1:
                            e.wait_ge(_s_ph4, s_g8[j - 1])
                        e.tensor_copy(out=lg[:, :], in_=aux[:, 0:E]).then_inc(_v_ph4, 1)
                        e.wait_ge(_v_ph4, la)
                        e.tensor_reduce(nmx8[:, :], lg[:, :], axis=AX.X, op=ALU.max,
                                        negate=True).then_inc(_v_ph4, 1)
                        e.wait_ge(_v_ph4, lb)
                        e.tensor_scalar(dlg[:, :], lg[:, :], nmx8[:, :], None,
                                        ALU.add).then_inc(_v_ph4, 1)
                    ph.on("vector", lgev)
                    v_lg[j] = v_ph4.inc(1)

                    def gate_s(e, j=j, w=v_lg[j]):
                        e.wait_ge(_v_ph4, w)
                        e.activation(p8[:, :], lg[:, :], ACT.Exp, bias=nmx8[:, :]).then_inc(_s_ph4, 1)
                    ph.on("scalar", gate_s)
                    s_g8[j] = s_ph4.inc(1)

                    ga = v_ph4.inc(1)
                    gb = v_ph4.inc(1)

                    def gate_v(e, j=j, w=s_g8[j], ga=ga, gb=gb):
                        e.wait_ge(_s_ph4, w)
                        e.reduce_sum(se8[:, :], p8[:, :], axis=AX.X).then_inc(_v_ph4, 1)
                        e.wait_ge(_v_ph4, ga)
                        e.reciprocal(se8[:, :], se8[:, :]).then_inc(_v_ph4, 1)
                        e.wait_ge(_v_ph4, gb)
                        e.tensor_scalar(fscale[:, :], se8[:, :], FFN_SCALE, None, ALU.mult)
                        e.tensor_scalar(mask8[:, :], dlg[:, :], 0.0, None,
                                        ALU.is_equal).then_inc(_v_ph4, 1)
                    ph.on("vector", gate_v)
                    v_g8[j] = v_ph4.inc(1)

                    def hmm(e, j=j, w=trp_s[j][3], wl=v_lg[j]):
                        e.wait_ge(_s_ph4, w)
                        e.wait_ge(_v_ph4, wl)
                        ins = None
                        for c in range(16):
                            ins = e.matmul(aux[:, 0:ER], xn2fm[:, 128 * c:128 * (c + 1)],
                                           at_bf[:, ER * c:ER * (c + 1)],
                                           start=(c == 0), stop=(c == 15))
                        ins.then_inc(_t_mm4, 1)
                    ph.on("tensor", hmm)
                    t_h[j] = t_mm4.inc(1)

                    def hev(e, j=j, wt=t_h[j], wg=v_g8[j]):
                        e.wait_ge(_t_mm4, wt)
                        e.wait_ge(_v_ph4, wg)
                        if j >= 1:
                            e.wait_ge(_t_mm4, t_hT[j - 1])
                        m3 = mask8[:, :].unsqueeze(2).to_broadcast([128, E, R])
                        e.tensor_tensor(hbf[:, :].rearrange("p (e r) -> p e r", r=R),
                                        aux[:, 0:ER].rearrange("p (e r) -> p e r", r=R),
                                        m3, ALU.mult).then_inc(_v_ph4, 1)
                    ph.on("vector", hev)
                    v_h[j] = v_ph4.inc(1)

                    def hTt(e, j=j, w=v_h[j]):
                        e.wait_ge(_v_ph4, w)
                        if j >= 1:
                            e.wait_ge(_s_ph4, s_hT[j - 1])
                        ins = None
                        for u in range(4):
                            ins = e.transpose(trx[:, 128 * u:128 * (u + 1)],
                                              hbf[:, 128 * u:128 * (u + 1)], c_ident[:, :])
                        ins.then_inc(_t_mm4, 1)
                    ph.on("tensor", hTt)
                    t_hT[j] = t_mm4.inc(1)

                    def hTe(e, j=j, w=t_hT[j],
                            wy=(None if j == 0 else (_t_mm4, t_y[j - 1][1]))):
                        e.wait_ge(_t_mm4, w)
                        _w(e, wy)
                        e.copy(out=hT[:, :], in_=trx[:, :]).then_inc(_s_ph4, 1)
                    ph.on("scalar", hTe)
                    s_hT[j] = s_ph4.inc(1)

                    for yh in range(2):
                        def ymm(e, j=j, yh=yh, w=s_hT[j],
                                wev=(None if (j == 0 and yh == 0) else
                                     (_v_ph4, v_x2[j - 1][1] if yh == 0 else v_x2[j][0]))):
                            e.wait_ge(_s_ph4, w)
                            _w(e, wev)
                            ins = None
                            for r2 in range(4):
                                for s2 in range(2):
                                    ins = e.matmul(yps[:, 512 * s2:512 * (s2 + 1)],
                                                   hT[:, 128 * r2:128 * (r2 + 1)],
                                                   bw_bf[:, D * r2 + 1024 * yh + 512 * s2:
                                                         D * r2 + 1024 * yh + 512 * (s2 + 1)],
                                                   start=(r2 == 0), stop=(r2 == 3))
                            ins.then_inc(_t_mm4, 1)
                        ph.on("tensor", ymm)
                        t_y[j][yh] = t_mm4.inc(1)

                        def x2ev(e, j=j, yh=yh, w=t_y[j][yh],
                                 ws=(None if j == 0 else d_st[j - 1])):
                            e.wait_ge(_t_mm4, w)
                            if yh == 0:
                                _w(e, ws)
                            e.scalar_tensor_tensor(x2ts[0][:, 1024 * yh:1024 * (yh + 1)],
                                                   yps[:, :], fscale[:, :],
                                                   x1t[:, 1024 * yh:1024 * (yh + 1)],
                                                   ALU.mult, ALU.add).then_inc(_v_ph4, 1)
                        ph.on("vector", x2ev)
                        v_x2[j][yh] = v_ph4.inc(1)

                    def st(e, j=j, w=v_x2[j][1]):
                        e.wait_ge(_v_ph4, w)
                        e.dma_start(out=out[128 * j:128 * (j + 1), :],
                                    in_=x2ts[0][:, :]).then_inc(d_outp[j % 2].h, 16)
                    ph.on("sync", st)
                    d_st[j] = (d_outp[j % 2].h, d_outp[j % 2].inc(16))
                    ph.flush()

                emit_front(0)
                emit_vecA(0)
                for j in range(1, NSLOT):
                    emit_front(j)
                    emit_mid(j - 1)
                    emit_vecA(j)
                emit_mid(NSLOT - 1)

                def fin(e, t0=d_outp[0].n, t1=d_outp[1].n, td=d_dbg.n, tx=d_dbgx.n):
                    e.wait_ge(d_outp[0].h, t0)
                    e.wait_ge(d_outp[1].h, t1)
                    if td:
                        e.wait_ge(_d_dbg, td)
                    if tx:
                        e.wait_ge(_d_dbgx, tx)
                ph.on("gpsimd", fin)
                ph.flush()

    return nc



def _pack_weights(w_qkv, w_o, A, Bw, Wg):
    w_qkv = np.asarray(w_qkv, dtype=np.float32)
    WqT = np.ascontiguousarray(w_qkv[0:D].T)
    WkT = np.ascontiguousarray(w_qkv[D:2 * D].T)
    WvT = np.ascontiguousarray(w_qkv[2 * D:3 * D].T)
    wqp = np.ascontiguousarray(WqT.reshape(D, 16, 128).transpose(1, 0, 2))
    wkp = np.ascontiguousarray(WkT.reshape(D, 16, 128).transpose(1, 0, 2))
    wot_ = np.ascontiguousarray(np.asarray(w_o, dtype=np.float32).T)
    att_ = np.ascontiguousarray(np.asarray(A, dtype=np.float32).transpose(2, 0, 1).reshape(D, ER))
    bwt_ = np.ascontiguousarray(np.asarray(Bw, dtype=np.float32).transpose(0, 2, 1).reshape(ER, D))
    wgt_ = np.ascontiguousarray(np.asarray(Wg, dtype=np.float32).T)
    return wqp, wkp, WvT, wot_, att_, bwt_, wgt_


def _masks(role_a):
    ii = np.arange(128)
    m_diag = np.where(ii[:, None] > ii[None, :], NEGINF, 0.0).astype(np.float32)
    m_pad = np.full((128, 128), NEGINF if role_a else 0.0, dtype=np.float32)
    return m_diag, np.ascontiguousarray(m_diag.T), m_pad


def _perm(role_a):
    perm = np.zeros(NB, dtype=np.int64)
    for j in range(NSLOT):
        if role_a:
            perm[2 * j] = 2 * j + 1
            perm[2 * j + 1] = 2 * j
        else:
            perm[2 * j] = 2 * j
            perm[2 * j + 1] = 2 * j + 1
    return perm


def make_in_maps(x, w_qkv, w_o, A, Bw, Wg):
    x = np.asarray(x, dtype=np.float32)
    wqp, wkp, wvt_, wot_, att_, bwt_, wgt_ = _pack_weights(w_qkv, w_o, A, Bw, Wg)
    ident = np.eye(128, dtype=BF)
    in_maps = []
    for c in range(8):
        b = c // 2
        role_a = (c % 2 == 0)
        perm = _perm(role_a)
        xp_ = np.ascontiguousarray(x[b].reshape(NB, 128, D)[perm].reshape(S, D))
        m_diag, m_diagT, m_pad = _masks(role_a)
        in_maps.append({
            "xp": xp_, "wqp": wqp, "wkp": wkp, "wvt": wvt_, "wot": wot_,
            "att": att_, "bwt": bwt_, "wgt": wgt_,
            "mdiag": m_diag, "mpad": m_pad,
            "ident": ident,
        })
    return in_maps


def assemble_out(results):
    out = np.empty((BATCH, S, D), dtype=np.float32)
    for c in range(8):
        b = c // 2
        role_a = (c % 2 == 0)
        oc = np.asarray(results[c]["out"]).reshape(NSLOT, 128, D)
        for j in range(NSLOT):
            a = 2 * j if role_a else 2 * j + 1
            out[b, 128 * a:128 * (a + 1), :] = oc[j]
    return out


def kernel(L0_attn_hidden_states, L0_attn_w_qkv, L0_attn_w_o, L0_ffn_A, L0_ffn_Bw, L0_ffn_Wg):
    in_maps = make_in_maps(L0_attn_hidden_states, L0_attn_w_qkv, L0_attn_w_o,
                           L0_ffn_A, L0_ffn_Bw, L0_ffn_Wg)
    nc = build_program(debug=False)
    res = run_bass_kernel_spmd(nc, in_maps, core_ids=list(range(8)))
    return assemble_out(res.results)


# version 14
# speedup vs baseline: 1.0426x; 1.0426x over previous
from contextlib import ExitStack

import numpy as np
import ml_dtypes

import concourse.bass as bass
import concourse.mybir as mybir
from concourse.bass_utils import run_bass_kernel_spmd

F32 = mybir.dt.float32
BF16 = mybir.dt.bfloat16
ALU = mybir.AluOpType
ACT = mybir.ActivationFunctionType
AX = mybir.AxisListType

D = 2048
S = 4096
BATCH = 4
NH = 16
NB = 32
NSLOT = 16
NTOK = NSLOT * 128
NC = 16
E, R, ER = 8, 64, 512
NEGINF = -1.0e30
EPS = 1e-5
ISQ = float(1.0 / np.sqrt(128.0))
MSHIFT = 60.0
FFN_SCALE = 4.0

BF = ml_dtypes.bfloat16
ENGS = ("sync", "scalar", "vector", "tensor", "gpsimd")


class Sem:
    def __init__(self, handle):
        self.h = handle
        self.n = 0

    def inc(self, k=1):
        self.n += k
        return self.n


class EngW:

    def __init__(self, e):
        object.__setattr__(self, "_e", e)
        object.__setattr__(self, "last", None)

    def __getattr__(self, n):
        a = getattr(object.__getattribute__(self, "_e"), n)
        if callable(a):
            def f(*args, **kw):
                r = a(*args, **kw)
                if hasattr(r, "then_inc"):
                    object.__setattr__(self, "last", r)
                return r
            return f
        return a


class Ph:

    def __init__(self, blk):
        self.blk = blk
        self.q = {k: [] for k in ENGS}

    def on(self, eng, fn):
        self.q[eng].append(fn)

    def flush(self):
        for eng in ENGS:
            fns = self.q[eng]
            self.q[eng] = []
            if fns:
                getattr(self.blk, eng)(
                    lambda e, fns=tuple(fns): [f(EngW(e)) for f in fns][-1])


def _w(e, t):
    if t is None:
        return
    if isinstance(t, list):
        for u in t:
            _w(e, u)
        return
    e.wait_ge(t[0], t[1])


def _bank_chunks(lo, hi, bank=512):
    out = []
    while lo < hi:
        nxt = min(hi, (lo // bank + 1) * bank)
        out.append((lo, nxt))
        lo = nxt
    return out


def build_program(debug=False):
    nc = bass.Bass("TRN2", target_bir_lowering=False, detect_race_conditions=False)

    xp = nc.dram_tensor("xp", [S, D], F32, kind="ExternalInput").ap()
    wqp = nc.dram_tensor("wqp", [16, D, 128], F32, kind="ExternalInput").ap()
    wkp = nc.dram_tensor("wkp", [16, D, 128], F32, kind="ExternalInput").ap()
    wvt = nc.dram_tensor("wvt", [D, D], F32, kind="ExternalInput").ap()
    wot = nc.dram_tensor("wot", [D, D], F32, kind="ExternalInput").ap()
    att = nc.dram_tensor("att", [D, ER], F32, kind="ExternalInput").ap()
    bwt = nc.dram_tensor("bwt", [ER, D], F32, kind="ExternalInput").ap()
    wgt = nc.dram_tensor("wgt", [D, E], F32, kind="ExternalInput").ap()
    mdiag = nc.dram_tensor("mdiag", [128, 128], F32, kind="ExternalInput").ap()
    mpad = nc.dram_tensor("mpad", [128, 128], F32, kind="ExternalInput").ap()
    ident_d = nc.dram_tensor("ident", [128, 128], BF16, kind="ExternalInput").ap()
    out = nc.dram_tensor("out", [NTOK, D], F32, kind="ExternalOutput").ap()

    scr_kind = "ExternalOutput" if debug else "Internal"
    k_scr = nc.dram_tensor("k_scr", [D, S], BF16, kind=scr_kind).ap()
    v_scr = nc.dram_tensor("v_scr", [S, D], BF16, kind=scr_kind).ap()
    q_scr = nc.dram_tensor("q_scr", [D, NTOK], BF16, kind=scr_kind).ap()
    if debug:
        xn_dbg = nc.dram_tensor("xn_dbg", [128, NC * S], BF16, kind="ExternalOutput").ap()
        attn_dbg = nc.dram_tensor("attn_dbg", [128, NH * NTOK], BF16, kind="ExternalOutput").ap()
        rs_dbg = nc.dram_tensor("rs_dbg", [1, NH * NTOK], F32, kind="ExternalOutput").ap()
        negm_dbg = nc.dram_tensor("negm_dbg", [NH, D], BF16, kind="ExternalOutput").ap()

    with ExitStack() as _ctx:
        def EC(cm):
            return _ctx.enter_context(cm)

        _d_cst = EC(nc.semaphore("d_cst")); _d_x1 = EC(nc.semaphore("d_x1"))
        _v_ph1 = EC(nc.semaphore("v_ph1")); _s_sq1 = EC(nc.semaphore("s_sq1"))
        _t_tr1 = EC(nc.semaphore("t_tr1")); _s_ev1 = EC(nc.semaphore("s_ev1"))
        _d_w2 = EC(nc.semaphore("d_w2")); _t_mm2 = EC(nc.semaphore("t_mm2"))
        _s_ev2 = EC(nc.semaphore("s_ev2")); _t_nm2 = EC(nc.semaphore("t_nm2"))
        _v_nm2 = EC(nc.semaphore("v_nm2")); _d_sp2 = EC(nc.semaphore("d_sp2"))
        _s_bnd = EC(nc.semaphore("s_bnd")); _t_bnd = EC(nc.semaphore("t_bnd"))
        _v_bnd = EC(nc.semaphore("v_bnd")); _d_kvq = EC(nc.semaphore("d_kvq"))
        _t_sc = EC(nc.semaphore("t_sc")); _v_mk = EC(nc.semaphore("v_mk"))
        _s_xp = EC(nc.semaphore("s_xp")); _t_pv = EC(nc.semaphore("t_pv"))
        _v_ev3 = EC(nc.semaphore("v_ev3")); _s_ev3 = EC(nc.semaphore("s_ev3"))
        _t_bc3 = EC(nc.semaphore("t_bc3")); _d_w4 = EC(nc.semaphore("d_w4"))
        _d_x4 = EC(nc.semaphore("d_x4")); _t_mm4 = EC(nc.semaphore("t_mm4"))
        _v_ph4 = EC(nc.semaphore("v_ph4")); _s_ph4 = EC(nc.semaphore("s_ph4"))
        _d_out = EC(nc.semaphore("d_out"))
        _d_x1b = EC(nc.semaphore("d_x1b")); _d_w2b = EC(nc.semaphore("d_w2b"))
        _d_sp2b = EC(nc.semaphore("d_sp2b")); _d_kvqb = EC(nc.semaphore("d_kvqb"))
        _d_w4b = EC(nc.semaphore("d_w4b")); _d_outb = EC(nc.semaphore("d_outb"))
        _d_dbgx = EC(nc.semaphore("d_dbgx")); _d_dbg = EC(nc.semaphore("d_dbg"))
        _t_pa = EC(nc.semaphore("t_pa")); _v_pa = EC(nc.semaphore("v_pa"))
        c_mdiag = EC(nc.sbuf_tensor("c_mdiag", [128, 128], F32))
        c_mpad = EC(nc.sbuf_tensor("c_mpad", [128, 128], F32))
        ones128 = EC(nc.sbuf_tensor("ones128", [128, 128], BF16))
        c_ident = EC(nc.sbuf_tensor("c_ident", [128, 128], BF16))
        ones_col = EC(nc.sbuf_tensor("ones_col", [128, 1], BF16))
        ones_row = EC(nc.sbuf_tensor("ones_row", [1, 128], BF16))
        eps_col = EC(nc.sbuf_tensor("eps_col", [128, 1], F32))
        blk = EC(nc.Block())
        d_cst = Sem(_d_cst); d_x1 = Sem(_d_x1); v_ph1 = Sem(_v_ph1)
        s_sq1 = Sem(_s_sq1); t_tr1 = Sem(_t_tr1); s_ev1 = Sem(_s_ev1)
        d_w2 = Sem(_d_w2); t_mm2 = Sem(_t_mm2)
        s_ev2 = Sem(_s_ev2); t_nm2 = Sem(_t_nm2); v_nm2 = Sem(_v_nm2)
        d_sp2 = Sem(_d_sp2); s_bnd = Sem(_s_bnd); t_bnd = Sem(_t_bnd)
        v_bnd = Sem(_v_bnd); d_kvq = Sem(_d_kvq); t_sc = Sem(_t_sc)
        v_mk = Sem(_v_mk); s_xp = Sem(_s_xp); t_pv = Sem(_t_pv)
        v_ev3 = Sem(_v_ev3); s_ev3 = Sem(_s_ev3); t_bc3 = Sem(_t_bc3)
        d_w4 = Sem(_d_w4); d_x4 = Sem(_d_x4)
        t_mm4 = Sem(_t_mm4); v_ph4 = Sem(_v_ph4); s_ph4 = Sem(_s_ph4)
        d_out = Sem(_d_out)
        d_x1p = [Sem(_d_x1), Sem(_d_x1b)]
        d_w2p = [Sem(_d_w2), Sem(_d_w2b)]
        d_sp2p = [Sem(_d_sp2), Sem(_d_sp2b)]
        d_kvqp = [Sem(_d_kvq), Sem(_d_kvqb)]
        d_w4b = Sem(_d_w4b)
        t_pa = Sem(_t_pa); v_pa = Sem(_v_pa)
        d_outp = [Sem(_d_out), Sem(_d_outb)]
        d_dbgx = Sem(_d_dbgx); d_dbg = Sem(_d_dbg)

        ph = Ph(blk)

        def p0_dma(e):
            e.dma_start(out=c_mdiag[:, :], in_=mdiag[:, :]).then_inc(_d_cst, 16)
            e.dma_start(out=c_mpad[:, :], in_=mpad[:, :]).then_inc(_d_cst, 16)
            e.dma_start(out=c_ident[:, :], in_=ident_d[:, :]).then_inc(_d_cst, 16)
        ph.on("sync", p0_dma)
        cst_ld = d_cst.inc(48)

        def p0_v(e):
            e.memset(ones_col[:, :], 1.0)
            e.memset(ones128[:, :], 1.0)
            e.memset(eps_col[:, :], EPS)
            e.memset(ones_row[:, :], 1.0).then_inc(_v_ph1, 1)
        ph.on("vector", p0_v)
        ones_ready = (_v_ph1, v_ph1.inc(1))
        ident_ready = (_d_cst, cst_ld)
        ph.flush()

        with nc.sbuf_tensor("xn", [128, NC * S], BF16) as xn:
            xn3 = xn[:, :].rearrange("p (c t) -> p c t", t=S)

            with ExitStack() as _c1:
                xt0 = _c1.enter_context(nc.sbuf_tensor("xt0", [128, D], F32))
                xt1 = _c1.enter_context(nc.sbuf_tensor("xt1", [128, D], F32))
                xn_tm0 = _c1.enter_context(nc.sbuf_tensor("xn_tm0", [128, D], BF16))
                xn_tm1 = _c1.enter_context(nc.sbuf_tensor("xn_tm1", [128, D], BF16))
                sq0 = _c1.enter_context(nc.sbuf_tensor("sq0", [128, D], F32))
                sq1 = _c1.enter_context(nc.sbuf_tensor("sq1", [128, D], F32))
                sm1 = _c1.enter_context(nc.sbuf_tensor("sm1", [128, 1], F32))
                ssq1 = _c1.enter_context(nc.sbuf_tensor("ssq1", [128, 1], F32))
                mv = _c1.enter_context(nc.sbuf_tensor("mv", [128, 2], F32))
                sd1 = _c1.enter_context(nc.sbuf_tensor("sd1", [128, 1], F32))
                rstd1 = _c1.enter_context(nc.sbuf_tensor("rstd1", [128, 1], F32))
                tr_ps0 = _c1.enter_context(nc.psum_tensor("tr_ps0", [128, 512], BF16))
                tr_ps1 = _c1.enter_context(nc.psum_tensor("tr_ps1", [128, 512], BF16))
                xts = [xt0, xt1]
                xn_tms = [xn_tm0, xn_tm1]
                sqs = [sq0, sq1]
                tr_pss = [tr_ps0, tr_ps1]

                v_apply = [0] * NB
                v_aggr = [0] * NB
                s_sq = [0] * NB
                s_sqr = [0] * NB
                t_trg = []
                s_evg = []

                for i in range(NB):
                    pp = i % 2

                    def p1_dma(e, i=i, pp=pp, w=(None if i < 2 else (_v_ph1, v_apply[i - 2]))):
                        _w(e, w)
                        e.dma_start(out=xts[pp][:, :],
                                    in_=xp[128 * i:128 * (i + 1), :]).then_inc(d_x1p[pp].h, 16)
                    ph.on("sync", p1_dma)
                    x_ld = (d_x1p[pp].h, d_x1p[pp].inc(16))

                    def p1_sqr(e, i=i, pp=pp, x_ld=x_ld,
                               w=(None if i < 2 else (_v_ph1, v_aggr[i - 2]))):
                        _w(e, x_ld)
                        _w(e, w)
                        e.activation(sqs[pp][:, :], xts[pp][:, :], ACT.Square).then_inc(_s_sq1, 1)
                    ph.on("scalar", p1_sqr)
                    s_sqr[i] = s_sq1.inc(1)

                    va = v_ph1.inc(1)
                    vb = v_ph1.inc(1)

                    def p1_stats(e, i=i, pp=pp, x_ld=x_ld, wsq=s_sqr[i], va=va, vb=vb,
                                 w=(None if i < 1 else (_s_sq1, s_sq[i - 1]))):
                        _w(e, x_ld)
                        _w(e, w)
                        e.reduce_sum(sm1[:, :], xts[pp][:, :], axis=AX.X)
                        e.wait_ge(_s_sq1, wsq)
                        e.reduce_sum(ssq1[:, :], sqs[pp][:, :], axis=AX.X).then_inc(_v_ph1, 1)
                        e.wait_ge(_v_ph1, va)
                        e.tensor_scalar(mv[:, 0:1], sm1[:, :], 1.0 / D, None, ALU.mult)
                        e.tensor_scalar(mv[:, 1:2], ssq1[:, :], 1.0 / D, None, ALU.mult)
                        e.tensor_scalar(sm1[:, :], sm1[:, :], -1.0 / D, None,
                                        ALU.mult).then_inc(_v_ph1, 1)
                        e.wait_ge(_v_ph1, vb)
                        e.scalar_tensor_tensor(mv[:, 1:2], mv[:, 0:1], sm1[:, :], mv[:, 1:2],
                                               ALU.mult, ALU.add).then_inc(_v_ph1, 1)
                    ph.on("vector", p1_stats)
                    v_aggr[i] = v_ph1.inc(1)
                    aggr = v_aggr[i]

                    def p1_sqrt(e, aggr=aggr, w1=(ones_ready if i == 0 else None)):
                        _w(e, w1)
                        e.wait_ge(_v_ph1, aggr)
                        e.activation(sd1[:, :], mv[:, 1:2], ACT.Sqrt, bias=eps_col[:, :]).then_inc(_s_sq1, 1)
                    ph.on("scalar", p1_sqrt)
                    s_sq[i] = s_sq1.inc(1)

                    vr = v_ph1.inc(1)

                    def p1_apply(e, i=i, pp=pp, sq=s_sq[i], vr=vr,
                                 w=(None if i < 2 else (_t_tr1, None))):
                        e.wait_ge(_s_sq1, sq)
                        e.reciprocal(rstd1[:, :], sd1[:, :]).then_inc(_v_ph1, 1)
                        if i >= 2:
                            e.wait_ge(_t_tr1, t_trg[4 * (i - 2) + 3])
                        e.wait_ge(_v_ph1, vr)
                        e.tensor_scalar(xn_tms[pp][:, :], xts[pp][:, :], mv[:, 0:1], rstd1[:, :],
                                        ALU.subtract, ALU.mult).then_inc(_v_ph1, 1)
                    ph.on("vector", p1_apply)
                    v_apply[i] = v_ph1.inc(1)

                    for g in range(4):
                        gi = 4 * i + g
                        tpp = gi % 2

                        def p1_tr(e, i=i, pp=pp, g=g, gi=gi, tpp=tpp,
                                  wa=v_apply[i],
                                  we=(None if gi < 2 else (_s_ev1, None)),
                                  wid=(ident_ready if gi == 0 else None)):
                            _w(e, wid)
                            e.wait_ge(_v_ph1, wa)
                            if gi >= 2:
                                e.wait_ge(_s_ev1, s_evg[gi - 2])
                            for u in range(4):
                                c = 4 * g + u
                                e.transpose(tr_pss[tpp][:, 128 * u:128 * (u + 1)],
                                            xn_tms[pp][:, 128 * c:128 * (c + 1)],
                                            c_ident[:, :])
                            e.last.then_inc(_t_tr1, 1)
                        ph.on("tensor", p1_tr)
                        t_trg.append(t_tr1.inc(1))

                        def p1_ev(e, i=i, g=g, gi=gi, tpp=tpp, wt=t_trg[gi]):
                            e.wait_ge(_t_tr1, wt)
                            dst = xn3[:, 4 * g:4 * (g + 1), 128 * i:128 * (i + 1)]
                            src = tr_pss[tpp][:, :].rearrange("p (u t) -> p u t", t=128)
                            e.copy(out=dst, in_=src).then_inc(_s_ev1, 1)
                        ph.on("scalar", p1_ev)
                        s_evg.append(s_ev1.inc(1))
                    if i % 4 == 3:
                        ph.flush()
                ph.flush()
                xn_ready = (_s_ev1, s_ev1.n)

                def xn_gate(sg):
                    last_tile = 16 * sg + 15
                    return (_s_ev1, s_evg[4 * last_tile + 3])

            wvb0_cm = nc.sbuf_tensor("wvb0", [128, NC * 1024], BF16)
            wvb0 = wvb0_cm.__enter__()
            wv3 = wvt[:, :].rearrange("(c p) v -> p c v", p=128)

            def wvb0_dma(e, w=xn_ready):
                _w(e, w)
                e.dma_start(out=wvb0[:, :].rearrange("p (c u) -> p c u", u=1024),
                            in_=wv3[:, :, 0:1024]).then_inc(d_w2p[0].h, 16)
            ph.on("gpsimd", wvb0_dma)
            wvb0_ld = (d_w2p[0].h, d_w2p[0].inc(16))

            with ExitStack() as _c2:
                wb0 = _c2.enter_context(nc.sbuf_tensor("wb0", [128, D], BF16))
                wb1 = _c2.enter_context(nc.sbuf_tensor("wb1", [128, D], BF16))
                kst0 = _c2.enter_context(nc.sbuf_tensor("kst0", [128, 2048], BF16))
                kst1 = _c2.enter_context(nc.sbuf_tensor("kst1", [128, 2048], BF16))
                kps0 = _c2.enter_context(nc.psum_tensor("kps0", [128, 2048], F32))
                kps1 = _c2.enter_context(nc.psum_tensor("kps1", [128, 2048], F32))
                wbs = [wb0, wb1]
                ksts = [kst0, kst1]
                kpss = [kps0, kps1]

                mm_f = []
                ev_f = []
                sp_f = []
                ld_f = []
                lastmm_of_o = []

                def emit_proj(wpack, scr, ntok, rhs_for, scale, gate):
                    nsg = ntok // 2048
                    base = len(ld_f)
                    for o in range(16):
                        pp = o % 2
                        oo = base + o

                        def w_dma(e, o=o, pp=pp, oo=oo,
                                  wxn=(xn_ready if oo < 2 else None)):
                            _w(e, wxn)
                            if oo >= 2:
                                e.wait_ge(_t_mm2, lastmm_of_o[oo - 2])
                            e.dma_start(
                                out=wbs[pp][:, :].rearrange("p (c u) -> p c u", u=128),
                                in_=wpack[o].rearrange("(c p) u -> p c u", p=128),
                            ).then_inc(d_w2p[pp].h, 16)
                        ph.on("gpsimd", w_dma)
                        ld_f.append((d_w2p[pp].h, d_w2p[pp].inc(16)))

                        for sg in range(nsg):
                            st = len(mm_f)
                            spp = st % 2

                            def p2_mm(e, o=o, pp=pp, sg=sg, st=st, spp=spp, oo=oo,
                                      rhs_for=rhs_for,
                                      wxn=(gate(sg) if oo in (0, 1) else None)):
                                _w(e, wxn)
                                _w(e, ld_f[oo])
                                if st >= 2:
                                    e.wait_ge(_s_ev2, ev_f[st - 2])
                                for c in range(16):
                                    for s4 in range(4):
                                        e.matmul(kpss[spp][:, 512 * s4:512 * (s4 + 1)],
                                                 wbs[pp][:, 128 * c:128 * (c + 1)],
                                                 rhs_for(c, sg * 2048 + 512 * s4, 512),
                                                 start=(c == 0), stop=(c == 15))
                                e.last.then_inc(_t_mm2, 1)
                            ph.on("tensor", p2_mm)
                            mm_f.append(t_mm2.inc(1))

                            def p2_ev(e, st=st, spp=spp, scale=scale):
                                e.wait_ge(_t_mm2, mm_f[st])
                                if st >= 2:
                                    _w(e, sp_f[st - 2])
                                e.activation(ksts[spp][:, :], kpss[spp][:, :], ACT.Copy,
                                             scale=scale).then_inc(_s_ev2, 1)
                            ph.on("scalar", p2_ev)
                            ev_f.append(s_ev2.inc(1))

                            def p2_sp(e, o=o, sg=sg, st=st, spp=spp, scr=scr):
                                e.wait_ge(_s_ev2, ev_f[st])
                                e.dma_start(out=scr[128 * o:128 * (o + 1), 2048 * sg:2048 * (sg + 1)],
                                            in_=ksts[spp][:, :]).then_inc(d_sp2p[spp].h, 16)
                            ph.on("sync", p2_sp)
                            sp_f.append((d_sp2p[spp].h, d_sp2p[spp].inc(16)))

                        lastmm_of_o.append(mm_f[-1])
                        if o % 4 == 3:
                            ph.flush()
                    ph.flush()

                def k_rhs(c, lo, width):
                    return xn[:, c * S + lo: c * S + lo + width]

                xq = xn3.rearrange("p c (j u) -> p c j u", u=128)

                def q_rhs(c, lo, width):
                    j0 = lo // 128
                    nj = width // 128
                    return xq[:, c, 1 + 2 * j0: 2 * (j0 + nj):2, :]

                emit_proj(wkp, k_scr, S, k_rhs, 1.0, xn_gate)
                emit_proj(wqp, q_scr, NTOK, q_rhs, ISQ, lambda sg: xn_ready)
                kq_spills = [(d_sp2p[0].h, d_sp2p[0].n), (d_sp2p[1].h, d_sp2p[1].n)]
                kq_mm_last = (_t_mm2, t_mm2.n)
                kq_ev_last = (_s_ev2, s_ev2.n)


            with ExitStack() as _c2v:
                wvb1 = _c2v.enter_context(nc.sbuf_tensor("wvb1", [128, NC * 1024], BF16))
                vst0 = _c2v.enter_context(nc.sbuf_tensor("vst0", [128, 1024], BF16))
                vst1 = _c2v.enter_context(nc.sbuf_tensor("vst1", [128, 1024], BF16))
                vps0 = _c2v.enter_context(nc.psum_tensor("vps0", [128, 1024], F32))
                vps1 = _c2v.enter_context(nc.psum_tensor("vps1", [128, 1024], F32))
                wvbs = [wvb0, wvb1]
                vsts = [vst0, vst1]
                vpss = [vps0, vps1]

                vld = [0] * 2
                vld[0] = wvb0_ld
                vmm_f = []
                vev_f = []
                vsp_f = []
                def v1_dma(e, wsp=tuple(kq_spills), wnm=kq_mm_last):
                    for t in wsp:
                        _w(e, t)
                    _w(e, wnm)
                    e.dma_start(out=wvb1[:, :].rearrange("p (c u) -> p c u", u=1024),
                                in_=wv3[:, :, 1024:2048]).then_inc(d_w2p[1].h, 16)
                ph.on("gpsimd", v1_dma)
                vld[1] = (d_w2p[1].h, d_w2p[1].inc(16))

                for h2 in range(2):
                    hpp = h2 % 2

                    for i in range(NB):
                        st = len(vmm_f)
                        pp = st % 2

                        def v_mm(e, h2=h2, hpp=hpp, i=i, st=st, pp=pp,
                                 wb=(kq_ev_last if st == 0 else None)):
                            _w(e, wb)
                            _w(e, vld[h2])
                            if st >= 2:
                                e.wait_ge(_s_ev2, vev_f[st - 2])
                            for c in range(16):
                                for s2 in range(2):
                                    e.matmul(vpss[pp][:, 512 * s2:512 * (s2 + 1)],
                                             xn[:, c * S + 128 * i: c * S + 128 * (i + 1)],
                                             wvbs[hpp][:, 1024 * c + 512 * s2:1024 * c + 512 * (s2 + 1)],
                                             start=(c == 0), stop=(c == 15))
                            e.last.then_inc(_t_mm2, 1)
                        ph.on("tensor", v_mm)
                        vmm_f.append(t_mm2.inc(1))

                        def v_ev(e, st=st, pp=pp):
                            e.wait_ge(_t_mm2, vmm_f[st])
                            if st >= 2:
                                _w(e, vsp_f[st - 2])
                            e.activation(vsts[pp][:, :], vpss[pp][:, :], ACT.Copy).then_inc(_s_ev2, 1)
                        ph.on("scalar", v_ev)
                        vev_f.append(s_ev2.inc(1))

                        def v_sp(e, h2=h2, i=i, st=st, pp=pp):
                            e.wait_ge(_s_ev2, vev_f[st])
                            e.dma_start(out=v_scr[128 * i:128 * (i + 1), 1024 * h2:1024 * (h2 + 1)],
                                        in_=vsts[pp][:, :]).then_inc(d_sp2p[pp].h, 16)
                        ph.on("sync", v_sp)
                        vsp_f.append((d_sp2p[pp].h, d_sp2p[pp].inc(16)))
                        if i % 8 == 7:
                            ph.flush()
                    ph.flush()
                all_spills = [(d_sp2p[0].h, d_sp2p[0].n), (d_sp2p[1].h, d_sp2p[1].n)]

            dbg_xn_done = None
            if debug:
                def dbg_xn(e, w=tuple(all_spills)):
                    for t in w:
                        _w(e, t)
                    e.dma_start(out=xn_dbg[:, :], in_=xn[:, :]).then_inc(_d_dbgx, 16)
                ph.on("gpsimd", dbg_xn)
                dbg_xn_done = (_d_dbgx, d_dbgx.inc(16))
                ph.flush()

            wvb0_cm.__exit__(None, None, None)

        with ExitStack() as _c3o:
            attn_sb = _c3o.enter_context(nc.sbuf_tensor("attn_sb", [128, NH * NTOK], BF16))
            wo_bf = _c3o.enter_context(nc.sbuf_tensor("wo_bf", [128, NC * D], BF16))

            with ExitStack() as _c3:
                kb0 = _c3.enter_context(nc.sbuf_tensor("kb0", [128, S], BF16))
                kb1 = _c3.enter_context(nc.sbuf_tensor("kb1", [128, S], BF16))
                vb0 = _c3.enter_context(nc.sbuf_tensor("vb0", [128, S], BF16))
                vb1 = _c3.enter_context(nc.sbuf_tensor("vb1", [128, S], BF16))
                qb0 = _c3.enter_context(nc.sbuf_tensor("qb0", [128, NTOK], BF16))
                qb1 = _c3.enter_context(nc.sbuf_tensor("qb1", [128, NTOK], BF16))
                pT0 = _c3.enter_context(nc.sbuf_tensor("pT0", [128, 1024], BF16))
                pT1 = _c3.enter_context(nc.sbuf_tensor("pT1", [128, 1024], BF16))
                pT2 = _c3.enter_context(nc.sbuf_tensor("pT2", [128, 1024], BF16))
                bc_sb = _c3.enter_context(nc.sbuf_tensor("bc_sb", [128, 1024], F32))
                negm0 = _c3.enter_context(nc.sbuf_tensor("negm0", [128, D], BF16))
                negm1 = _c3.enter_context(nc.sbuf_tensor("negm1", [128, D], BF16))
                skq0 = _c3.enter_context(nc.sbuf_tensor("skq0", [128, 1024], BF16))
                skq1 = _c3.enter_context(nc.sbuf_tensor("skq1", [128, 1024], BF16))
                kslot = _c3.enter_context(nc.sbuf_tensor("kslot", [128, NSLOT], F32))
                ksc1 = _c3.enter_context(nc.sbuf_tensor("ksc1", [128, NSLOT], F32))
                ksc2 = _c3.enter_context(nc.sbuf_tensor("ksc2", [128, NSLOT], F32))
                nrm = _c3.enter_context(nc.sbuf_tensor("nrm", [128, 1024], F32))
                sqt = _c3.enter_context(nc.sbuf_tensor("sqt", [128, 1024], F32))
                attn_ps = _c3.enter_context(nc.psum_tensor("attn_ps", [128, 1024], F32))
                sT0 = _c3.enter_context(nc.psum_tensor("sT0", [128, 1024], F32))
                sT1 = _c3.enter_context(nc.psum_tensor("sT1", [128, 1024], F32))
                rs_ps = _c3.enter_context(nc.psum_tensor("rs_ps", [128, 1024], F32))
                kbs = [kb0, kb1]; vbs = [vb0, vb1]; qbs = [qb0, qb1]
                pTs = [pT0, pT1, pT2]; sTs = [sT0, sT1]; negms = [negm0, negm1]
                skq = [skq0, skq1]

                head_ld = [0] * NH
                head_t_done = [0] * NH

                def emit_head_load(h, extra=()):
                    hp2 = h % 2

                    def ld(e, h=h, hp2=hp2, extra=tuple(extra)):
                        for t in extra:
                            _w(e, t)
                        if h >= 2:
                            e.wait_ge(_t_pv, head_t_done[h - 2])
                        e.dma_start(out=kbs[hp2][:, :],
                                    in_=k_scr[128 * h:128 * (h + 1), :]).then_inc(d_kvqp[hp2].h, 16)
                        e.dma_start(out=vbs[hp2][:, :].rearrange("p (t u) -> p t u", u=128),
                                    in_=v_scr[:, 128 * h:128 * (h + 1)].rearrange("(t p) u -> p t u", p=128),
                                    ).then_inc(d_kvqp[hp2].h, 16)
                        e.dma_start(out=qbs[hp2][:, :],
                                    in_=q_scr[128 * h:128 * (h + 1), :]).then_inc(d_kvqp[hp2].h, 16)
                    ph.on("sync", ld)
                    head_ld[h] = (d_kvqp[hp2].h, d_kvqp[hp2].inc(48))

                emit_head_load(0, extra=all_spills)

                h0 = head_ld[0]

                def w4_dma(e, w=h0, wd=dbg_xn_done):
                    _w(e, w)
                    _w(e, wd)
                    for c in range(16):
                        e.dma_start(out=wo_bf[:, D * c:D * (c + 1)],
                                    in_=wot[128 * c:128 * (c + 1), :]).then_inc(_d_w4, 16)
                ph.on("gpsimd", w4_dma)
                w4_ready = (_d_w4, d_w4.inc(16 * 16))
                ph.flush()

                last_sT_exp = [None, None]
                last_sT_red = [None, None]
                last_pT_rd = [None, None, None]
                last_attn_rd = None
                last_rs_rd = None
                last_rs_dbg = None
                negm_last_rd = [None, None]
                negm_dbg_rd = [None, None]
                negm_rdy = [None] * NH
                sT_step = [0]
                skq_rd = [None, None]
                nrm_rd = [None]

                def makeA(h):
                    hp = h % 2
                    thunks = []
                    state = {"first": True}

                    def kn2_thunk(c):
                        def emit():
                            pc = c % 2
                            pp = sT_step[0] % 2
                            sT_step[0] += 1
                            wld = head_ld[h] if state["first"] else None
                            state["first"] = False
                            wsq_rd = skq_rd[pc]
                            w1 = last_sT_exp[pp]
                            w2 = last_sT_red[pp]

                            def pa_sq(e, c=c, pc=pc, hp=hp, wld=wld, wsq_rd=wsq_rd):
                                _w(e, wld)
                                _w(e, wsq_rd)
                                e.activation(skq[pc][:, :],
                                             kbs[hp][:, 1024 * c:1024 * (c + 1)],
                                             ACT.Square).then_inc(_s_xp, 1)
                            ph.on("scalar", pa_sq)
                            sq_v = s_xp.inc(1)

                            def pa_mm(e, pc=pc, pp=pp, sq_v=sq_v, w1=w1, w2=w2,
                                      won=(ones_ready if (h == 0 and c == 0) else None)):
                                _w(e, won)
                                e.wait_ge(_s_xp, sq_v)
                                _w(e, w1)
                                _w(e, w2)
                                ins = None
                                for lo, hi in _bank_chunks(0, 1024):
                                    ins = e.matmul(sTs[pp][:, lo:hi], ones128[:, :],
                                                   skq[pc][:, lo:hi],
                                                   start=True, stop=True)
                                ins.then_inc(_t_pa, 1)
                            ph.on("tensor", pa_mm)
                            mm_v = t_pa.inc(1)
                            skq_rd[pc] = (_t_pa, mm_v)

                            def pa_red(e, c=c, pp=pp, mm_v=mm_v):
                                e.wait_ge(_t_pa, mm_v)
                                e.tensor_reduce(
                                    kslot[:, 4 * c:4 * (c + 1)],
                                    sTs[pp][:, 0:1024].rearrange("p (j u) -> p j u", u=256),
                                    axis=AX.X, op=ALU.max).then_inc(_v_pa, 1)
                            ph.on("vector", pa_red)
                            last_sT_red[pp] = (_v_pa, v_pa.inc(1))
                        return emit

                    def scan_thunk():
                        def emit():
                            vred = v_pa.n

                            def pa_scan(e, vred=vred):
                                e.wait_ge(_v_pa, vred)
                                e.tensor_tensor(ksc1[:, 1:16], kslot[:, 1:16],
                                                kslot[:, 0:15], ALU.max).then_inc(_v_pa, 1)
                                e.tensor_copy(out=ksc1[:, 0:1],
                                              in_=kslot[:, 0:1]).then_inc(_v_pa, 1)
                                e.wait_ge(_v_pa, vred + 2)
                                e.tensor_tensor(ksc2[:, 2:16], ksc1[:, 2:16],
                                                ksc1[:, 0:14], ALU.max).then_inc(_v_pa, 1)
                                e.tensor_copy(out=ksc2[:, 0:2],
                                              in_=ksc1[:, 0:2]).then_inc(_v_pa, 1)
                                e.wait_ge(_v_pa, vred + 4)
                                e.tensor_tensor(ksc1[:, 4:16], ksc2[:, 4:16],
                                                ksc2[:, 0:12], ALU.max).then_inc(_v_pa, 1)
                                e.tensor_copy(out=ksc1[:, 0:4],
                                              in_=ksc2[:, 0:4]).then_inc(_v_pa, 1)
                                e.wait_ge(_v_pa, vred + 6)
                                e.tensor_tensor(ksc2[:, 8:16], ksc1[:, 8:16],
                                                ksc1[:, 0:8], ALU.max).then_inc(_v_pa, 1)
                                e.tensor_copy(out=ksc2[:, 0:8],
                                              in_=ksc1[:, 0:8]).then_inc(_v_pa, 1)
                            ph.on("vector", pa_scan)
                            state["scan_v"] = v_pa.inc(8)
                        return emit

                    def qn2_thunk(c):
                        def emit():
                            pc = c % 2
                            pp = sT_step[0] % 2
                            sT_step[0] += 1
                            wsq_rd = skq_rd[pc]
                            w1 = last_sT_exp[pp]
                            w2 = last_sT_red[pp]

                            def pa_sqq(e, c=c, pc=pc, hp=hp, wsq_rd=wsq_rd):
                                _w(e, wsq_rd)
                                e.activation(skq[pc][:, :],
                                             qbs[hp][:, 1024 * c:1024 * (c + 1)],
                                             ACT.Square).then_inc(_s_xp, 1)
                            ph.on("scalar", pa_sqq)
                            sq_v = s_xp.inc(1)

                            def pa_mmq(e, pc=pc, pp=pp, sq_v=sq_v, w1=w1, w2=w2):
                                e.wait_ge(_s_xp, sq_v)
                                _w(e, w1)
                                _w(e, w2)
                                ins = None
                                for lo, hi in _bank_chunks(0, 1024):
                                    ins = e.matmul(sTs[pp][:, lo:hi], ones128[:, :],
                                                   skq[pc][:, lo:hi],
                                                   start=True, stop=True)
                                ins.then_inc(_t_pa, 1)
                            ph.on("tensor", pa_mmq)
                            mm_v = t_pa.inc(1)
                            skq_rd[pc] = (_t_pa, mm_v)

                            def pa_mul(e, c=c, pp=pp, mm_v=mm_v,
                                       scv=state["scan_v"], wnr=nrm_rd[0]):
                                e.wait_ge(_t_pa, mm_v)
                                e.wait_ge(_v_pa, scv)
                                _w(e, wnr)
                                e.tensor_tensor(
                                    nrm[:, :].rearrange("p (j u) -> p j u", u=128),
                                    sTs[pp][:, 0:1024].rearrange("p (j u) -> p j u", u=128),
                                    ksc2[:, 8 * c:8 * (c + 1)].unsqueeze(2).to_broadcast([128, 8, 128]),
                                    ALU.mult).then_inc(_v_pa, 1)
                            ph.on("vector", pa_mul)
                            mul_v = v_pa.inc(1)
                            last_sT_red[pp] = (_v_pa, mul_v)

                            sq2_v = s_xp.inc(1)
                            af_v = s_xp.inc(1)

                            def pa_sqrt(e, c=c, hp=hp, mul_v=mul_v, sq2_v=sq2_v,
                                        wn=(negm_last_rd[hp] if c == 0 else None),
                                        wnd=(negm_dbg_rd[hp] if c == 0 else None)):
                                e.wait_ge(_v_pa, mul_v)
                                e.activation(sqt[:, :], nrm[:, :],
                                             ACT.Sqrt).then_inc(_s_xp, 1)
                                _w(e, wn)
                                _w(e, wnd)
                                e.wait_ge(_s_xp, sq2_v)
                                e.activation(negms[hp][:, 1024 * c:1024 * (c + 1)],
                                             sqt[:, :], ACT.Copy, bias=MSHIFT,
                                             scale=-1.0).then_inc(_s_xp, 1)
                            ph.on("scalar", pa_sqrt)
                            nrm_rd[0] = (_s_xp, sq2_v)
                            if c == 1:
                                negm_rdy[h] = (_s_xp, af_v)
                        return emit

                    for c in range(4):
                        thunks.append(kn2_thunk(c))
                    thunks.append(scan_thunk())
                    thunks.append(qn2_thunk(0))
                    thunks.append(qn2_thunk(1))

                    if debug:
                        def dbg_thunk():
                            def pa_dbg(e, h=h, hp=hp, w=negm_rdy[h]):
                                _w(e, w)
                                e.dma_start(out=negm_dbg[h:h + 1, :],
                                            in_=negms[hp][0:1, :]).then_inc(_d_dbg, 16)
                            ph.on("gpsimd", pa_dbg)
                            negm_dbg_rd[hp] = (_d_dbg, d_dbg.inc(16))
                        thunks.append(dbg_thunk)
                    return thunks

                for t in makeA(0):
                    t()
                ph.flush()

                for h in range(NH):
                    hp = h % 2
                    if h + 1 < NH:
                        emit_head_load(h + 1)
                        thunksA = makeA(h + 1)
                    else:
                        thunksA = []
                    ai = 0

                    for g in range(2):
                        kmax = 16 * g + 16
                        sc_d = [0] * kmax
                        mk_d = [0] * kmax
                        xp_d = [0] * kmax
                        pv_d = [0] * kmax
                        kap_pp = [0] * kmax
                        kap_qp = [0] * kmax

                        def geom(kap, g=g):
                            jmin = max(8 * g, kap // 2)
                            astart = (jmin - 8 * g) * 128
                            return astart, 1024 - astart

                        def emit_pv(kap, g=g, h=h, hp=hp, kmax=kmax, pv_d=pv_d,
                                    xp_d=xp_d, kap_pp=kap_pp, kap_qp=kap_qp):
                            astart, width = geom(kap)
                            pp = kap_qp[kap]
                            wattn = last_attn_rd if kap == 0 else None
                            wrs = last_rs_rd if kap == 0 else None

                            def p3_pv(e, kap=kap, pp=pp, hp=hp, astart=astart,
                                      g=g, kmax=kmax, wattn=wattn, wrs=wrs,
                                      xpv=xp_d[kap]):
                                e.wait_ge(_s_xp, xpv)
                                _w(e, wattn)
                                _w(e, wrs)
                                for lo, hi in _bank_chunks(astart, 1024):
                                    stp = (kap == 16 * g + 8 * (lo // 512) + 7)
                                    e.matmul(rs_ps[:, lo:hi],
                                             ones128[:, :],
                                             pTs[pp][:, lo:hi],
                                             start=(kap == 0), stop=stp)
                                ins = None
                                for lo, hi in _bank_chunks(astart, 1024):
                                    stp = (kap == 16 * g + 8 * (lo // 512) + 7)
                                    ins = e.matmul(attn_ps[:, lo:hi],
                                                   vbs[hp][:, 128 * kap:128 * (kap + 1)],
                                                   pTs[pp][:, lo:hi],
                                                   start=(kap == 0), stop=stp)
                                ins.then_inc(_t_pv, 1)
                            ph.on("tensor", p3_pv)
                            pv_d[kap] = t_pv.inc(1)
                            last_pT_rd[pp] = (_t_pv, pv_d[kap])

                        for kap in range(kmax):
                            astart, width = geom(kap)
                            pp = sT_step[0] % 2
                            sT_step[0] += 1
                            kap_pp[kap] = pp
                            qp = kap % 3
                            kap_qp[kap] = qp
                            has_mask = (kap // 2 >= 8 * g)

                            w1 = last_sT_exp[pp]
                            w2 = last_sT_red[pp]
                            wnm = (negm_rdy[h] if (g == 0 and kap <= 1) else None)
                            use_r1 = (pp == 0)

                            def p3_sc(e, kap=kap, pp=pp, hp=hp, g=g, astart=astart,
                                      w1=w1, w2=w2, use_r1=use_r1,
                                      wnm=(wnm if use_r1 else None)):
                                _w(e, w1)
                                _w(e, w2)
                                _w(e, wnm)
                                ins = None
                                for lo, hi in _bank_chunks(astart, 1024):
                                    ins = e.matmul(sTs[pp][:, lo:hi],
                                                   kbs[hp][:, 128 * kap:128 * (kap + 1)],
                                                   qbs[hp][:, 1024 * g + lo:1024 * g + hi],
                                                   start=True, stop=not use_r1)
                                if use_r1:
                                    for lo, hi in _bank_chunks(astart, 1024):
                                        ins = e.matmul(sTs[pp][:, lo:hi],
                                                       ones_row[0:1, :],
                                                       negms[hp][0:1, 1024 * g + lo:1024 * g + hi],
                                                       start=False, stop=True)
                                ins.then_inc(_t_sc, 1)
                            ph.on("tensor", p3_sc)
                            sc_d[kap] = t_sc.inc(1)

                            mk_base = v_mk.n

                            def p3_mk(e, kap=kap, pp=pp, hp=hp, g=g, astart=astart,
                                      sc=sc_d[kap], has_mask=has_mask,
                                      wnm=(None if use_r1 else wnm),
                                      use_r1=use_r1, mk_base=mk_base):
                                e.wait_ge(_t_sc, sc)
                                _w(e, wnm)
                                if not use_r1:
                                    e.tensor_tensor(sTs[pp][:, astart:1024],
                                                    sTs[pp][:, astart:1024],
                                                    negms[hp][:, 1024 * g + astart:1024 * (g + 1)],
                                                    ALU.add).then_inc(_v_mk, 1)
                                if has_mask:
                                    if not use_r1:
                                        e.wait_ge(_v_mk, mk_base + 1)
                                    msk = c_mdiag if (kap % 2 == 1) else c_mpad
                                    e.tensor_tensor(sTs[pp][:, astart:astart + 128],
                                                    sTs[pp][:, astart:astart + 128],
                                                    msk[:, :], ALU.add).then_inc(_v_mk, 1)
                            if use_r1 and not has_mask:
                                mk_d[kap] = None
                            else:
                                ph.on("vector", p3_mk)
                                mk_d[kap] = v_mk.inc((0 if use_r1 else 1) + (1 if has_mask else 0))

                            wpT = last_pT_rd[qp]

                            def p3_xp(e, kap=kap, pp=pp, qp=qp, astart=astart,
                                      mk=mk_d[kap], sc=sc_d[kap], wpT=wpT):
                                if mk is not None:
                                    e.wait_ge(_v_mk, mk)
                                else:
                                    e.wait_ge(_t_sc, sc)
                                _w(e, wpT)
                                e.activation(pTs[qp][:, astart:1024],
                                             sTs[pp][:, astart:1024],
                                             ACT.Exp).then_inc(_s_xp, 1)
                            ph.on("scalar", p3_xp)
                            xp_d[kap] = s_xp.inc(1)
                            last_sT_exp[pp] = (_s_xp, xp_d[kap])

                            if kap >= 2:
                                emit_pv(kap - 2)

                            if g == 1 and kap % 2 == 0 and kap < 16 and ai < len(thunksA):
                                thunksA[ai]()
                                ai += 1

                        emit_pv(kmax - 2)
                        emit_pv(kmax - 1)

                        ra = v_ev3.inc(1)

                        def p3_rs(e, pv=pv_d[kmax - 1], wd=last_rs_dbg, ra=ra, h=h, g=g,
                                  wxd=(dbg_xn_done if (h == 0 and g == 0) else None)):
                            e.wait_ge(_t_pv, pv)
                            _w(e, wd)
                            _w(e, wxd)
                            e.reciprocal(bc_sb[:, :], rs_ps[:, 0:1024]).then_inc(_v_ev3, 1)
                            e.wait_ge(_v_ev3, ra)
                            e.tensor_tensor(attn_sb[:, h * NTOK + 1024 * g: h * NTOK + 1024 * (g + 1)],
                                            attn_ps[:, :], bc_sb[:, :], ALU.mult).then_inc(_v_ev3, 1)
                        ph.on("vector", p3_rs)
                        last_rs_rd = (_v_ev3, ra)
                        last_attn_rd = (_v_ev3, v_ev3.inc(1))

                        if debug:
                            def p3_dbg(e, h=h, g=g, w=last_attn_rd):
                                _w(e, w)
                                e.dma_start(out=rs_dbg[0:1, h * NTOK + 1024 * g: h * NTOK + 1024 * (g + 1)],
                                            in_=bc_sb[0:1, :]).then_inc(_d_dbg, 16)
                            ph.on("gpsimd", p3_dbg)
                            last_rs_dbg = (_d_dbg, d_dbg.inc(16))
                        ph.flush()

                    while ai < len(thunksA):
                        thunksA[ai]()
                        ai += 1
                    ph.flush()
                    negm_last_rd[hp] = [(_v_mk, v_mk.n), (_t_sc, t_sc.n)]
                    head_t_done[h] = t_pv.n

                attn_all = last_attn_rd
                ph.flush()

            if debug:
                def dbg_attn(e, w=attn_all):
                    _w(e, w)
                    e.dma_start(out=attn_dbg[:, :], in_=attn_sb[:, :]).then_inc(_d_dbg, 16)
                ph.on("gpsimd", dbg_attn)
                d_dbg.inc(16)
                ph.flush()

            with ExitStack() as _c4:
                at_bf = _c4.enter_context(nc.sbuf_tensor("at_bf", [128, NC * ER], BF16))
                bw_bf = _c4.enter_context(nc.sbuf_tensor("bw_bf", [128, 4 * D], BF16))
                wg_bf = _c4.enter_context(nc.sbuf_tensor("wg_bf", [128, NC * E], BF16))
                x4t = _c4.enter_context(nc.sbuf_tensor("x4t", [128, D], F32))
                x1t = _c4.enter_context(nc.sbuf_tensor("x1t", [128, D], F32))
                xn2tm = _c4.enter_context(nc.sbuf_tensor("xn2tm", [128, D], BF16))
                xn2fm = _c4.enter_context(nc.sbuf_tensor("xn2fm", [128, D], BF16))
                sq4 = _c4.enter_context(nc.sbuf_tensor("sq4", [128, D], F32))
                sm4 = _c4.enter_context(nc.sbuf_tensor("sm4", [128, 1], F32))
                ssq4 = _c4.enter_context(nc.sbuf_tensor("ssq4", [128, 1], F32))
                mv4 = _c4.enter_context(nc.sbuf_tensor("mv4", [128, 2], F32))
                sd4 = _c4.enter_context(nc.sbuf_tensor("sd4", [128, 1], F32))
                rstd4 = _c4.enter_context(nc.sbuf_tensor("rstd4", [128, 1], F32))
                lg = _c4.enter_context(nc.sbuf_tensor("lg", [128, E], F32))
                nmx8 = _c4.enter_context(nc.sbuf_tensor("nmx8", [128, 1], F32))
                mask8 = _c4.enter_context(nc.sbuf_tensor("mask8", [128, E], BF16))
                dlg = _c4.enter_context(nc.sbuf_tensor("dlg", [128, E], F32))
                p8 = _c4.enter_context(nc.sbuf_tensor("p8", [128, E], F32))
                se8 = _c4.enter_context(nc.sbuf_tensor("se8", [128, 1], F32))
                fscale = _c4.enter_context(nc.sbuf_tensor("fscale", [128, 1], F32))
                hbf = _c4.enter_context(nc.sbuf_tensor("hbf", [128, ER], BF16))
                hT = _c4.enter_context(nc.sbuf_tensor("hT", [128, ER], BF16))
                x2t0 = _c4.enter_context(nc.sbuf_tensor("x2t0", [128, D], F32))
                big = _c4.enter_context(nc.psum_tensor("big", [128, 2048], F32))
                yps = _c4.enter_context(nc.psum_tensor("yps", [128, 1024], F32))
                aux = _c4.enter_context(nc.psum_tensor("aux", [128, 512], F32))
                trx = _c4.enter_context(nc.psum_tensor("trx", [128, 512], BF16))
                x2ts = [x2t0, x2t0]

                def ffn_w_dma(e, w=attn_all):
                    _w(e, w)
                    for c in range(16):
                        e.dma_start(out=at_bf[:, ER * c:ER * (c + 1)],
                                    in_=att[128 * c:128 * (c + 1), :]).then_inc(_d_w4b, 16)
                    for c in range(4):
                        e.dma_start(out=bw_bf[:, D * c:D * (c + 1)],
                                    in_=bwt[128 * c:128 * (c + 1), :]).then_inc(_d_w4b, 16)
                    e.dma_start(out=wg_bf[:, :].rearrange("p (c u) -> p c u", u=E),
                                in_=wgt[:, :].rearrange("(c p) u -> p c u", p=128)).then_inc(_d_w4b, 16)
                ph.on("gpsimd", ffn_w_dma)
                ffn_w_ready = (_d_w4b, d_w4b.inc(16 * 21))
                ph.flush()

                x_ld = [0] * NSLOT
                t_oproj = [0] * NSLOT
                t_ophs = [[0, 0] for _ in range(NSLOT)]
                v_x1 = [0] * NSLOT
                s_ln = [0] * NSLOT
                v_xn2 = [0] * NSLOT
                trp_t = [[0] * 4 for _ in range(NSLOT)]
                trp_s = [[0] * 4 for _ in range(NSLOT)]
                t_lg = [0] * NSLOT
                v_lg = [0] * NSLOT
                s_g8 = [0] * NSLOT
                v_g8 = [0] * NSLOT
                t_h = [0] * NSLOT
                v_h = [0] * NSLOT
                t_hT = [0] * NSLOT
                s_hT = [0] * NSLOT
                t_y = [[0, 0] for _ in range(NSLOT)]
                v_x2 = [[0, 0] for _ in range(NSLOT)]
                d_st = [0] * NSLOT

                def emit_front(j):
                    def x4_dma(e, j=j, wa=(attn_all if j == 0 else None)):
                        _w(e, wa)
                        if j >= 1:
                            e.wait_ge(_v_ph4, v_x1[j - 1])
                        e.dma_start(out=x4t[:, :],
                                    in_=xp[128 * (2 * j + 1):128 * (2 * j + 2), :]).then_inc(_d_x4, 16)
                    ph.on("sync", x4_dma)
                    x_ld[j] = (_d_x4, d_x4.inc(16))

                    def oproj(e, j=j,
                              ww=(w4_ready if j == 0 else None),
                              wa=(attn_all if j == 0 else None),
                              wb=(None if j == 0 else (_v_ph4, v_x1[j - 1]))):
                        _w(e, ww)
                        _w(e, wa)
                        _w(e, wb)
                        for hh in range(16):
                            for s4 in range(4):
                                e.matmul(big[:, 512 * s4:512 * (s4 + 1)],
                                         attn_sb[:, hh * NTOK + 128 * j:hh * NTOK + 128 * (j + 1)],
                                         wo_bf[:, D * hh + 512 * s4: D * hh + 512 * (s4 + 1)],
                                         start=(hh == 0), stop=(hh == 15))
                        e.last.then_inc(_t_mm4, 1)
                    ph.on("tensor", oproj)
                    t_ophs[j][0] = t_mm4.n
                    t_ophs[j][1] = t_mm4.n + 1
                    t_oproj[j] = t_mm4.inc(1)
                    t_ophs[j][0] = t_oproj[j]
                    t_ophs[j][1] = t_oproj[j]

                def emit_vecA(j):
                    xa = v_ph4.inc(1)
                    xb = v_ph4.inc(1)

                    def x1ev(e, j=j, wtA=t_ophs[j][0], wtB=t_ophs[j][1], wx=x_ld[j], xa=xa, xb=xb):
                        e.wait_ge(_t_mm4, wtA)
                        _w(e, wx)
                        if j >= 1:
                            e.wait_ge(_s_ph4, s_ln[j - 1])
                            e.wait_ge(_v_ph4, v_x2[j - 1][1])
                        e.tensor_tensor(x1t[:, 0:1024], big[:, 0:1024],
                                        x4t[:, 0:1024], ALU.add).then_inc(_v_ph4, 1)
                        e.wait_ge(_t_mm4, wtB)
                        e.tensor_tensor(x1t[:, 1024:2048], big[:, 1024:2048],
                                        x4t[:, 1024:2048], ALU.add).then_inc(_v_ph4, 1)
                        e.wait_ge(_v_ph4, xb)
                        e.reduce_sum(sm4[:, :], x1t[:, :], axis=AX.X).then_inc(_v_ph4, 1)
                    ph.on("vector", x1ev)
                    v_x1[j] = v_ph4.inc(1)

                    def x1sqr(e, j=j, w=v_x1[j]):
                        e.wait_ge(_v_ph4, w)
                        e.activation(sq4[:, :], x1t[:, :], ACT.Square).then_inc(_s_ph4, 1)
                    ph.on("scalar", x1sqr)
                    s_x1sq = s_ph4.inc(1)

                    xva = v_ph4.inc(1)
                    xvb = v_ph4.inc(1)

                    def x1var(e, j=j, w=s_x1sq, xva=xva, xvb=xvb):
                        e.wait_ge(_s_ph4, w)
                        e.reduce_sum(ssq4[:, :], sq4[:, :], axis=AX.X).then_inc(_v_ph4, 1)
                        e.wait_ge(_v_ph4, xva)
                        e.tensor_scalar(mv4[:, 0:1], sm4[:, :], 1.0 / D, None, ALU.mult)
                        e.tensor_scalar(mv4[:, 1:2], ssq4[:, :], 1.0 / D, None, ALU.mult)
                        e.tensor_scalar(sm4[:, :], sm4[:, :], -1.0 / D, None,
                                        ALU.mult).then_inc(_v_ph4, 1)
                        e.wait_ge(_v_ph4, xvb)
                        e.scalar_tensor_tensor(mv4[:, 1:2], mv4[:, 0:1], sm4[:, :], mv4[:, 1:2],
                                               ALU.mult, ALU.add).then_inc(_v_ph4, 1)
                    ph.on("vector", x1var)
                    v_x1v = v_ph4.inc(1)

                    def ln2s(e, j=j, w=v_x1v):
                        e.wait_ge(_v_ph4, w)
                        e.activation(sd4[:, :], mv4[:, 1:2], ACT.Sqrt, bias=eps_col[:, :]).then_inc(_s_ph4, 1)
                    ph.on("scalar", ln2s)
                    s_ln[j] = s_ph4.inc(1)

                    lra = v_ph4.inc(1)

                    def ln2v(e, j=j, w=s_ln[j], lra=lra):
                        e.wait_ge(_s_ph4, w)
                        e.reciprocal(rstd4[:, :], sd4[:, :]).then_inc(_v_ph4, 1)
                        if j >= 1:
                            e.wait_ge(_t_mm4, trp_t[j - 1][3])
                        e.wait_ge(_v_ph4, lra)
                        e.tensor_scalar(xn2tm[:, :], x1t[:, :], mv4[:, 0:1], rstd4[:, :],
                                        ALU.subtract, ALU.mult).then_inc(_v_ph4, 1)
                    ph.on("vector", ln2v)
                    v_xn2[j] = v_ph4.inc(1)

                def emit_mid(j):
                    for gq in range(4):
                        def trq(e, j=j, gq=gq, wv=v_xn2[j]):
                            e.wait_ge(_v_ph4, wv)
                            if gq >= 1:
                                e.wait_ge(_s_ph4, trp_s[j][gq - 1])
                            elif j >= 1:
                                e.wait_ge(_s_ph4, s_hT[j - 1])
                            if gq == 0 and j >= 1:
                                e.wait_ge(_t_mm4, t_h[j - 1])
                            ins = None
                            for u in range(4):
                                c = 4 * gq + u
                                ins = e.transpose(trx[:, 128 * u:128 * (u + 1)],
                                                  xn2tm[:, 128 * c:128 * (c + 1)], c_ident[:, :])
                            ins.then_inc(_t_mm4, 1)
                        ph.on("tensor", trq)
                        trp_t[j][gq] = t_mm4.inc(1)

                        def trqe(e, j=j, gq=gq, wt=trp_t[j][gq]):
                            e.wait_ge(_t_mm4, wt)
                            e.copy(out=xn2fm[:, 512 * gq:512 * (gq + 1)], in_=trx[:, :]).then_inc(_s_ph4, 1)
                        ph.on("scalar", trqe)
                        trp_s[j][gq] = s_ph4.inc(1)

                    def lgmm(e, j=j, w=trp_s[j][3],
                             wf=(ffn_w_ready if j == 0 else None)):
                        e.wait_ge(_s_ph4, w)
                        _w(e, wf)
                        if j >= 1:
                            e.wait_ge(_v_ph4, v_h[j - 1])
                        ins = None
                        for c in range(16):
                            ins = e.matmul(aux[:, 0:E], xn2fm[:, 128 * c:128 * (c + 1)],
                                           wg_bf[:, E * c:E * (c + 1)],
                                           start=(c == 0), stop=(c == 15))
                        ins.then_inc(_t_mm4, 1)
                    ph.on("tensor", lgmm)
                    t_lg[j] = t_mm4.inc(1)

                    la = v_ph4.inc(1)
                    lb = v_ph4.inc(1)

                    def lgev(e, j=j, w=t_lg[j], la=la, lb=lb):
                        e.wait_ge(_t_mm4, w)
                        if j >= 1:
                            e.wait_ge(_s_ph4, s_g8[j - 1])
                        e.tensor_copy(out=lg[:, :], in_=aux[:, 0:E]).then_inc(_v_ph4, 1)
                        e.wait_ge(_v_ph4, la)
                        e.tensor_reduce(nmx8[:, :], lg[:, :], axis=AX.X, op=ALU.max,
                                        negate=True).then_inc(_v_ph4, 1)
                        e.wait_ge(_v_ph4, lb)
                        e.tensor_scalar(dlg[:, :], lg[:, :], nmx8[:, :], None,
                                        ALU.add).then_inc(_v_ph4, 1)
                    ph.on("vector", lgev)
                    v_lg[j] = v_ph4.inc(1)

                    def gate_s(e, j=j, w=v_lg[j]):
                        e.wait_ge(_v_ph4, w)
                        e.activation(p8[:, :], lg[:, :], ACT.Exp, bias=nmx8[:, :]).then_inc(_s_ph4, 1)
                    ph.on("scalar", gate_s)
                    s_g8[j] = s_ph4.inc(1)

                    ga = v_ph4.inc(1)
                    gb = v_ph4.inc(1)

                    def gate_v(e, j=j, w=s_g8[j], ga=ga, gb=gb):
                        e.wait_ge(_s_ph4, w)
                        e.reduce_sum(se8[:, :], p8[:, :], axis=AX.X).then_inc(_v_ph4, 1)
                        e.wait_ge(_v_ph4, ga)
                        e.reciprocal(se8[:, :], se8[:, :]).then_inc(_v_ph4, 1)
                        e.wait_ge(_v_ph4, gb)
                        e.tensor_scalar(fscale[:, :], se8[:, :], FFN_SCALE, None, ALU.mult)
                        e.tensor_scalar(mask8[:, :], dlg[:, :], 0.0, None,
                                        ALU.is_equal).then_inc(_v_ph4, 1)
                    ph.on("vector", gate_v)
                    v_g8[j] = v_ph4.inc(1)

                    def hmm(e, j=j, w=trp_s[j][3], wl=v_lg[j]):
                        e.wait_ge(_s_ph4, w)
                        e.wait_ge(_v_ph4, wl)
                        ins = None
                        for c in range(16):
                            ins = e.matmul(aux[:, 0:ER], xn2fm[:, 128 * c:128 * (c + 1)],
                                           at_bf[:, ER * c:ER * (c + 1)],
                                           start=(c == 0), stop=(c == 15))
                        ins.then_inc(_t_mm4, 1)
                    ph.on("tensor", hmm)
                    t_h[j] = t_mm4.inc(1)

                    def hev(e, j=j, wt=t_h[j], wg=v_g8[j]):
                        e.wait_ge(_t_mm4, wt)
                        e.wait_ge(_v_ph4, wg)
                        if j >= 1:
                            e.wait_ge(_t_mm4, t_hT[j - 1])
                        m3 = mask8[:, :].unsqueeze(2).to_broadcast([128, E, R])
                        e.tensor_tensor(hbf[:, :].rearrange("p (e r) -> p e r", r=R),
                                        aux[:, 0:ER].rearrange("p (e r) -> p e r", r=R),
                                        m3, ALU.mult).then_inc(_v_ph4, 1)
                    ph.on("vector", hev)
                    v_h[j] = v_ph4.inc(1)

                    def hTt(e, j=j, w=v_h[j]):
                        e.wait_ge(_v_ph4, w)
                        if j >= 1:
                            e.wait_ge(_s_ph4, s_hT[j - 1])
                        ins = None
                        for u in range(4):
                            ins = e.transpose(trx[:, 128 * u:128 * (u + 1)],
                                              hbf[:, 128 * u:128 * (u + 1)], c_ident[:, :])
                        ins.then_inc(_t_mm4, 1)
                    ph.on("tensor", hTt)
                    t_hT[j] = t_mm4.inc(1)

                    def hTe(e, j=j, w=t_hT[j],
                            wy=(None if j == 0 else (_t_mm4, t_y[j - 1][1]))):
                        e.wait_ge(_t_mm4, w)
                        _w(e, wy)
                        e.copy(out=hT[:, :], in_=trx[:, :]).then_inc(_s_ph4, 1)
                    ph.on("scalar", hTe)
                    s_hT[j] = s_ph4.inc(1)

                    for yh in range(2):
                        def ymm(e, j=j, yh=yh, w=s_hT[j],
                                wev=(None if (j == 0 and yh == 0) else
                                     (_v_ph4, v_x2[j - 1][1] if yh == 0 else v_x2[j][0]))):
                            e.wait_ge(_s_ph4, w)
                            _w(e, wev)
                            ins = None
                            for r2 in range(4):
                                for s2 in range(2):
                                    ins = e.matmul(yps[:, 512 * s2:512 * (s2 + 1)],
                                                   hT[:, 128 * r2:128 * (r2 + 1)],
                                                   bw_bf[:, D * r2 + 1024 * yh + 512 * s2:
                                                         D * r2 + 1024 * yh + 512 * (s2 + 1)],
                                                   start=(r2 == 0), stop=(r2 == 3))
                            ins.then_inc(_t_mm4, 1)
                        ph.on("tensor", ymm)
                        t_y[j][yh] = t_mm4.inc(1)

                        def x2ev(e, j=j, yh=yh, w=t_y[j][yh],
                                 ws=(None if j == 0 else d_st[j - 1])):
                            e.wait_ge(_t_mm4, w)
                            if yh == 0:
                                _w(e, ws)
                            e.scalar_tensor_tensor(x2ts[0][:, 1024 * yh:1024 * (yh + 1)],
                                                   yps[:, :], fscale[:, :],
                                                   x1t[:, 1024 * yh:1024 * (yh + 1)],
                                                   ALU.mult, ALU.add).then_inc(_v_ph4, 1)
                        ph.on("vector", x2ev)
                        v_x2[j][yh] = v_ph4.inc(1)

                    def st(e, j=j, w=v_x2[j][1]):
                        e.wait_ge(_v_ph4, w)
                        e.dma_start(out=out[128 * j:128 * (j + 1), :],
                                    in_=x2ts[0][:, :]).then_inc(d_outp[j % 2].h, 16)
                    ph.on("sync", st)
                    d_st[j] = (d_outp[j % 2].h, d_outp[j % 2].inc(16))
                    ph.flush()

                emit_front(0)
                emit_vecA(0)
                for j in range(1, NSLOT):
                    emit_front(j)
                    emit_mid(j - 1)
                    emit_vecA(j)
                emit_mid(NSLOT - 1)

                def fin(e, t0=d_outp[0].n, t1=d_outp[1].n, td=d_dbg.n, tx=d_dbgx.n):
                    e.wait_ge(d_outp[0].h, t0)
                    e.wait_ge(d_outp[1].h, t1)
                    if td:
                        e.wait_ge(_d_dbg, td)
                    if tx:
                        e.wait_ge(_d_dbgx, tx)
                ph.on("gpsimd", fin)
                ph.flush()

    return nc



def _pack_weights(w_qkv, w_o, A, Bw, Wg):
    w_qkv = np.asarray(w_qkv, dtype=np.float32)
    WqT = np.ascontiguousarray(w_qkv[0:D].T)
    WkT = np.ascontiguousarray(w_qkv[D:2 * D].T)
    WvT = np.ascontiguousarray(w_qkv[2 * D:3 * D].T)
    wqp = np.ascontiguousarray(WqT.reshape(D, 16, 128).transpose(1, 0, 2))
    wkp = np.ascontiguousarray(WkT.reshape(D, 16, 128).transpose(1, 0, 2))
    wot_ = np.ascontiguousarray(np.asarray(w_o, dtype=np.float32).T)
    att_ = np.ascontiguousarray(np.asarray(A, dtype=np.float32).transpose(2, 0, 1).reshape(D, ER))
    bwt_ = np.ascontiguousarray(np.asarray(Bw, dtype=np.float32).transpose(0, 2, 1).reshape(ER, D))
    wgt_ = np.ascontiguousarray(np.asarray(Wg, dtype=np.float32).T)
    return wqp, wkp, WvT, wot_, att_, bwt_, wgt_


def _masks(role_a):
    ii = np.arange(128)
    m_diag = np.where(ii[:, None] > ii[None, :], NEGINF, 0.0).astype(np.float32)
    m_pad = np.full((128, 128), NEGINF if role_a else 0.0, dtype=np.float32)
    return m_diag, np.ascontiguousarray(m_diag.T), m_pad


def _perm(role_a):
    perm = np.zeros(NB, dtype=np.int64)
    for j in range(NSLOT):
        if role_a:
            perm[2 * j] = 2 * j + 1
            perm[2 * j + 1] = 2 * j
        else:
            perm[2 * j] = 2 * j
            perm[2 * j + 1] = 2 * j + 1
    return perm


def make_in_maps(x, w_qkv, w_o, A, Bw, Wg):
    x = np.asarray(x, dtype=np.float32)
    wqp, wkp, wvt_, wot_, att_, bwt_, wgt_ = _pack_weights(w_qkv, w_o, A, Bw, Wg)
    ident = np.eye(128, dtype=BF)
    in_maps = []
    for c in range(8):
        b = c // 2
        role_a = (c % 2 == 0)
        perm = _perm(role_a)
        xp_ = np.ascontiguousarray(x[b].reshape(NB, 128, D)[perm].reshape(S, D))
        m_diag, m_diagT, m_pad = _masks(role_a)
        in_maps.append({
            "xp": xp_, "wqp": wqp, "wkp": wkp, "wvt": wvt_, "wot": wot_,
            "att": att_, "bwt": bwt_, "wgt": wgt_,
            "mdiag": m_diag, "mpad": m_pad,
            "ident": ident,
        })
    return in_maps


def assemble_out(results):
    out = np.empty((BATCH, S, D), dtype=np.float32)
    for c in range(8):
        b = c // 2
        role_a = (c % 2 == 0)
        oc = np.asarray(results[c]["out"]).reshape(NSLOT, 128, D)
        for j in range(NSLOT):
            a = 2 * j if role_a else 2 * j + 1
            out[b, 128 * a:128 * (a + 1), :] = oc[j]
    return out


def kernel(L0_attn_hidden_states, L0_attn_w_qkv, L0_attn_w_o, L0_ffn_A, L0_ffn_Bw, L0_ffn_Wg):
    in_maps = make_in_maps(L0_attn_hidden_states, L0_attn_w_qkv, L0_attn_w_o,
                           L0_ffn_A, L0_ffn_Bw, L0_ffn_Wg)
    nc = build_program(debug=False)
    res = run_bass_kernel_spmd(nc, in_maps, core_ids=list(range(8)))
    return assemble_out(res.results)


# version 15
# speedup vs baseline: 1.0851x; 1.0408x over previous
from contextlib import ExitStack

import numpy as np
import ml_dtypes

import concourse.bass as bass
import concourse.mybir as mybir
from concourse.bass_utils import run_bass_kernel_spmd

F32 = mybir.dt.float32
BF16 = mybir.dt.bfloat16
ALU = mybir.AluOpType
ACT = mybir.ActivationFunctionType
AX = mybir.AxisListType

D = 2048
S = 4096
BATCH = 4
NH = 16
NB = 32
NSLOT = 16
NTOK = NSLOT * 128
NC = 16
E, R, ER = 8, 64, 512
NEGINF = -1.0e30
EPS = 1e-5
ISQ = float(1.0 / np.sqrt(128.0))
MSHIFT = 60.0
FFN_SCALE = 4.0

BF = ml_dtypes.bfloat16
ENGS = ("sync", "scalar", "vector", "tensor", "gpsimd")


class Sem:
    def __init__(self, handle):
        self.h = handle
        self.n = 0

    def inc(self, k=1):
        self.n += k
        return self.n


class EngW:

    def __init__(self, e):
        object.__setattr__(self, "_e", e)
        object.__setattr__(self, "last", None)

    def __getattr__(self, n):
        a = getattr(object.__getattribute__(self, "_e"), n)
        if callable(a):
            def f(*args, **kw):
                r = a(*args, **kw)
                if hasattr(r, "then_inc"):
                    object.__setattr__(self, "last", r)
                return r
            return f
        return a


class Ph:

    def __init__(self, blk):
        self.blk = blk
        self.q = {k: [] for k in ENGS}

    def on(self, eng, fn):
        self.q[eng].append(fn)

    def flush(self):
        for eng in ENGS:
            fns = self.q[eng]
            self.q[eng] = []
            if fns:
                getattr(self.blk, eng)(
                    lambda e, fns=tuple(fns): [f(EngW(e)) for f in fns][-1])


def _w(e, t):
    if t is None:
        return
    if isinstance(t, list):
        for u in t:
            _w(e, u)
        return
    e.wait_ge(t[0], t[1])


def _bank_chunks(lo, hi, bank=512):
    out = []
    while lo < hi:
        nxt = min(hi, (lo // bank + 1) * bank)
        out.append((lo, nxt))
        lo = nxt
    return out


def build_program(debug=False):
    nc = bass.Bass("TRN2", target_bir_lowering=False, detect_race_conditions=False)

    xp = nc.dram_tensor("xp", [S, D], F32, kind="ExternalInput").ap()
    wqp = nc.dram_tensor("wqp", [16, D, 128], F32, kind="ExternalInput").ap()
    wkp = nc.dram_tensor("wkp", [16, D, 128], F32, kind="ExternalInput").ap()
    wvt = nc.dram_tensor("wvt", [D, D], F32, kind="ExternalInput").ap()
    wot = nc.dram_tensor("wot", [D, D], F32, kind="ExternalInput").ap()
    att = nc.dram_tensor("att", [D, ER], F32, kind="ExternalInput").ap()
    bwt = nc.dram_tensor("bwt", [ER, D], F32, kind="ExternalInput").ap()
    wgt = nc.dram_tensor("wgt", [D, E], F32, kind="ExternalInput").ap()
    mdiag = nc.dram_tensor("mdiag", [128, 128], F32, kind="ExternalInput").ap()
    mpad = nc.dram_tensor("mpad", [128, 128], F32, kind="ExternalInput").ap()
    ident_d = nc.dram_tensor("ident", [128, 128], BF16, kind="ExternalInput").ap()
    out = nc.dram_tensor("out", [NTOK, D], F32, kind="ExternalOutput").ap()

    scr_kind = "ExternalOutput" if debug else "Internal"
    k_scr = nc.dram_tensor("k_scr", [D, S], BF16, kind=scr_kind).ap()
    v_scr = nc.dram_tensor("v_scr", [S, D], BF16, kind=scr_kind).ap()
    q_scr = nc.dram_tensor("q_scr", [D, NTOK], BF16, kind=scr_kind).ap()
    if debug:
        xn_dbg = nc.dram_tensor("xn_dbg", [128, NC * S], BF16, kind="ExternalOutput").ap()
        attn_dbg = nc.dram_tensor("attn_dbg", [128, NH * NTOK], BF16, kind="ExternalOutput").ap()
        rs_dbg = nc.dram_tensor("rs_dbg", [1, NH * NTOK], F32, kind="ExternalOutput").ap()
        negm_dbg = nc.dram_tensor("negm_dbg", [NH, D], BF16, kind="ExternalOutput").ap()

    with ExitStack() as _ctx:
        def EC(cm):
            return _ctx.enter_context(cm)

        _d_cst = EC(nc.semaphore("d_cst")); _d_x1 = EC(nc.semaphore("d_x1"))
        _v_ph1 = EC(nc.semaphore("v_ph1")); _s_sq1 = EC(nc.semaphore("s_sq1"))
        _t_tr1 = EC(nc.semaphore("t_tr1")); _s_ev1 = EC(nc.semaphore("s_ev1"))
        _d_w2 = EC(nc.semaphore("d_w2")); _t_mm2 = EC(nc.semaphore("t_mm2"))
        _s_ev2 = EC(nc.semaphore("s_ev2")); _t_nm2 = EC(nc.semaphore("t_nm2"))
        _v_nm2 = EC(nc.semaphore("v_nm2")); _d_sp2 = EC(nc.semaphore("d_sp2"))
        _s_bnd = EC(nc.semaphore("s_bnd")); _t_bnd = EC(nc.semaphore("t_bnd"))
        _v_bnd = EC(nc.semaphore("v_bnd")); _d_kvq = EC(nc.semaphore("d_kvq"))
        _t_sc = EC(nc.semaphore("t_sc")); _v_mk = EC(nc.semaphore("v_mk"))
        _s_xp = EC(nc.semaphore("s_xp")); _t_pv = EC(nc.semaphore("t_pv"))
        _v_ev3 = EC(nc.semaphore("v_ev3")); _s_ev3 = EC(nc.semaphore("s_ev3"))
        _t_bc3 = EC(nc.semaphore("t_bc3")); _d_w4 = EC(nc.semaphore("d_w4"))
        _d_x4 = EC(nc.semaphore("d_x4")); _t_mm4 = EC(nc.semaphore("t_mm4"))
        _v_ph4 = EC(nc.semaphore("v_ph4")); _s_ph4 = EC(nc.semaphore("s_ph4"))
        _d_out = EC(nc.semaphore("d_out"))
        _d_x1b = EC(nc.semaphore("d_x1b")); _d_w2b = EC(nc.semaphore("d_w2b"))
        _d_sp2b = EC(nc.semaphore("d_sp2b")); _d_kvqb = EC(nc.semaphore("d_kvqb"))
        _d_w4b = EC(nc.semaphore("d_w4b")); _d_outb = EC(nc.semaphore("d_outb"))
        _d_dbgx = EC(nc.semaphore("d_dbgx")); _d_dbg = EC(nc.semaphore("d_dbg"))
        _t_pa = EC(nc.semaphore("t_pa")); _v_pa = EC(nc.semaphore("v_pa"))
        c_mdiag = EC(nc.sbuf_tensor("c_mdiag", [128, 128], F32))
        c_mpad = EC(nc.sbuf_tensor("c_mpad", [128, 128], F32))
        ones128 = EC(nc.sbuf_tensor("ones128", [128, 128], BF16))
        c_ident = EC(nc.sbuf_tensor("c_ident", [128, 128], BF16))
        ones_col = EC(nc.sbuf_tensor("ones_col", [128, 1], BF16))
        ones_row = EC(nc.sbuf_tensor("ones_row", [1, 128], BF16))
        eps_col = EC(nc.sbuf_tensor("eps_col", [128, 1], F32))
        blk = EC(nc.Block())
        d_cst = Sem(_d_cst); d_x1 = Sem(_d_x1); v_ph1 = Sem(_v_ph1)
        s_sq1 = Sem(_s_sq1); t_tr1 = Sem(_t_tr1); s_ev1 = Sem(_s_ev1)
        d_w2 = Sem(_d_w2); t_mm2 = Sem(_t_mm2)
        s_ev2 = Sem(_s_ev2); t_nm2 = Sem(_t_nm2); v_nm2 = Sem(_v_nm2)
        d_sp2 = Sem(_d_sp2); s_bnd = Sem(_s_bnd); t_bnd = Sem(_t_bnd)
        v_bnd = Sem(_v_bnd); d_kvq = Sem(_d_kvq); t_sc = Sem(_t_sc)
        v_mk = Sem(_v_mk); s_xp = Sem(_s_xp); t_pv = Sem(_t_pv)
        v_ev3 = Sem(_v_ev3); s_ev3 = Sem(_s_ev3); t_bc3 = Sem(_t_bc3)
        d_w4 = Sem(_d_w4); d_x4 = Sem(_d_x4)
        t_mm4 = Sem(_t_mm4); v_ph4 = Sem(_v_ph4); s_ph4 = Sem(_s_ph4)
        d_out = Sem(_d_out)
        d_x1p = [Sem(_d_x1), Sem(_d_x1b)]
        d_w2p = [Sem(_d_w2), Sem(_d_w2b)]
        d_sp2p = [Sem(_d_sp2), Sem(_d_sp2b)]
        d_kvqp = [Sem(_d_kvq), Sem(_d_kvqb)]
        d_w4b = Sem(_d_w4b)
        t_pa = Sem(_t_pa); v_pa = Sem(_v_pa)
        d_outp = [Sem(_d_out), Sem(_d_outb)]
        d_dbgx = Sem(_d_dbgx); d_dbg = Sem(_d_dbg)

        ph = Ph(blk)

        def p0_dma(e):
            e.dma_start(out=c_mdiag[:, :], in_=mdiag[:, :]).then_inc(_d_cst, 16)
            e.dma_start(out=c_mpad[:, :], in_=mpad[:, :]).then_inc(_d_cst, 16)
            e.dma_start(out=c_ident[:, :], in_=ident_d[:, :]).then_inc(_d_cst, 16)
        ph.on("sync", p0_dma)
        cst_ld = d_cst.inc(48)

        def p0_v(e):
            e.memset(ones_col[:, :], 1.0)
            e.memset(ones128[:, :], 1.0)
            e.memset(eps_col[:, :], EPS)
            e.memset(ones_row[:, :], 1.0).then_inc(_v_ph1, 1)
        ph.on("vector", p0_v)
        ones_ready = (_v_ph1, v_ph1.inc(1))
        ident_ready = (_d_cst, cst_ld)
        ph.flush()

        with nc.sbuf_tensor("xn", [128, NC * S], BF16) as xn:
            xn3 = xn[:, :].rearrange("p (c t) -> p c t", t=S)

            with ExitStack() as _c1:
                xt0 = _c1.enter_context(nc.sbuf_tensor("xt0", [128, D], F32))
                xt1 = _c1.enter_context(nc.sbuf_tensor("xt1", [128, D], F32))
                xn_tm0 = _c1.enter_context(nc.sbuf_tensor("xn_tm0", [128, D], BF16))
                xn_tm1 = _c1.enter_context(nc.sbuf_tensor("xn_tm1", [128, D], BF16))
                sq0 = _c1.enter_context(nc.sbuf_tensor("sq0", [128, D], F32))
                sq1 = _c1.enter_context(nc.sbuf_tensor("sq1", [128, D], F32))
                sm1 = _c1.enter_context(nc.sbuf_tensor("sm1", [128, 1], F32))
                ssq1 = _c1.enter_context(nc.sbuf_tensor("ssq1", [128, 1], F32))
                mv = _c1.enter_context(nc.sbuf_tensor("mv", [128, 2], F32))
                sd1 = _c1.enter_context(nc.sbuf_tensor("sd1", [128, 1], F32))
                rstd1 = _c1.enter_context(nc.sbuf_tensor("rstd1", [128, 1], F32))
                tr_ps0 = _c1.enter_context(nc.psum_tensor("tr_ps0", [128, 512], BF16))
                tr_ps1 = _c1.enter_context(nc.psum_tensor("tr_ps1", [128, 512], BF16))
                xts = [xt0, xt1]
                xn_tms = [xn_tm0, xn_tm1]
                sqs = [sq0, sq1]
                tr_pss = [tr_ps0, tr_ps1]

                v_apply = [0] * NB
                v_aggr = [0] * NB
                s_sq = [0] * NB
                s_sqr = [0] * NB
                t_trg = []
                s_evg = []

                for i in range(NB):
                    pp = i % 2

                    def p1_dma(e, i=i, pp=pp, w=(None if i < 2 else (_v_ph1, v_apply[i - 2]))):
                        _w(e, w)
                        e.dma_start(out=xts[pp][:, :],
                                    in_=xp[128 * i:128 * (i + 1), :]).then_inc(d_x1p[pp].h, 16)
                    ph.on("sync", p1_dma)
                    x_ld = (d_x1p[pp].h, d_x1p[pp].inc(16))

                    def p1_sqr(e, i=i, pp=pp, x_ld=x_ld,
                               w=(None if i < 2 else (_v_ph1, v_aggr[i - 2]))):
                        _w(e, x_ld)
                        _w(e, w)
                        e.activation(sqs[pp][:, :], xts[pp][:, :], ACT.Square).then_inc(_s_sq1, 1)
                    ph.on("scalar", p1_sqr)
                    s_sqr[i] = s_sq1.inc(1)

                    va = v_ph1.inc(1)
                    vb = v_ph1.inc(1)

                    def p1_stats(e, i=i, pp=pp, x_ld=x_ld, wsq=s_sqr[i], va=va, vb=vb,
                                 w=(None if i < 1 else (_s_sq1, s_sq[i - 1]))):
                        _w(e, x_ld)
                        _w(e, w)
                        e.reduce_sum(sm1[:, :], xts[pp][:, :], axis=AX.X)
                        e.wait_ge(_s_sq1, wsq)
                        e.reduce_sum(ssq1[:, :], sqs[pp][:, :], axis=AX.X).then_inc(_v_ph1, 1)
                        e.wait_ge(_v_ph1, va)
                        e.tensor_scalar(mv[:, 0:1], sm1[:, :], 1.0 / D, None, ALU.mult)
                        e.tensor_scalar(mv[:, 1:2], ssq1[:, :], 1.0 / D, None, ALU.mult)
                        e.tensor_scalar(sm1[:, :], sm1[:, :], -1.0 / D, None,
                                        ALU.mult).then_inc(_v_ph1, 1)
                        e.wait_ge(_v_ph1, vb)
                        e.scalar_tensor_tensor(mv[:, 1:2], mv[:, 0:1], sm1[:, :], mv[:, 1:2],
                                               ALU.mult, ALU.add).then_inc(_v_ph1, 1)
                    ph.on("vector", p1_stats)
                    v_aggr[i] = v_ph1.inc(1)
                    aggr = v_aggr[i]

                    def p1_sqrt(e, aggr=aggr, w1=(ones_ready if i == 0 else None)):
                        _w(e, w1)
                        e.wait_ge(_v_ph1, aggr)
                        e.activation(sd1[:, :], mv[:, 1:2], ACT.Sqrt, bias=eps_col[:, :]).then_inc(_s_sq1, 1)
                    ph.on("scalar", p1_sqrt)
                    s_sq[i] = s_sq1.inc(1)

                    vr = v_ph1.inc(1)

                    def p1_apply(e, i=i, pp=pp, sq=s_sq[i], vr=vr,
                                 w=(None if i < 2 else (_t_tr1, None))):
                        e.wait_ge(_s_sq1, sq)
                        e.reciprocal(rstd1[:, :], sd1[:, :]).then_inc(_v_ph1, 1)
                        if i >= 2:
                            e.wait_ge(_t_tr1, t_trg[4 * (i - 2) + 3])
                        e.wait_ge(_v_ph1, vr)
                        e.tensor_scalar(xn_tms[pp][:, :], xts[pp][:, :], mv[:, 0:1], rstd1[:, :],
                                        ALU.subtract, ALU.mult).then_inc(_v_ph1, 1)
                    ph.on("vector", p1_apply)
                    v_apply[i] = v_ph1.inc(1)

                    for g in range(4):
                        gi = 4 * i + g
                        tpp = gi % 2

                        def p1_tr(e, i=i, pp=pp, g=g, gi=gi, tpp=tpp,
                                  wa=v_apply[i],
                                  we=(None if gi < 2 else (_s_ev1, None)),
                                  wid=(ident_ready if gi == 0 else None)):
                            _w(e, wid)
                            e.wait_ge(_v_ph1, wa)
                            if gi >= 2:
                                e.wait_ge(_s_ev1, s_evg[gi - 2])
                            for u in range(4):
                                c = 4 * g + u
                                e.transpose(tr_pss[tpp][:, 128 * u:128 * (u + 1)],
                                            xn_tms[pp][:, 128 * c:128 * (c + 1)],
                                            c_ident[:, :])
                            e.last.then_inc(_t_tr1, 1)
                        ph.on("tensor", p1_tr)
                        t_trg.append(t_tr1.inc(1))

                        def p1_ev(e, i=i, g=g, gi=gi, tpp=tpp, wt=t_trg[gi]):
                            e.wait_ge(_t_tr1, wt)
                            dst = xn3[:, 4 * g:4 * (g + 1), 128 * i:128 * (i + 1)]
                            src = tr_pss[tpp][:, :].rearrange("p (u t) -> p u t", t=128)
                            e.copy(out=dst, in_=src).then_inc(_s_ev1, 1)
                        ph.on("scalar", p1_ev)
                        s_evg.append(s_ev1.inc(1))
                    if i % 4 == 3:
                        ph.flush()
                ph.flush()
                xn_ready = (_s_ev1, s_ev1.n)

                def xn_gate(sg):
                    last_tile = 16 * sg + 15
                    return (_s_ev1, s_evg[4 * last_tile + 3])

            wvb0_cm = nc.sbuf_tensor("wvb0", [128, NC * 1024], BF16)
            wvb0 = wvb0_cm.__enter__()
            wv3 = wvt[:, :].rearrange("(c p) v -> p c v", p=128)

            def wvb0_dma(e, w=xn_ready):
                _w(e, w)
                e.dma_start(out=wvb0[:, :].rearrange("p (c u) -> p c u", u=1024),
                            in_=wv3[:, :, 0:1024]).then_inc(d_w2p[0].h, 16)
            ph.on("gpsimd", wvb0_dma)
            wvb0_ld = (d_w2p[0].h, d_w2p[0].inc(16))

            with ExitStack() as _c2:
                wb0 = _c2.enter_context(nc.sbuf_tensor("wb0", [128, D], BF16))
                wb1 = _c2.enter_context(nc.sbuf_tensor("wb1", [128, D], BF16))
                kst0 = _c2.enter_context(nc.sbuf_tensor("kst0", [128, 2048], BF16))
                kst1 = _c2.enter_context(nc.sbuf_tensor("kst1", [128, 2048], BF16))
                kps0 = _c2.enter_context(nc.psum_tensor("kps0", [128, 2048], F32))
                kps1 = _c2.enter_context(nc.psum_tensor("kps1", [128, 2048], F32))
                wbs = [wb0, wb1]
                ksts = [kst0, kst1]
                kpss = [kps0, kps1]

                mm_f = []
                ev_f = []
                sp_f = []
                ld_f = []
                lastmm_of_o = []

                def emit_proj(wpack, scr, ntok, rhs_for, scale, gate):
                    nsg = ntok // 2048
                    base = len(ld_f)
                    for o in range(16):
                        pp = o % 2
                        oo = base + o

                        def w_dma(e, o=o, pp=pp, oo=oo,
                                  wxn=(xn_ready if oo < 2 else None)):
                            _w(e, wxn)
                            if oo >= 2:
                                e.wait_ge(_t_mm2, lastmm_of_o[oo - 2])
                            e.dma_start(
                                out=wbs[pp][:, :].rearrange("p (c u) -> p c u", u=128),
                                in_=wpack[o].rearrange("(c p) u -> p c u", p=128),
                            ).then_inc(d_w2p[pp].h, 16)
                        ph.on("gpsimd", w_dma)
                        ld_f.append((d_w2p[pp].h, d_w2p[pp].inc(16)))

                        for sg in range(nsg):
                            st = len(mm_f)
                            spp = st % 2

                            def p2_mm(e, o=o, pp=pp, sg=sg, st=st, spp=spp, oo=oo,
                                      rhs_for=rhs_for,
                                      wxn=(gate(sg) if oo in (0, 1) else None)):
                                _w(e, wxn)
                                _w(e, ld_f[oo])
                                if st >= 2:
                                    e.wait_ge(_s_ev2, ev_f[st - 2])
                                for c in range(16):
                                    for s4 in range(4):
                                        e.matmul(kpss[spp][:, 512 * s4:512 * (s4 + 1)],
                                                 wbs[pp][:, 128 * c:128 * (c + 1)],
                                                 rhs_for(c, sg * 2048 + 512 * s4, 512),
                                                 start=(c == 0), stop=(c == 15))
                                e.last.then_inc(_t_mm2, 1)
                            ph.on("tensor", p2_mm)
                            mm_f.append(t_mm2.inc(1))

                            def p2_ev(e, st=st, spp=spp, scale=scale):
                                e.wait_ge(_t_mm2, mm_f[st])
                                if st >= 2:
                                    _w(e, sp_f[st - 2])
                                e.activation(ksts[spp][:, :], kpss[spp][:, :], ACT.Copy,
                                             scale=scale).then_inc(_s_ev2, 1)
                            ph.on("scalar", p2_ev)
                            ev_f.append(s_ev2.inc(1))

                            def p2_sp(e, o=o, sg=sg, st=st, spp=spp, scr=scr):
                                e.wait_ge(_s_ev2, ev_f[st])
                                e.dma_start(out=scr[128 * o:128 * (o + 1), 2048 * sg:2048 * (sg + 1)],
                                            in_=ksts[spp][:, :]).then_inc(d_sp2p[spp].h, 16)
                            ph.on("sync", p2_sp)
                            sp_f.append((d_sp2p[spp].h, d_sp2p[spp].inc(16)))

                        lastmm_of_o.append(mm_f[-1])
                        if o % 4 == 3:
                            ph.flush()
                    ph.flush()

                def k_rhs(c, lo, width):
                    return xn[:, c * S + lo: c * S + lo + width]

                xq = xn3.rearrange("p c (j u) -> p c j u", u=128)

                def q_rhs(c, lo, width):
                    j0 = lo // 128
                    nj = width // 128
                    return xq[:, c, 1 + 2 * j0: 2 * (j0 + nj):2, :]

                emit_proj(wkp, k_scr, S, k_rhs, 1.0, xn_gate)
                emit_proj(wqp, q_scr, NTOK, q_rhs, ISQ, lambda sg: xn_ready)
                kq_spills = [(d_sp2p[0].h, d_sp2p[0].n), (d_sp2p[1].h, d_sp2p[1].n)]
                kq_mm_last = (_t_mm2, t_mm2.n)
                kq_ev_last = (_s_ev2, s_ev2.n)


            with ExitStack() as _c2v:
                wvb1 = _c2v.enter_context(nc.sbuf_tensor("wvb1", [128, NC * 1024], BF16))
                vst0 = _c2v.enter_context(nc.sbuf_tensor("vst0", [128, 1024], BF16))
                vst1 = _c2v.enter_context(nc.sbuf_tensor("vst1", [128, 1024], BF16))
                vps0 = _c2v.enter_context(nc.psum_tensor("vps0", [128, 1024], F32))
                vps1 = _c2v.enter_context(nc.psum_tensor("vps1", [128, 1024], F32))
                wvbs = [wvb0, wvb1]
                vsts = [vst0, vst1]
                vpss = [vps0, vps1]

                vld = [0] * 2
                vld[0] = wvb0_ld
                vmm_f = []
                vev_f = []
                vsp_f = []
                def v1_dma(e, wsp=tuple(kq_spills), wnm=kq_mm_last):
                    for t in wsp:
                        _w(e, t)
                    _w(e, wnm)
                    e.dma_start(out=wvb1[:, :].rearrange("p (c u) -> p c u", u=1024),
                                in_=wv3[:, :, 1024:2048]).then_inc(d_w2p[1].h, 16)
                ph.on("gpsimd", v1_dma)
                vld[1] = (d_w2p[1].h, d_w2p[1].inc(16))

                for h2 in range(2):
                    hpp = h2 % 2

                    for i in range(NB):
                        st = len(vmm_f)
                        pp = st % 2

                        def v_mm(e, h2=h2, hpp=hpp, i=i, st=st, pp=pp,
                                 wb=(kq_ev_last if st == 0 else None)):
                            _w(e, wb)
                            _w(e, vld[h2])
                            if st >= 2:
                                e.wait_ge(_s_ev2, vev_f[st - 2])
                            for c in range(16):
                                for s2 in range(2):
                                    e.matmul(vpss[pp][:, 512 * s2:512 * (s2 + 1)],
                                             xn[:, c * S + 128 * i: c * S + 128 * (i + 1)],
                                             wvbs[hpp][:, 1024 * c + 512 * s2:1024 * c + 512 * (s2 + 1)],
                                             start=(c == 0), stop=(c == 15))
                            e.last.then_inc(_t_mm2, 1)
                        ph.on("tensor", v_mm)
                        vmm_f.append(t_mm2.inc(1))

                        def v_ev(e, st=st, pp=pp):
                            e.wait_ge(_t_mm2, vmm_f[st])
                            if st >= 2:
                                _w(e, vsp_f[st - 2])
                            e.activation(vsts[pp][:, :], vpss[pp][:, :], ACT.Copy).then_inc(_s_ev2, 1)
                        ph.on("scalar", v_ev)
                        vev_f.append(s_ev2.inc(1))

                        def v_sp(e, h2=h2, i=i, st=st, pp=pp):
                            e.wait_ge(_s_ev2, vev_f[st])
                            e.dma_start(out=v_scr[128 * i:128 * (i + 1), 1024 * h2:1024 * (h2 + 1)],
                                        in_=vsts[pp][:, :]).then_inc(d_sp2p[pp].h, 16)
                        ph.on("sync", v_sp)
                        vsp_f.append((d_sp2p[pp].h, d_sp2p[pp].inc(16)))
                        if i % 8 == 7:
                            ph.flush()
                    ph.flush()
                all_spills = [(d_sp2p[0].h, d_sp2p[0].n), (d_sp2p[1].h, d_sp2p[1].n)]

            dbg_xn_done = None
            if debug:
                def dbg_xn(e, w=tuple(all_spills)):
                    for t in w:
                        _w(e, t)
                    e.dma_start(out=xn_dbg[:, :], in_=xn[:, :]).then_inc(_d_dbgx, 16)
                ph.on("gpsimd", dbg_xn)
                dbg_xn_done = (_d_dbgx, d_dbgx.inc(16))
                ph.flush()

            wvb0_cm.__exit__(None, None, None)

        with ExitStack() as _c3o:
            attn_sb = _c3o.enter_context(nc.sbuf_tensor("attn_sb", [128, NH * NTOK], BF16))
            wo_bf = _c3o.enter_context(nc.sbuf_tensor("wo_bf", [128, NC * D], BF16))

            with ExitStack() as _c3:
                kb0 = _c3.enter_context(nc.sbuf_tensor("kb0", [128, S], BF16))
                kb1 = _c3.enter_context(nc.sbuf_tensor("kb1", [128, S], BF16))
                vb0 = _c3.enter_context(nc.sbuf_tensor("vb0", [128, S], BF16))
                vb1 = _c3.enter_context(nc.sbuf_tensor("vb1", [128, S], BF16))
                qb0 = _c3.enter_context(nc.sbuf_tensor("qb0", [128, NTOK], BF16))
                qb1 = _c3.enter_context(nc.sbuf_tensor("qb1", [128, NTOK], BF16))
                pT0 = _c3.enter_context(nc.sbuf_tensor("pT0", [128, 1024], BF16))
                pT1 = _c3.enter_context(nc.sbuf_tensor("pT1", [128, 1024], BF16))
                pT2 = _c3.enter_context(nc.sbuf_tensor("pT2", [128, 1024], BF16))
                bc_sb = _c3.enter_context(nc.sbuf_tensor("bc_sb", [128, 1024], F32))
                negm0 = _c3.enter_context(nc.sbuf_tensor("negm0", [128, D], BF16))
                negm1 = _c3.enter_context(nc.sbuf_tensor("negm1", [128, D], BF16))
                skq0 = _c3.enter_context(nc.sbuf_tensor("skq0", [128, 1024], BF16))
                skq1 = _c3.enter_context(nc.sbuf_tensor("skq1", [128, 1024], BF16))
                kslot = _c3.enter_context(nc.sbuf_tensor("kslot", [128, NSLOT], F32))
                ksc1 = _c3.enter_context(nc.sbuf_tensor("ksc1", [128, NSLOT], F32))
                ksc2 = _c3.enter_context(nc.sbuf_tensor("ksc2", [128, NSLOT], F32))
                nrm = _c3.enter_context(nc.sbuf_tensor("nrm", [128, 1024], F32))
                sqt = _c3.enter_context(nc.sbuf_tensor("sqt", [128, 1024], F32))
                attn_ps = _c3.enter_context(nc.psum_tensor("attn_ps", [128, 1024], F32))
                sT0 = _c3.enter_context(nc.psum_tensor("sT0", [128, 512], F32))
                sT1 = _c3.enter_context(nc.psum_tensor("sT1", [128, 512], F32))
                sT2 = _c3.enter_context(nc.psum_tensor("sT2", [128, 512], F32))
                sT3 = _c3.enter_context(nc.psum_tensor("sT3", [128, 512], F32))
                rs_ps = _c3.enter_context(nc.psum_tensor("rs_ps", [128, 1024], F32))
                kbs = [kb0, kb1]; vbs = [vb0, vb1]; qbs = [qb0, qb1]
                pTs = [pT0, pT1, pT2]; sTs = [sT0, sT1, sT2, sT3]; negms = [negm0, negm1]
                skq = [skq0, skq1]

                head_ld = [0] * NH
                head_t_done = [0] * NH

                def emit_head_load(h, extra=()):
                    hp2 = h % 2

                    def ld(e, h=h, hp2=hp2, extra=tuple(extra)):
                        for t in extra:
                            _w(e, t)
                        if h >= 2:
                            e.wait_ge(_t_pv, head_t_done[h - 2])
                        e.dma_start(out=kbs[hp2][:, :],
                                    in_=k_scr[128 * h:128 * (h + 1), :]).then_inc(d_kvqp[hp2].h, 16)
                        e.dma_start(out=vbs[hp2][:, :].rearrange("p (t u) -> p t u", u=128),
                                    in_=v_scr[:, 128 * h:128 * (h + 1)].rearrange("(t p) u -> p t u", p=128),
                                    ).then_inc(d_kvqp[hp2].h, 16)
                        e.dma_start(out=qbs[hp2][:, :],
                                    in_=q_scr[128 * h:128 * (h + 1), :]).then_inc(d_kvqp[hp2].h, 16)
                    ph.on("sync", ld)
                    head_ld[h] = (d_kvqp[hp2].h, d_kvqp[hp2].inc(48))

                emit_head_load(0, extra=all_spills)

                h0 = head_ld[0]

                def w4_dma(e, w=h0, wd=dbg_xn_done):
                    _w(e, w)
                    _w(e, wd)
                    for c in range(16):
                        e.dma_start(out=wo_bf[:, D * c:D * (c + 1)],
                                    in_=wot[128 * c:128 * (c + 1), :]).then_inc(_d_w4, 16)
                ph.on("gpsimd", w4_dma)
                w4_ready = (_d_w4, d_w4.inc(16 * 16))
                ph.flush()

                last_sT_exp = [None] * 4
                last_sT_red = [None] * 4
                last_pT_rd = [None, None, None]
                last_attn_rd = None
                last_rs_rd = None
                last_rs_dbg = None
                negm_last_rd = [None, None]
                negm_dbg_rd = [None, None]
                negm_rdy = [None] * NH
                sT_step = [0]
                skq_rd = [None, None]
                nrm_rd = [None]

                def makeA(h):
                    hp = h % 2
                    thunks = []
                    state = {"first": True}

                    def kn2_thunk(c):
                        def emit():
                            pc = c % 2
                            wld = head_ld[h] if state["first"] else None
                            state["first"] = False
                            wsq_rd = skq_rd[pc]

                            def pa_sq(e, c=c, pc=pc, hp=hp, wld=wld, wsq_rd=wsq_rd):
                                _w(e, wld)
                                _w(e, wsq_rd)
                                e.activation(skq[pc][:, :],
                                             kbs[hp][:, 1024 * c:1024 * (c + 1)],
                                             ACT.Square).then_inc(_s_xp, 1)
                            ph.on("scalar", pa_sq)
                            sq_v = s_xp.inc(1)

                            mm_vs = []
                            for sl in range(2):
                                pp = sT_step[0] % 4
                                sT_step[0] += 1
                                w1 = last_sT_exp[pp]
                                w2 = last_sT_red[pp]

                                def pa_mm(e, pc=pc, pp=pp, sl=sl, sq_v=sq_v,
                                          w1=w1, w2=w2,
                                          won=(ones_ready if (h == 0 and c == 0 and sl == 0) else None)):
                                    _w(e, won)
                                    e.wait_ge(_s_xp, sq_v)
                                    _w(e, w1)
                                    _w(e, w2)
                                    e.matmul(sTs[pp][:, 0:512], ones128[:, :],
                                             skq[pc][:, 512 * sl:512 * (sl + 1)],
                                             start=True, stop=True).then_inc(_t_pa, 1)
                                ph.on("tensor", pa_mm)
                                mm_v = t_pa.inc(1)
                                mm_vs.append(mm_v)

                                def pa_red(e, c=c, pp=pp, sl=sl, mm_v=mm_v):
                                    e.wait_ge(_t_pa, mm_v)
                                    e.tensor_reduce(
                                        kslot[:, 4 * c + 2 * sl:4 * c + 2 * sl + 2],
                                        sTs[pp][:, 0:512].rearrange("p (j u) -> p j u", u=256),
                                        axis=AX.X, op=ALU.max).then_inc(_v_pa, 1)
                                ph.on("vector", pa_red)
                                last_sT_red[pp] = (_v_pa, v_pa.inc(1))
                            skq_rd[pc] = (_t_pa, mm_vs[-1])
                        return emit

                    def scan_thunk():
                        def emit():
                            vred = v_pa.n

                            def pa_scan(e, vred=vred):
                                e.wait_ge(_v_pa, vred)
                                e.tensor_tensor(ksc1[:, 1:16], kslot[:, 1:16],
                                                kslot[:, 0:15], ALU.max).then_inc(_v_pa, 1)
                                e.tensor_copy(out=ksc1[:, 0:1],
                                              in_=kslot[:, 0:1]).then_inc(_v_pa, 1)
                                e.wait_ge(_v_pa, vred + 2)
                                e.tensor_tensor(ksc2[:, 2:16], ksc1[:, 2:16],
                                                ksc1[:, 0:14], ALU.max).then_inc(_v_pa, 1)
                                e.tensor_copy(out=ksc2[:, 0:2],
                                              in_=ksc1[:, 0:2]).then_inc(_v_pa, 1)
                                e.wait_ge(_v_pa, vred + 4)
                                e.tensor_tensor(ksc1[:, 4:16], ksc2[:, 4:16],
                                                ksc2[:, 0:12], ALU.max).then_inc(_v_pa, 1)
                                e.tensor_copy(out=ksc1[:, 0:4],
                                              in_=ksc2[:, 0:4]).then_inc(_v_pa, 1)
                                e.wait_ge(_v_pa, vred + 6)
                                e.tensor_tensor(ksc2[:, 8:16], ksc1[:, 8:16],
                                                ksc1[:, 0:8], ALU.max).then_inc(_v_pa, 1)
                                e.tensor_copy(out=ksc2[:, 0:8],
                                              in_=ksc1[:, 0:8]).then_inc(_v_pa, 1)
                            ph.on("vector", pa_scan)
                            state["scan_v"] = v_pa.inc(8)
                        return emit

                    def qn2_thunk(c):
                        def emit():
                            pc = c % 2
                            wsq_rd = skq_rd[pc]

                            def pa_sqq(e, c=c, pc=pc, hp=hp, wsq_rd=wsq_rd):
                                _w(e, wsq_rd)
                                e.activation(skq[pc][:, :],
                                             qbs[hp][:, 1024 * c:1024 * (c + 1)],
                                             ACT.Square).then_inc(_s_xp, 1)
                            ph.on("scalar", pa_sqq)
                            sq_v = s_xp.inc(1)

                            mm_vs = []
                            mul_vs = []
                            for sl in range(2):
                                pp = sT_step[0] % 4
                                sT_step[0] += 1
                                w1 = last_sT_exp[pp]
                                w2 = last_sT_red[pp]

                                def pa_mmq(e, pc=pc, pp=pp, sl=sl, sq_v=sq_v,
                                           w1=w1, w2=w2):
                                    e.wait_ge(_s_xp, sq_v)
                                    _w(e, w1)
                                    _w(e, w2)
                                    e.matmul(sTs[pp][:, 0:512], ones128[:, :],
                                             skq[pc][:, 512 * sl:512 * (sl + 1)],
                                             start=True, stop=True).then_inc(_t_pa, 1)
                                ph.on("tensor", pa_mmq)
                                mm_v = t_pa.inc(1)
                                mm_vs.append(mm_v)

                                def pa_mul(e, c=c, pp=pp, sl=sl, mm_v=mm_v,
                                           scv=state["scan_v"], wnr=(nrm_rd[0] if sl == 0 else None)):
                                    e.wait_ge(_t_pa, mm_v)
                                    e.wait_ge(_v_pa, scv)
                                    _w(e, wnr)
                                    e.tensor_tensor(
                                        nrm[:, 512 * sl:512 * (sl + 1)].rearrange("p (j u) -> p j u", u=128),
                                        sTs[pp][:, 0:512].rearrange("p (j u) -> p j u", u=128),
                                        ksc2[:, 8 * c + 4 * sl:8 * c + 4 * sl + 4].unsqueeze(2).to_broadcast([128, 4, 128]),
                                        ALU.mult).then_inc(_v_pa, 1)
                                ph.on("vector", pa_mul)
                                mul_v = v_pa.inc(1)
                                mul_vs.append(mul_v)
                                last_sT_red[pp] = (_v_pa, mul_v)
                            skq_rd[pc] = (_t_pa, mm_vs[-1])

                            sq2_v = s_xp.inc(1)
                            af_v = s_xp.inc(1)

                            def pa_sqrt(e, c=c, hp=hp, mul_v=mul_vs[-1], sq2_v=sq2_v,
                                        wn=(negm_last_rd[hp] if c == 0 else None),
                                        wnd=(negm_dbg_rd[hp] if c == 0 else None)):
                                e.wait_ge(_v_pa, mul_v)
                                e.activation(sqt[:, :], nrm[:, :],
                                             ACT.Sqrt).then_inc(_s_xp, 1)
                                _w(e, wn)
                                _w(e, wnd)
                                e.wait_ge(_s_xp, sq2_v)
                                e.activation(negms[hp][:, 1024 * c:1024 * (c + 1)],
                                             sqt[:, :], ACT.Copy, bias=MSHIFT,
                                             scale=-1.0).then_inc(_s_xp, 1)
                            ph.on("scalar", pa_sqrt)
                            nrm_rd[0] = (_s_xp, sq2_v)
                            if c == 1:
                                negm_rdy[h] = (_s_xp, af_v)
                        return emit

                    for c in range(4):
                        thunks.append(kn2_thunk(c))
                    thunks.append(scan_thunk())
                    thunks.append(qn2_thunk(0))
                    thunks.append(qn2_thunk(1))

                    if debug:
                        def dbg_thunk():
                            def pa_dbg(e, h=h, hp=hp, w=negm_rdy[h]):
                                _w(e, w)
                                e.dma_start(out=negm_dbg[h:h + 1, :],
                                            in_=negms[hp][0:1, :]).then_inc(_d_dbg, 16)
                            ph.on("gpsimd", pa_dbg)
                            negm_dbg_rd[hp] = (_d_dbg, d_dbg.inc(16))
                        thunks.append(dbg_thunk)
                    return thunks

                for t in makeA(0):
                    t()
                ph.flush()

                for h in range(NH):
                    hp = h % 2
                    if h + 1 < NH:
                        emit_head_load(h + 1)
                        thunksA = makeA(h + 1)
                    else:
                        thunksA = []
                    ai = 0

                    for g in range(2):
                        kmax = 16 * g + 16
                        sc_d = [0] * kmax
                        mk_d = [0] * kmax
                        xp_d = [0] * kmax
                        pv_d = [0] * kmax
                        kap_pp = [0] * kmax
                        kap_qp = [0] * kmax

                        def geom(kap, g=g):
                            jmin = max(8 * g, kap // 2)
                            astart = (jmin - 8 * g) * 128
                            return astart, 1024 - astart

                        def emit_pv(kap, g=g, h=h, hp=hp, kmax=kmax, pv_d=pv_d,
                                    xp_d=xp_d, kap_pp=kap_pp, kap_qp=kap_qp):
                            astart, width = geom(kap)
                            pp = kap_qp[kap]
                            wattn = last_attn_rd if kap == 0 else None
                            wrs = last_rs_rd if kap == 0 else None

                            def p3_pv(e, kap=kap, pp=pp, hp=hp, astart=astart,
                                      g=g, kmax=kmax, wattn=wattn, wrs=wrs,
                                      xpv=xp_d[kap]):
                                e.wait_ge(_s_xp, xpv)
                                _w(e, wattn)
                                _w(e, wrs)
                                for lo, hi in _bank_chunks(astart, 1024):
                                    stp = (kap == 16 * g + 8 * (lo // 512) + 7)
                                    e.matmul(rs_ps[:, lo:hi],
                                             ones128[:, :],
                                             pTs[pp][:, lo:hi],
                                             start=(kap == 0), stop=stp)
                                ins = None
                                for lo, hi in _bank_chunks(astart, 1024):
                                    stp = (kap == 16 * g + 8 * (lo // 512) + 7)
                                    ins = e.matmul(attn_ps[:, lo:hi],
                                                   vbs[hp][:, 128 * kap:128 * (kap + 1)],
                                                   pTs[pp][:, lo:hi],
                                                   start=(kap == 0), stop=stp)
                                ins.then_inc(_t_pv, 1)
                            ph.on("tensor", p3_pv)
                            pv_d[kap] = t_pv.inc(1)
                            last_pT_rd[pp] = (_t_pv, pv_d[kap])

                        for kap in range(kmax):
                            astart, width = geom(kap)
                            qp = kap % 3
                            kap_qp[kap] = qp
                            has_mask = (kap // 2 >= 8 * g)
                            wpT = last_pT_rd[qp]

                            for lo, hi in _bank_chunks(astart, 1024):
                                pp = sT_step[0] % 4
                                sT_step[0] += 1
                                w1 = last_sT_exp[pp]
                                w2 = last_sT_red[pp]
                                first = (lo == astart)
                                wnm = (negm_rdy[h] if (g == 0 and kap == 0 and first) else None)
                                haswin = has_mask and first

                                def p3_sc(e, kap=kap, pp=pp, hp=hp, g=g,
                                          lo=lo, hi=hi, w1=w1, w2=w2):
                                    _w(e, w1)
                                    _w(e, w2)
                                    e.matmul(sTs[pp][:, 0:hi - lo],
                                             kbs[hp][:, 128 * kap:128 * (kap + 1)],
                                             qbs[hp][:, 1024 * g + lo:1024 * g + hi],
                                             start=True, stop=True).then_inc(_t_sc, 1)
                                ph.on("tensor", p3_sc)
                                sc_v = t_sc.inc(1)

                                mk_base = v_mk.n

                                def p3_mk(e, kap=kap, pp=pp, hp=hp, g=g, lo=lo, hi=hi,
                                          sc=sc_v, haswin=haswin, wnm=wnm, mk_base=mk_base):
                                    e.wait_ge(_t_sc, sc)
                                    _w(e, wnm)
                                    e.tensor_tensor(sTs[pp][:, 0:hi - lo],
                                                    sTs[pp][:, 0:hi - lo],
                                                    negms[hp][:, 1024 * g + lo:1024 * g + hi],
                                                    ALU.add).then_inc(_v_mk, 1)
                                    if haswin:
                                        e.wait_ge(_v_mk, mk_base + 1)
                                        msk = c_mdiag if (kap % 2 == 1) else c_mpad
                                        e.tensor_tensor(sTs[pp][:, 0:128],
                                                        sTs[pp][:, 0:128],
                                                        msk[:, :], ALU.add).then_inc(_v_mk, 1)
                                ph.on("vector", p3_mk)
                                mk_v = v_mk.inc(2 if haswin else 1)

                                def p3_xp(e, kap=kap, pp=pp, qp=qp, lo=lo, hi=hi,
                                          mk=mk_v, wpT=(wpT if first else None)):
                                    e.wait_ge(_v_mk, mk)
                                    _w(e, wpT)
                                    e.activation(pTs[qp][:, lo:hi],
                                                 sTs[pp][:, 0:hi - lo],
                                                 ACT.Exp).then_inc(_s_xp, 1)
                                ph.on("scalar", p3_xp)
                                last_sT_exp[pp] = (_s_xp, s_xp.inc(1))
                            xp_d[kap] = s_xp.n

                            if kap >= 2:
                                emit_pv(kap - 2)

                            if g == 1 and kap % 2 == 0 and kap < 16 and ai < len(thunksA):
                                thunksA[ai]()
                                ai += 1

                        emit_pv(kmax - 2)
                        emit_pv(kmax - 1)

                        ra = v_ev3.inc(1)

                        def p3_rs(e, pv=pv_d[kmax - 1], wd=last_rs_dbg, ra=ra, h=h, g=g,
                                  wxd=(dbg_xn_done if (h == 0 and g == 0) else None)):
                            e.wait_ge(_t_pv, pv)
                            _w(e, wd)
                            _w(e, wxd)
                            e.reciprocal(bc_sb[:, :], rs_ps[:, 0:1024]).then_inc(_v_ev3, 1)
                            e.wait_ge(_v_ev3, ra)
                            e.tensor_tensor(attn_sb[:, h * NTOK + 1024 * g: h * NTOK + 1024 * (g + 1)],
                                            attn_ps[:, :], bc_sb[:, :], ALU.mult).then_inc(_v_ev3, 1)
                        ph.on("vector", p3_rs)
                        last_rs_rd = (_v_ev3, ra)
                        last_attn_rd = (_v_ev3, v_ev3.inc(1))

                        if debug:
                            def p3_dbg(e, h=h, g=g, w=last_attn_rd):
                                _w(e, w)
                                e.dma_start(out=rs_dbg[0:1, h * NTOK + 1024 * g: h * NTOK + 1024 * (g + 1)],
                                            in_=bc_sb[0:1, :]).then_inc(_d_dbg, 16)
                            ph.on("gpsimd", p3_dbg)
                            last_rs_dbg = (_d_dbg, d_dbg.inc(16))
                        ph.flush()

                    while ai < len(thunksA):
                        thunksA[ai]()
                        ai += 1
                    ph.flush()
                    negm_last_rd[hp] = (_v_mk, v_mk.n)
                    head_t_done[h] = t_pv.n

                attn_all = last_attn_rd
                ph.flush()

            if debug:
                def dbg_attn(e, w=attn_all):
                    _w(e, w)
                    e.dma_start(out=attn_dbg[:, :], in_=attn_sb[:, :]).then_inc(_d_dbg, 16)
                ph.on("gpsimd", dbg_attn)
                d_dbg.inc(16)
                ph.flush()

            with ExitStack() as _c4:
                at_bf = _c4.enter_context(nc.sbuf_tensor("at_bf", [128, NC * ER], BF16))
                bw_bf = _c4.enter_context(nc.sbuf_tensor("bw_bf", [128, 4 * D], BF16))
                wg_bf = _c4.enter_context(nc.sbuf_tensor("wg_bf", [128, NC * E], BF16))
                x4t = _c4.enter_context(nc.sbuf_tensor("x4t", [128, D], F32))
                x1t = _c4.enter_context(nc.sbuf_tensor("x1t", [128, D], F32))
                xn2tm = _c4.enter_context(nc.sbuf_tensor("xn2tm", [128, D], BF16))
                xn2fm = _c4.enter_context(nc.sbuf_tensor("xn2fm", [128, D], BF16))
                sq4 = _c4.enter_context(nc.sbuf_tensor("sq4", [128, D], F32))
                sm4 = _c4.enter_context(nc.sbuf_tensor("sm4", [128, 1], F32))
                ssq4 = _c4.enter_context(nc.sbuf_tensor("ssq4", [128, 1], F32))
                mv4 = _c4.enter_context(nc.sbuf_tensor("mv4", [128, 2], F32))
                sd4 = _c4.enter_context(nc.sbuf_tensor("sd4", [128, 1], F32))
                rstd4 = _c4.enter_context(nc.sbuf_tensor("rstd4", [128, 1], F32))
                lg = _c4.enter_context(nc.sbuf_tensor("lg", [128, E], F32))
                nmx8 = _c4.enter_context(nc.sbuf_tensor("nmx8", [128, 1], F32))
                mask8 = _c4.enter_context(nc.sbuf_tensor("mask8", [128, E], BF16))
                dlg = _c4.enter_context(nc.sbuf_tensor("dlg", [128, E], F32))
                p8 = _c4.enter_context(nc.sbuf_tensor("p8", [128, E], F32))
                se8 = _c4.enter_context(nc.sbuf_tensor("se8", [128, 1], F32))
                fscale = _c4.enter_context(nc.sbuf_tensor("fscale", [128, 1], F32))
                hbf = _c4.enter_context(nc.sbuf_tensor("hbf", [128, ER], BF16))
                hT = _c4.enter_context(nc.sbuf_tensor("hT", [128, ER], BF16))
                x2t0 = _c4.enter_context(nc.sbuf_tensor("x2t0", [128, D], F32))
                big = _c4.enter_context(nc.psum_tensor("big", [128, 2048], F32))
                yps = _c4.enter_context(nc.psum_tensor("yps", [128, 1024], F32))
                aux = _c4.enter_context(nc.psum_tensor("aux", [128, 512], F32))
                trx = _c4.enter_context(nc.psum_tensor("trx", [128, 512], BF16))
                x2ts = [x2t0, x2t0]

                def ffn_w_dma(e, w=attn_all):
                    _w(e, w)
                    for c in range(16):
                        e.dma_start(out=at_bf[:, ER * c:ER * (c + 1)],
                                    in_=att[128 * c:128 * (c + 1), :]).then_inc(_d_w4b, 16)
                    for c in range(4):
                        e.dma_start(out=bw_bf[:, D * c:D * (c + 1)],
                                    in_=bwt[128 * c:128 * (c + 1), :]).then_inc(_d_w4b, 16)
                    e.dma_start(out=wg_bf[:, :].rearrange("p (c u) -> p c u", u=E),
                                in_=wgt[:, :].rearrange("(c p) u -> p c u", p=128)).then_inc(_d_w4b, 16)
                ph.on("gpsimd", ffn_w_dma)
                ffn_w_ready = (_d_w4b, d_w4b.inc(16 * 21))
                ph.flush()

                x_ld = [0] * NSLOT
                t_oproj = [0] * NSLOT
                t_ophs = [[0, 0] for _ in range(NSLOT)]
                v_x1 = [0] * NSLOT
                s_ln = [0] * NSLOT
                v_xn2 = [0] * NSLOT
                trp_t = [[0] * 4 for _ in range(NSLOT)]
                trp_s = [[0] * 4 for _ in range(NSLOT)]
                t_lg = [0] * NSLOT
                v_lg = [0] * NSLOT
                s_g8 = [0] * NSLOT
                v_g8 = [0] * NSLOT
                t_h = [0] * NSLOT
                v_h = [0] * NSLOT
                t_hT = [0] * NSLOT
                s_hT = [0] * NSLOT
                t_y = [[0, 0] for _ in range(NSLOT)]
                v_x2 = [[0, 0] for _ in range(NSLOT)]
                d_st = [0] * NSLOT

                def emit_front(j):
                    def x4_dma(e, j=j, wa=(attn_all if j == 0 else None)):
                        _w(e, wa)
                        if j >= 1:
                            e.wait_ge(_v_ph4, v_x1[j - 1])
                        e.dma_start(out=x4t[:, :],
                                    in_=xp[128 * (2 * j + 1):128 * (2 * j + 2), :]).then_inc(_d_x4, 16)
                    ph.on("sync", x4_dma)
                    x_ld[j] = (_d_x4, d_x4.inc(16))

                    def oproj(e, j=j,
                              ww=(w4_ready if j == 0 else None),
                              wa=(attn_all if j == 0 else None),
                              wb=(None if j == 0 else (_v_ph4, v_x1[j - 1]))):
                        _w(e, ww)
                        _w(e, wa)
                        _w(e, wb)
                        for hh in range(16):
                            for s4 in range(4):
                                e.matmul(big[:, 512 * s4:512 * (s4 + 1)],
                                         attn_sb[:, hh * NTOK + 128 * j:hh * NTOK + 128 * (j + 1)],
                                         wo_bf[:, D * hh + 512 * s4: D * hh + 512 * (s4 + 1)],
                                         start=(hh == 0), stop=(hh == 15))
                        e.last.then_inc(_t_mm4, 1)
                    ph.on("tensor", oproj)
                    t_ophs[j][0] = t_mm4.n
                    t_ophs[j][1] = t_mm4.n + 1
                    t_oproj[j] = t_mm4.inc(1)
                    t_ophs[j][0] = t_oproj[j]
                    t_ophs[j][1] = t_oproj[j]

                def emit_vecA(j):
                    xa = v_ph4.inc(1)
                    xb = v_ph4.inc(1)

                    def x1ev(e, j=j, wtA=t_ophs[j][0], wtB=t_ophs[j][1], wx=x_ld[j], xa=xa, xb=xb):
                        e.wait_ge(_t_mm4, wtA)
                        _w(e, wx)
                        if j >= 1:
                            e.wait_ge(_s_ph4, s_ln[j - 1])
                            e.wait_ge(_v_ph4, v_x2[j - 1][1])
                        e.tensor_tensor(x1t[:, 0:1024], big[:, 0:1024],
                                        x4t[:, 0:1024], ALU.add).then_inc(_v_ph4, 1)
                        e.wait_ge(_t_mm4, wtB)
                        e.tensor_tensor(x1t[:, 1024:2048], big[:, 1024:2048],
                                        x4t[:, 1024:2048], ALU.add).then_inc(_v_ph4, 1)
                        e.wait_ge(_v_ph4, xb)
                        e.reduce_sum(sm4[:, :], x1t[:, :], axis=AX.X).then_inc(_v_ph4, 1)
                    ph.on("vector", x1ev)
                    v_x1[j] = v_ph4.inc(1)

                    def x1sqr(e, j=j, w=v_x1[j]):
                        e.wait_ge(_v_ph4, w)
                        e.activation(sq4[:, :], x1t[:, :], ACT.Square).then_inc(_s_ph4, 1)
                    ph.on("scalar", x1sqr)
                    s_x1sq = s_ph4.inc(1)

                    xva = v_ph4.inc(1)
                    xvb = v_ph4.inc(1)

                    def x1var(e, j=j, w=s_x1sq, xva=xva, xvb=xvb):
                        e.wait_ge(_s_ph4, w)
                        e.reduce_sum(ssq4[:, :], sq4[:, :], axis=AX.X).then_inc(_v_ph4, 1)
                        e.wait_ge(_v_ph4, xva)
                        e.tensor_scalar(mv4[:, 0:1], sm4[:, :], 1.0 / D, None, ALU.mult)
                        e.tensor_scalar(mv4[:, 1:2], ssq4[:, :], 1.0 / D, None, ALU.mult)
                        e.tensor_scalar(sm4[:, :], sm4[:, :], -1.0 / D, None,
                                        ALU.mult).then_inc(_v_ph4, 1)
                        e.wait_ge(_v_ph4, xvb)
                        e.scalar_tensor_tensor(mv4[:, 1:2], mv4[:, 0:1], sm4[:, :], mv4[:, 1:2],
                                               ALU.mult, ALU.add).then_inc(_v_ph4, 1)
                    ph.on("vector", x1var)
                    v_x1v = v_ph4.inc(1)

                    def ln2s(e, j=j, w=v_x1v):
                        e.wait_ge(_v_ph4, w)
                        e.activation(sd4[:, :], mv4[:, 1:2], ACT.Sqrt, bias=eps_col[:, :]).then_inc(_s_ph4, 1)
                    ph.on("scalar", ln2s)
                    s_ln[j] = s_ph4.inc(1)

                    lra = v_ph4.inc(1)

                    def ln2v(e, j=j, w=s_ln[j], lra=lra):
                        e.wait_ge(_s_ph4, w)
                        e.reciprocal(rstd4[:, :], sd4[:, :]).then_inc(_v_ph4, 1)
                        if j >= 1:
                            e.wait_ge(_t_mm4, trp_t[j - 1][3])
                        e.wait_ge(_v_ph4, lra)
                        e.tensor_scalar(xn2tm[:, :], x1t[:, :], mv4[:, 0:1], rstd4[:, :],
                                        ALU.subtract, ALU.mult).then_inc(_v_ph4, 1)
                    ph.on("vector", ln2v)
                    v_xn2[j] = v_ph4.inc(1)

                def emit_mid(j):
                    for gq in range(4):
                        def trq(e, j=j, gq=gq, wv=v_xn2[j]):
                            e.wait_ge(_v_ph4, wv)
                            if gq >= 1:
                                e.wait_ge(_s_ph4, trp_s[j][gq - 1])
                            elif j >= 1:
                                e.wait_ge(_s_ph4, s_hT[j - 1])
                            if gq == 0 and j >= 1:
                                e.wait_ge(_t_mm4, t_h[j - 1])
                            ins = None
                            for u in range(4):
                                c = 4 * gq + u
                                ins = e.transpose(trx[:, 128 * u:128 * (u + 1)],
                                                  xn2tm[:, 128 * c:128 * (c + 1)], c_ident[:, :])
                            ins.then_inc(_t_mm4, 1)
                        ph.on("tensor", trq)
                        trp_t[j][gq] = t_mm4.inc(1)

                        def trqe(e, j=j, gq=gq, wt=trp_t[j][gq]):
                            e.wait_ge(_t_mm4, wt)
                            e.copy(out=xn2fm[:, 512 * gq:512 * (gq + 1)], in_=trx[:, :]).then_inc(_s_ph4, 1)
                        ph.on("scalar", trqe)
                        trp_s[j][gq] = s_ph4.inc(1)

                    def lgmm(e, j=j, w=trp_s[j][3],
                             wf=(ffn_w_ready if j == 0 else None)):
                        e.wait_ge(_s_ph4, w)
                        _w(e, wf)
                        if j >= 1:
                            e.wait_ge(_v_ph4, v_h[j - 1])
                        ins = None
                        for c in range(16):
                            ins = e.matmul(aux[:, 0:E], xn2fm[:, 128 * c:128 * (c + 1)],
                                           wg_bf[:, E * c:E * (c + 1)],
                                           start=(c == 0), stop=(c == 15))
                        ins.then_inc(_t_mm4, 1)
                    ph.on("tensor", lgmm)
                    t_lg[j] = t_mm4.inc(1)

                    la = v_ph4.inc(1)
                    lb = v_ph4.inc(1)

                    def lgev(e, j=j, w=t_lg[j], la=la, lb=lb):
                        e.wait_ge(_t_mm4, w)
                        if j >= 1:
                            e.wait_ge(_s_ph4, s_g8[j - 1])
                        e.tensor_copy(out=lg[:, :], in_=aux[:, 0:E]).then_inc(_v_ph4, 1)
                        e.wait_ge(_v_ph4, la)
                        e.tensor_reduce(nmx8[:, :], lg[:, :], axis=AX.X, op=ALU.max,
                                        negate=True).then_inc(_v_ph4, 1)
                        e.wait_ge(_v_ph4, lb)
                        e.tensor_scalar(dlg[:, :], lg[:, :], nmx8[:, :], None,
                                        ALU.add).then_inc(_v_ph4, 1)
                    ph.on("vector", lgev)
                    v_lg[j] = v_ph4.inc(1)

                    def gate_s(e, j=j, w=v_lg[j]):
                        e.wait_ge(_v_ph4, w)
                        e.activation(p8[:, :], lg[:, :], ACT.Exp, bias=nmx8[:, :]).then_inc(_s_ph4, 1)
                    ph.on("scalar", gate_s)
                    s_g8[j] = s_ph4.inc(1)

                    ga = v_ph4.inc(1)
                    gb = v_ph4.inc(1)

                    def gate_v(e, j=j, w=s_g8[j], ga=ga, gb=gb):
                        e.wait_ge(_s_ph4, w)
                        e.reduce_sum(se8[:, :], p8[:, :], axis=AX.X).then_inc(_v_ph4, 1)
                        e.wait_ge(_v_ph4, ga)
                        e.reciprocal(se8[:, :], se8[:, :]).then_inc(_v_ph4, 1)
                        e.wait_ge(_v_ph4, gb)
                        e.tensor_scalar(fscale[:, :], se8[:, :], FFN_SCALE, None, ALU.mult)
                        e.tensor_scalar(mask8[:, :], dlg[:, :], 0.0, None,
                                        ALU.is_equal).then_inc(_v_ph4, 1)
                    ph.on("vector", gate_v)
                    v_g8[j] = v_ph4.inc(1)

                    def hmm(e, j=j, w=trp_s[j][3], wl=v_lg[j]):
                        e.wait_ge(_s_ph4, w)
                        e.wait_ge(_v_ph4, wl)
                        ins = None
                        for c in range(16):
                            ins = e.matmul(aux[:, 0:ER], xn2fm[:, 128 * c:128 * (c + 1)],
                                           at_bf[:, ER * c:ER * (c + 1)],
                                           start=(c == 0), stop=(c == 15))
                        ins.then_inc(_t_mm4, 1)
                    ph.on("tensor", hmm)
                    t_h[j] = t_mm4.inc(1)

                    def hev(e, j=j, wt=t_h[j], wg=v_g8[j]):
                        e.wait_ge(_t_mm4, wt)
                        e.wait_ge(_v_ph4, wg)
                        if j >= 1:
                            e.wait_ge(_t_mm4, t_hT[j - 1])
                        m3 = mask8[:, :].unsqueeze(2).to_broadcast([128, E, R])
                        e.tensor_tensor(hbf[:, :].rearrange("p (e r) -> p e r", r=R),
                                        aux[:, 0:ER].rearrange("p (e r) -> p e r", r=R),
                                        m3, ALU.mult).then_inc(_v_ph4, 1)
                    ph.on("vector", hev)
                    v_h[j] = v_ph4.inc(1)

                    def hTt(e, j=j, w=v_h[j]):
                        e.wait_ge(_v_ph4, w)
                        if j >= 1:
                            e.wait_ge(_s_ph4, s_hT[j - 1])
                        ins = None
                        for u in range(4):
                            ins = e.transpose(trx[:, 128 * u:128 * (u + 1)],
                                              hbf[:, 128 * u:128 * (u + 1)], c_ident[:, :])
                        ins.then_inc(_t_mm4, 1)
                    ph.on("tensor", hTt)
                    t_hT[j] = t_mm4.inc(1)

                    def hTe(e, j=j, w=t_hT[j],
                            wy=(None if j == 0 else (_t_mm4, t_y[j - 1][1]))):
                        e.wait_ge(_t_mm4, w)
                        _w(e, wy)
                        e.copy(out=hT[:, :], in_=trx[:, :]).then_inc(_s_ph4, 1)
                    ph.on("scalar", hTe)
                    s_hT[j] = s_ph4.inc(1)

                    for yh in range(2):
                        def ymm(e, j=j, yh=yh, w=s_hT[j],
                                wev=(None if (j == 0 and yh == 0) else
                                     (_v_ph4, v_x2[j - 1][1] if yh == 0 else v_x2[j][0]))):
                            e.wait_ge(_s_ph4, w)
                            _w(e, wev)
                            ins = None
                            for r2 in range(4):
                                for s2 in range(2):
                                    ins = e.matmul(yps[:, 512 * s2:512 * (s2 + 1)],
                                                   hT[:, 128 * r2:128 * (r2 + 1)],
                                                   bw_bf[:, D * r2 + 1024 * yh + 512 * s2:
                                                         D * r2 + 1024 * yh + 512 * (s2 + 1)],
                                                   start=(r2 == 0), stop=(r2 == 3))
                            ins.then_inc(_t_mm4, 1)
                        ph.on("tensor", ymm)
                        t_y[j][yh] = t_mm4.inc(1)

                        def x2ev(e, j=j, yh=yh, w=t_y[j][yh],
                                 ws=(None if j == 0 else d_st[j - 1])):
                            e.wait_ge(_t_mm4, w)
                            if yh == 0:
                                _w(e, ws)
                            e.scalar_tensor_tensor(x2ts[0][:, 1024 * yh:1024 * (yh + 1)],
                                                   yps[:, :], fscale[:, :],
                                                   x1t[:, 1024 * yh:1024 * (yh + 1)],
                                                   ALU.mult, ALU.add).then_inc(_v_ph4, 1)
                        ph.on("vector", x2ev)
                        v_x2[j][yh] = v_ph4.inc(1)

                    def st(e, j=j, w=v_x2[j][1]):
                        e.wait_ge(_v_ph4, w)
                        e.dma_start(out=out[128 * j:128 * (j + 1), :],
                                    in_=x2ts[0][:, :]).then_inc(d_outp[j % 2].h, 16)
                    ph.on("sync", st)
                    d_st[j] = (d_outp[j % 2].h, d_outp[j % 2].inc(16))
                    ph.flush()

                emit_front(0)
                emit_vecA(0)
                for j in range(1, NSLOT):
                    emit_front(j)
                    emit_mid(j - 1)
                    emit_vecA(j)
                emit_mid(NSLOT - 1)

                def fin(e, t0=d_outp[0].n, t1=d_outp[1].n, td=d_dbg.n, tx=d_dbgx.n):
                    e.wait_ge(d_outp[0].h, t0)
                    e.wait_ge(d_outp[1].h, t1)
                    if td:
                        e.wait_ge(_d_dbg, td)
                    if tx:
                        e.wait_ge(_d_dbgx, tx)
                ph.on("gpsimd", fin)
                ph.flush()

    return nc



def _pack_weights(w_qkv, w_o, A, Bw, Wg):
    w_qkv = np.asarray(w_qkv, dtype=np.float32)
    WqT = np.ascontiguousarray(w_qkv[0:D].T)
    WkT = np.ascontiguousarray(w_qkv[D:2 * D].T)
    WvT = np.ascontiguousarray(w_qkv[2 * D:3 * D].T)
    wqp = np.ascontiguousarray(WqT.reshape(D, 16, 128).transpose(1, 0, 2))
    wkp = np.ascontiguousarray(WkT.reshape(D, 16, 128).transpose(1, 0, 2))
    wot_ = np.ascontiguousarray(np.asarray(w_o, dtype=np.float32).T)
    att_ = np.ascontiguousarray(np.asarray(A, dtype=np.float32).transpose(2, 0, 1).reshape(D, ER))
    bwt_ = np.ascontiguousarray(np.asarray(Bw, dtype=np.float32).transpose(0, 2, 1).reshape(ER, D))
    wgt_ = np.ascontiguousarray(np.asarray(Wg, dtype=np.float32).T)
    return wqp, wkp, WvT, wot_, att_, bwt_, wgt_


def _masks(role_a):
    ii = np.arange(128)
    m_diag = np.where(ii[:, None] > ii[None, :], NEGINF, 0.0).astype(np.float32)
    m_pad = np.full((128, 128), NEGINF if role_a else 0.0, dtype=np.float32)
    return m_diag, np.ascontiguousarray(m_diag.T), m_pad


def _perm(role_a):
    perm = np.zeros(NB, dtype=np.int64)
    for j in range(NSLOT):
        if role_a:
            perm[2 * j] = 2 * j + 1
            perm[2 * j + 1] = 2 * j
        else:
            perm[2 * j] = 2 * j
            perm[2 * j + 1] = 2 * j + 1
    return perm


def make_in_maps(x, w_qkv, w_o, A, Bw, Wg):
    x = np.asarray(x, dtype=np.float32)
    wqp, wkp, wvt_, wot_, att_, bwt_, wgt_ = _pack_weights(w_qkv, w_o, A, Bw, Wg)
    ident = np.eye(128, dtype=BF)
    in_maps = []
    for c in range(8):
        b = c // 2
        role_a = (c % 2 == 0)
        perm = _perm(role_a)
        xp_ = np.ascontiguousarray(x[b].reshape(NB, 128, D)[perm].reshape(S, D))
        m_diag, m_diagT, m_pad = _masks(role_a)
        in_maps.append({
            "xp": xp_, "wqp": wqp, "wkp": wkp, "wvt": wvt_, "wot": wot_,
            "att": att_, "bwt": bwt_, "wgt": wgt_,
            "mdiag": m_diag, "mpad": m_pad,
            "ident": ident,
        })
    return in_maps


def assemble_out(results):
    out = np.empty((BATCH, S, D), dtype=np.float32)
    for c in range(8):
        b = c // 2
        role_a = (c % 2 == 0)
        oc = np.asarray(results[c]["out"]).reshape(NSLOT, 128, D)
        for j in range(NSLOT):
            a = 2 * j if role_a else 2 * j + 1
            out[b, 128 * a:128 * (a + 1), :] = oc[j]
    return out


def kernel(L0_attn_hidden_states, L0_attn_w_qkv, L0_attn_w_o, L0_ffn_A, L0_ffn_Bw, L0_ffn_Wg):
    in_maps = make_in_maps(L0_attn_hidden_states, L0_attn_w_qkv, L0_attn_w_o,
                           L0_ffn_A, L0_ffn_Bw, L0_ffn_Wg)
    nc = build_program(debug=False)
    res = run_bass_kernel_spmd(nc, in_maps, core_ids=list(range(8)))
    return assemble_out(res.results)
